# version 27
# baseline (speedup 1.0000x reference)
import numpy as np
import ml_dtypes
from contextlib import ExitStack

import concourse.bass as bass
import concourse.mybir as mybir
from concourse.bass_utils import run_bass_kernel_spmd

F32 = mybir.dt.float32
BF16 = mybir.dt.bfloat16
I8 = mybir.dt.int8
AF = mybir.ActivationFunctionType
ALU = mybir.AluOpType
AX = mybir.AxisListType

DM = 2048
NH = 16
KVH = 4
INTER = 8192
T = 2048
HALO = 1024
TLOC = T + HALO
SCALE = 0.08838834764831845
EPS = 1e-5
NEG = -1.0e30

SEM_NAMES = (
    "pe", "dve", "act", "pool",
    "dK",
    "dXH0", "dXH1", "dXH2",
    "dXS0", "dXS1", "dXS2", "dXS3",
    "dXT",
    "dQ0", "dQ1",
    "dKW0", "dKW1", "dKW2", "dKW3",
    "dG0", "dG1",
    "dU0", "dU1",
    "dD0", "dD1", "dD2", "dD3",
    "dR0", "dR1",
    "dW2",
    "dX2a", "dX2b",
    "dOa", "dOb",
)


class Bld:
    def __init__(self, nc, ctx):
        self.nc = nc
        self.prog = {k: [] for k in ("t", "v", "s", "g", "y")}
        self.M = {}
        self.sems = {}
        for n in SEM_NAMES:
            self.sems[n] = ctx.enter_context(nc.semaphore(n))
            self.M[n] = 0

    def add(self, eng, build, waits=(), incs=()):
        waits = tuple((s, v) for (s, v) in waits if v > 0)
        for s, v in waits:
            assert v <= self.M[s], f"wait {s}>={v} but only {self.M[s]} issued"
        incs = tuple(incs)
        sems = self.sems

        def thunk(e, _b=build, _w=waits, _i=incs):
            for s, v in _w:
                e.wait_ge(sems[s], v)
            ins = _b(e)
            for s, a in _i:
                ins.then_inc(sems[s], a)
        self.prog[eng].append(thunk)
        for s, a in incs:
            self.M[s] += a

    def mk(self, eng, build, waits=()):
        c = {"t": "pe", "v": "dve", "s": "act", "g": "pool"}[eng]
        self.add(eng, build, waits, incs=((c, 1),))
        return self.M[c]

    def dma(self, eng, sem, out_ap, in_ap, waits=(), transpose=False):
        if transpose:
            self.add(eng, lambda e, o=out_ap, i=in_ap:
                     e.dma_start(o, i, transpose=True), waits,
                     incs=((sem, 16),))
        else:
            self.add(eng, lambda e, o=out_ap, i=in_ap: e.dma_start(o, i),
                     waits, incs=((sem, 16),))
        return self.M[sem]

    def floors(self, *engines):
        m = {"t": "pe", "v": "dve", "s": "act", "g": "pool"}
        return [(m[e], self.M[m[e]]) for e in engines]


def build_nc():
    nc = bass.Bass("TRN2", target_bir_lowering=False, debug=False)

    env = {}
    env["x_pad"] = nc.dram_tensor("x_pad", [TLOC, DM], F32, kind="ExternalInput").ap()
    env["kval"] = nc.dram_tensor("kval", [1, HALO], BF16, kind="ExternalInput").ap()
    env["wq"] = nc.dram_tensor("wq", [DM, DM], F32, kind="ExternalInput").ap()
    env["wk"] = nc.dram_tensor("wk", [512, DM], F32, kind="ExternalInput").ap()
    env["wv"] = nc.dram_tensor("wv", [DM, 512], F32, kind="ExternalInput").ap()
    env["wo"] = nc.dram_tensor("wo", [DM, DM], F32, kind="ExternalInput").ap()
    env["wg"] = nc.dram_tensor("wg", [4096, 4096], I8, kind="ExternalInput").ap()
    env["wu"] = nc.dram_tensor("wu", [4096, 4096], I8, kind="ExternalInput").ap()
    env["wd"] = nc.dram_tensor("wd", [8192, 2048], I8, kind="ExternalInput").ap()
    env["sg_in"] = nc.dram_tensor("sg_in", [64, 128], F32, kind="ExternalInput").ap()
    env["su_in"] = nc.dram_tensor("su_in", [64, 128], F32, kind="ExternalInput").ap()
    env["sd_in"] = nc.dram_tensor("sd_in", [1, DM], F32, kind="ExternalInput").ap()

    env["xbf_hbm"] = nc.dram_tensor("xbf_hbm", [TLOC, DM], BF16, kind="Internal").ap()
    env["x2_hbm"] = nc.dram_tensor("x2_hbm", [T, DM], F32, kind="Internal").ap()
    env["x2bf_hbm"] = nc.dram_tensor("x2bf_hbm", [T, DM], BF16, kind="Internal").ap()
    env["out"] = nc.dram_tensor("out", [T, DM], F32, kind="ExternalOutput").ap()

    ctx = ExitStack()
    with ctx:
        b = Bld(nc, ctx)
        M = b.M

        env["ident_bf"] = ctx.enter_context(nc.sbuf_tensor("ident_bf", [128, 128], BF16)).ap()
        env["identf"] = ctx.enter_context(nc.sbuf_tensor("identf", [128, 128], F32)).ap()
        env["mask2"] = ctx.enter_context(nc.sbuf_tensor("mask2", [128, 256], BF16)).ap()
        env["ones_bf"] = ctx.enter_context(nc.sbuf_tensor("ones_bf", [1, 128], BF16)).ap()
        env["ones_f"] = ctx.enter_context(nc.sbuf_tensor("ones_f", [1, 128], F32)).ap()
        env["kval_sb"] = ctx.enter_context(nc.sbuf_tensor("kval_sb", [1, HALO], BF16)).ap()
        env["sg_sb"] = ctx.enter_context(nc.sbuf_tensor("sg_sb", [128, 64], F32)).ap()
        env["su_sb"] = ctx.enter_context(nc.sbuf_tensor("su_sb", [128, 64], F32)).ap()

        ident_bf, identf, mask2 = env["ident_bf"], env["identf"], env["mask2"]
        ones_bf, ones_f = env["ones_bf"], env["ones_f"]

        b.add("g", lambda e: e.memset(ident_bf, 0.0))
        b.add("g", lambda e: e.affine_select(out=ident_bf, in_=ident_bf,
              compare_op=ALU.not_equal, fill=1.0, base=0,
              pattern=[[-1, 128]], channel_multiplier=1))
        b.add("g", lambda e: e.memset(identf, 0.0))
        b.add("g", lambda e: e.affine_select(out=identf, in_=identf,
              compare_op=ALU.not_equal, fill=1.0, base=0,
              pattern=[[-1, 128]], channel_multiplier=1))
        b.add("g", lambda e: e.memset(mask2, 0.0))
        b.add("g", lambda e: e.affine_select(out=mask2[:, 0:128], in_=mask2[:, 0:128],
              compare_op=ALU.is_ge, fill=NEG, base=0,
              pattern=[[-1, 128]], channel_multiplier=1))
        b.add("g", lambda e: e.affine_select(out=mask2[:, 128:256], in_=mask2[:, 128:256],
              compare_op=ALU.is_ge, fill=NEG, base=0,
              pattern=[[1, 128]], channel_multiplier=-1))
        b.add("g", lambda e: e.memset(ones_bf, 1.0))
        env["consts_done"] = b.mk("g", lambda e: e.memset(ones_f, 1.0))

        def emit_cast(c, strips=1):
            for s in range(strips):
                w0, w1 = s * (DM // strips), (s + 1) * (DM // strips)
                sem = f"dXH{c}" if strips == 1 else f"dXS{s}"
                b.dma("g", sem,
                      env["xbf_hbm"][c * 1024:(c + 1) * 1024, w0:w1],
                      env["x_pad"][c * 1024:(c + 1) * 1024, w0:w1])
        env["emit_cast"] = emit_cast


        env["k_loads"] = b.dma("y", "dK", env["kval_sb"], env["kval"])

        st = {
            "pq_free": [0, 0], "pv_free": [0, 0],
            "wq_free": [0, 0], "wk_free": [0, 0, 0, 0],
            "nQ": {0: 0, 1: 0}, "nKW": {0: 0, 1: 0, 2: 0, 3: 0},
            "qkv_ready": 0,
            "xres_n": 0, "xres_used": 0,
            "x2_spill": 0, "nX2": 0,
            "wo_mark": 0, "av_mark": 0,
            "xTp_used": 0,
            "scales_done": 0,
        }

        with ExitStack() as sup:
            env["KT"] = sup.enter_context(nc.sbuf_tensor("KT", [128, KVH, TLOC], BF16)).ap()
            env["Vn"] = sup.enter_context(nc.sbuf_tensor("Vn", [128, 24, 512], BF16)).ap()
            for half in (0, 1):
                emit_half(nc, b, st, half, env)

        emit_ffn(nc, b, st, env)

        b.add("y", lambda e: e.nop(),
              waits=(("dOa", M["dOa"]), ("dOb", M["dOb"])))

        with nc.Block() as block:
            @block.tensor
            def _(t):
                for f in b.prog["t"]:
                    f(t)

            @block.vector
            def _(v):
                for f in b.prog["v"]:
                    f(v)

            @block.scalar
            def _(s):
                for f in b.prog["s"]:
                    f(s)

            @block.gpsimd
            def _(g):
                for f in b.prog["g"]:
                    f(g)

            @block.sync
            def _(y):
                for f in b.prog["y"]:
                    f(y)

    return nc


def emit_half(nc, b, st, half, env):
    M = b.M
    KT, Vn = env["KT"], env["Vn"]
    ident_bf, identf = env["ident_bf"], env["identf"]
    mask2, ones_bf, ones_f = env["mask2"], env["ones_bf"], env["ones_f"]
    kval_sb = env["kval_sb"]
    sg_sb, su_sb = env["sg_sb"], env["su_sb"]
    x_pad, xbf_hbm = env["x_pad"], env["xbf_hbm"]
    wq, wk, wv, wo = env["wq"], env["wk"], env["wv"], env["wo"]
    x2_hbm, x2bf_hbm = env["x2_hbm"], env["x2bf_hbm"]

    chunks = [1, 0] if half == 0 else [2]
    fl_pe = b.floors("v", "s") if half == 1 else []
    fl_dma = (b.floors("t", "v", "s") +
              [("dW2", 16 * st["x2_spill"]), ("dX2a", 16 * st["nX2"]),
               ("dR0", 16 * st["xres_n"])]) if half == 1 else []
    first_pool = [True]
    first_sync = [True]
    first_pe = [True]

    def poolw(extra=()):
        w = list(extra)
        if first_pool[0]:
            w += fl_dma
            first_pool[0] = False
        return w

    def syncw(extra=()):
        w = list(extra)
        if first_sync[0]:
            w += fl_dma
            first_sync[0] = False
        return w

    def pew(extra=()):
        w = list(extra)
        if first_pe[0]:
            w += fl_pe
            first_pe[0] = False
        return w

    with ExitStack() as hs:
        QT = hs.enter_context(nc.sbuf_tensor(f"QT{half}", [128, NH, 1024], BF16)).ap()

        with ExitStack() as qs:
            if half == 0:
                env["emit_cast"](1, strips=4)
                xT_for = {c: qs.enter_context(
                    nc.sbuf_tensor(f"xT{c}", [128, 16, 1024], BF16)).ap()
                    for c in (0, 1)}
            else:
                xT_for = {2: qs.enter_context(
                    nc.sbuf_tensor("xT2", [128, 16, 1024], BF16)).ap()}
            WvSB = qs.enter_context(nc.sbuf_tensor(f"WvSB{half}", [128, 16, 512], BF16)).ap()
            wq_buf = [qs.enter_context(nc.sbuf_tensor(f"wqb{half}_{i}", [128, 2048], BF16)).ap()
                      for i in range(2)]
            wk_buf = [qs.enter_context(nc.sbuf_tensor(f"wkb{half}_{i}", [128, 2048], BF16)).ap()
                      for i in range(4)]
            pq = [qs.enter_context(nc.psum_tensor(f"pq{half}_{i}", [128, 1024], F32)).ap()
                  for i in range(2)]
            pv = [qs.enter_context(nc.psum_tensor(f"pv{half}_{i}", [128, 512], F32)).ap()
                  for i in range(2)]

            if half == 0:
                sgt_tmp = qs.enter_context(nc.sbuf_tensor("sgt_tmp", [64, 256], F32)).ap()
                b.dma("y", "dK", sgt_tmp[:, 0:128], env["sg_in"])
                k_loads2 = b.dma("y", "dK", sgt_tmp[:, 128:256], env["su_in"])
                m = b.mk("t", lambda e: e.transpose(
                    pq[0][:, 0:64], sgt_tmp[:, 0:128], identf[0:64, 0:64]),
                    waits=pew([("dK", k_loads2), ("pool", env["consts_done"])]))
                m = b.mk("t", lambda e: e.transpose(
                    pq[0][:, 64:128], sgt_tmp[:, 128:256], identf[0:64, 0:64]))
                v1 = b.mk("v", lambda e: e.tensor_copy(sg_sb, pq[0][:, 0:64]),
                          waits=(("pe", m),))
                v1 = b.mk("v", lambda e: e.tensor_copy(su_sb, pq[0][:, 64:128]))
                st["pq_free"][0] = v1
                st["pq_free"][1] = v1

            wv_loads = 0
            for c in chunks:
                xTa = xT_for[c]
                first_chunk = (c == chunks[0])
                w = [(f"dXH{c}", 16)]
                if half == 1 and c == 2:
                    w += fl_dma + [("dW2", 16 * st["x2_spill"]),
                                   ("dX2a", 16 * st["nX2"]),
                                   ("dR0", 16 * st["xres_n"])]
                xt_base = M["dXT"]
                strip_n = 4 if (half == 0 and c == 1) else 1
                if strip_n > 1:
                    w = []
                for dm in range(16):
                    wdm = w if dm == 0 else []
                    if strip_n > 1 and dm % 4 == 0:
                        wdm = list(wdm) + [(f"dXS{dm // 4}", 16)]
                    b.dma("s", "dXT", xTa[:, dm, :],
                          xbf_hbm[c * 1024:(c + 1) * 1024, dm * 128:(dm + 1) * 128],
                          waits=wdm, transpose=True)
                xt_ready = ("dXT", M["dXT"])

                for kh in range(KVH):
                    b.dma("g", f"dKW{kh}", wk_buf[kh],
                          wk[kh * 128:(kh + 1) * 128, :],
                          waits=poolw([("pe", st["wk_free"][kh])]))
                    st["nKW"][kh] += 1
                if first_chunk:
                    for dm in range(16):
                        wv_loads = b.dma("g", "dK", WvSB[:, dm, :],
                                         wv[dm * 128:(dm + 1) * 128, :],
                                         waits=poolw([xt_ready]) if dm == 0
                                         else poolw())
                for kh in range(KVH):
                    par = kh % 2
                    pm = None
                    for dm in range(16):
                        for nb in range(2):
                            w0 = []
                            if dm == 0 and nb == 0:
                                w0 = pew([(f"dKW{kh}", 16 * st["nKW"][kh]),
                                          ("dve", st["pq_free"][par])])
                                if not (first_chunk and kh == 0):
                                    w0.append(xt_ready)
                            if first_chunk and kh == 0 and nb == 0:
                                w0 = list(w0) + [("dXT", xt_base + 16 * (dm + 1))]
                            pm = b.mk("t", lambda e, dm=dm, nb=nb, kh=kh, par=par, xTa=xTa:
                                      e.matmul(pq[par][:, nb * 512:(nb + 1) * 512],
                                               wk_buf[kh][:, dm * 128:(dm + 1) * 128],
                                               xTa[:, dm, nb * 512:(nb + 1) * 512],
                                               start=(dm == 0), stop=(dm == 15)),
                                      waits=w0)
                    st["wk_free"][kh] = pm
                    ev = b.mk("v", lambda e, kh=kh, c=c, par=par:
                              e.tensor_copy(KT[:, kh, c * 1024:(c + 1) * 1024], pq[par]),
                              waits=(("pe", pm),))
                    st["pq_free"][par] = ev
                if half == 0 and c == 1:
                    for qh in (0, 1):
                        par = qh % 2
                        b.dma("g", f"dQ{par}", wq_buf[par],
                              wq[qh * 128:(qh + 1) * 128, :],
                              waits=poolw([("pe", st["wq_free"][par])]))
                        st["nQ"][par] += 1
                    st["q_pre"] = 2
                    env["emit_cast"](0)

                for tt in range(8):
                    par = tt % 2
                    pm = None
                    for dm in range(16):
                        w0 = ()
                        if dm == 0:
                            w0 = pew([("dK", wv_loads), xt_ready,
                                      ("dve", st["pv_free"][par])])
                        pm = b.mk("t", lambda e, tt=tt, dm=dm, par=par, xTa=xTa:
                                  e.matmul(pv[par],
                                           xTa[:, dm, tt * 128:(tt + 1) * 128],
                                           WvSB[:, dm, :],
                                           start=(dm == 0), stop=(dm == 15)),
                                  waits=w0)
                    ev = b.mk("v", lambda e, tt=tt, c=c, par=par:
                              e.tensor_copy(Vn[:, c * 8 + tt, :], pv[par]),
                              waits=(("pe", pm),))
                    st["pv_free"][par] = ev

                if c == half + 1:
                    for qh in range(NH):
                        par = qh % 2
                        if qh >= st.get("q_pre", 0):
                            b.dma("g", f"dQ{par}", wq_buf[par],
                                  wq[qh * 128:(qh + 1) * 128, :],
                                  waits=poolw([("pe", st["wq_free"][par])]))
                            st["nQ"][par] += 1
                        pm = None
                        for dm in range(16):
                            for nb in range(2):
                                w0 = ()
                                if dm == 0 and nb == 0:
                                    w0 = pew([(f"dQ{par}", 16 * st["nQ"][par]),
                                              xt_ready, ("dve", st["pq_free"][par])])
                                pm = b.mk("t", lambda e, qh=qh, dm=dm, nb=nb, par=par, xTa=xTa:
                                          e.matmul(pq[par][:, nb * 512:(nb + 1) * 512],
                                                   wq_buf[par][:, dm * 128:(dm + 1) * 128],
                                                   xTa[:, dm, nb * 512:(nb + 1) * 512],
                                                   start=(dm == 0), stop=(dm == 15)),
                                          waits=w0)
                        st["wq_free"][par] = pm
                        ev = b.mk("v", lambda e, qh=qh, par=par:
                                  e.tensor_scalar_mul(QT[:, qh, :], pq[par], SCALE),
                                  waits=(("pe", pm),))
                        st["pq_free"][par] = ev
                    st["q_pre"] = 0

            st["qkv_ready"] = M["dve"]

        with ExitStack() as ats:
            WoSB = ats.enter_context(nc.sbuf_tensor(f"WoSB{half}", [128, 16, DM], BF16)).ap()
            OT = ats.enter_context(nc.sbuf_tensor(f"OT{half}", [128, NH * 128], BF16)).ap()
            P_sb = [ats.enter_context(nc.sbuf_tensor(f"P{half}_{i}", [128, 1152], BF16)).ap()
                    for i in range(2)]
            Pn_sb = [ats.enter_context(nc.sbuf_tensor(f"Pn{half}_{i}", [128, 1152], BF16)).ap()
                     for i in range(2)]
            PT_sb = ats.enter_context(nc.sbuf_tensor(f"PT{half}", [128, 4608], BF16)).ap()
            h_sb = ats.enter_context(nc.sbuf_tensor(f"h{half}", [128, DM], F32)).ap()
            x2b = ats.enter_context(nc.sbuf_tensor(f"x2b{half}", [128, DM], BF16)).ap()
            x2f = ats.enter_context(nc.sbuf_tensor(f"x2f{half}", [128, DM], F32)).ap()
            xres = ats.enter_context(nc.sbuf_tensor(f"xres{half}", [128, DM], F32)).ap()
            tiny = ats.enter_context(nc.sbuf_tensor(f"tiny{half}", [128, 40], F32)).ap()
            psx = [ats.enter_context(nc.psum_tensor(f"psx{half}_{i}", [128, 1536], F32)).ap()
                   for i in range(2)]
            ppt = ats.enter_context(nc.psum_tensor(f"ppt{half}", [128, 1024], BF16)).ap()
            pav = ats.enter_context(nc.psum_tensor(f"pav{half}", [128, 512], F32)).ap()
            ppt_f32 = ppt.bitcast(F32)
            pav_bf = pav.bitcast(BF16)

            mx = [tiny[:, 0 + i:1 + i] for i in range(2)]
            bias = [tiny[:, 2 + i:3 + i] for i in range(2)]
            ssum = [tiny[:, 4 + i:5 + i] for i in range(2)]
            rcp = [tiny[:, 6 + i:7 + i] for i in range(2)]
            bnst = tiny[:, 8:32]
            mv = tiny[:, 32:34]
            veps = tiny[:, 34:35]
            lnv = tiny[:, 35:36]
            rstd = tiny[:, 36:37]
            bm = tiny[:, 37:38]

            proj_pe = M["pe"]
            wo_loads = 0
            for h in range(16):
                wo_loads = b.dma("g", "dK", WoSB[:, h, :],
                                 wo[h * 128:(h + 1) * 128, :],
                                 waits=poolw([("pe", proj_pe)]) if h == 0
                                 else poolw())
            if half == 0:
                env["emit_cast"](2)
            wo_wait = {"v": wo_loads}


            exp_done = [0, 0]
            pn_used = [0, 0]
            pnorm = [0, 0]
            av_evac = 0
            wo_evac = [0, 0]
            ln_done = 0
            x2b_done = 0

            PT4 = PT_sb.rearrange("p (k h t) -> p k h t", h=4, t=128)
            ppt8 = ppt.rearrange("p (k t) -> p k t", t=128)

            def mk_ctx(qi):
                return {"qi": qi, "q": half * 8 + qi,
                        "sc": {}, "norm": {}, "trans": {},
                        "ev_main": {-1: ("act", 0)}, "ev_t8": {-1: 0}}

            def emit_scores(cx, hh):
                q, qi = cx["q"], cx["qi"]
                p = hh % 2
                ps = psx[p]
                w0 = pew([("act", exp_done[p]), ("dve", st["qkv_ready"])])
                widths = (512, 512, 128)
                pm = None
                for nb in range(3):
                    w = widths[nb]
                    k0 = q * 128 + nb * 512
                    r1w = max(0, min(HALO - k0, w)) if q <= 7 else 0
                    has_mask = nb != 1
                    pm = b.mk("t", lambda e, hh=hh, nb=nb, w=w, k0=k0, qi=qi,
                              ps=ps, alone=(r1w == 0 and not has_mask):
                              e.matmul(ps[:, nb * 512:nb * 512 + w],
                                       QT[:, hh, qi * 128:(qi + 1) * 128],
                                       KT[:, hh // 4, k0:k0 + w],
                                       start=True, stop=alone),
                              waits=w0)
                    w0 = ()
                    if r1w:
                        pm = b.mk("t", lambda e, nb=nb, rw=r1w, k0=k0, ps=ps,
                                  alone=(not has_mask):
                                  e.matmul(ps[:, nb * 512:nb * 512 + rw],
                                           ones_bf, kval_sb[:, k0:k0 + rw],
                                           start=False, stop=alone))
                    if has_mask:
                        moff = 0 if nb == 0 else 128
                        poff = 0 if nb == 0 else 1024
                        pm = b.mk("t", lambda e, ps=ps, moff=moff, poff=poff:
                                  e.matmul(ps[:, poff:poff + 128],
                                           mask2[:, moff:moff + 128],
                                           ident_bf,
                                           start=False, stop=True))
                cx["sc"][hh] = pm

            def emit_m(cx, hh):
                p = hh % 2
                ps = psx[p]
                cx["sc"][(hh, "m")] = b.mk("v", lambda e, ps=ps, p=p: e.reduce_max(
                    mx[p], ps[:, 0:1152], axis=AX.X, negate=True),
                    waits=(("pe", cx["sc"][hh]),))

            def emit_exp(cx, hh):
                p = hh % 2
                exp_done[p] = b.mk("s", lambda e, p=p: e.activation(
                    P_sb[p], psx[p][:, 0:1152], AF.Exp,
                    bias=mx[p], scale=1.0, accum_out=ssum[p]),
                    waits=(("dve", cx["sc"][(hh, "m")]), ("dve", pnorm[p])))

            def emit_rn(cx, hh):
                p = hh % 2
                rcpm = b.mk("v", lambda e, p=p: e.reciprocal(rcp[p], ssum[p]),
                            waits=(("act", exp_done[p]),))
                pnorm[p] = b.mk("v", lambda e, p=p: e.tensor_scalar_mul(
                    Pn_sb[p], P_sb[p], rcp[p]),
                    waits=(("pe", pn_used[p]), ("dve", rcpm)))
                cx["norm"][hh] = pnorm[p]

            def emit_trans(cx, hh):
                nonlocal av_evac
                p = hh % 2
                j = hh % 4
                w0 = [("dve", cx["norm"][hh]), cx["ev_main"][hh - 1],
                      ("dve", cx["ev_t8"][hh - 1])]
                if hh == 0:
                    w0 += [("dve", wo_evac[0]), ("dve", wo_evac[1])]
                tm = None
                for k in range(8):
                    tm = b.mk("t", lambda e, k=k, p=p: e.transpose(
                        ppt[:, k * 128:(k + 1) * 128],
                        Pn_sb[p][:, k * 128:(k + 1) * 128], ident_bf),
                        waits=w0 if k == 0 else ())
                    w0 = ()
                w8 = [("dve", av_evac)] if j == 0 else []
                tm8 = b.mk("t", lambda e, p=p: e.transpose(
                    pav_bf[:, 0:128], Pn_sb[p][:, 1024:1152], ident_bf),
                    waits=w8)
                pn_used[p] = tm8
                cx["trans"][hh] = (tm, tm8)

            def emit_ev(cx, hh):
                tm, tm8 = cx["trans"][hh]
                j = hh % 4
                w0 = [("pe", tm)]
                if j == 0:
                    w0.append(("pe", st["av_mark"]))
                if hh % 2 == 0:
                    cx["ev_main"][hh] = ("act", b.mk("s", lambda e, j=j: e.copy(
                        PT4[:, 0:8, j, :], ppt8), waits=w0))
                else:
                    cx["ev_main"][hh] = ("dve", b.mk("v", lambda e, j=j: e.tensor_copy(
                        PT4[:, 0:8, j, :], ppt8), waits=w0))
                cx["ev_t8"][hh] = b.mk("v", lambda e, j=j: e.tensor_copy(
                    PT4[:, 8, j, :], pav_bf[:, 0:128]),
                    waits=(("pe", tm8),))

            def emit_av(cx, kvg):
                nonlocal av_evac
                q = cx["q"]
                pm = None
                for k in range(9):
                    w0 = ()
                    if k == 0:
                        w0 = (cx["ev_main"][kvg * 4 + 3],
                              ("dve", cx["ev_t8"][kvg * 4 + 3]),
                              ("dve", av_evac))
                    pm = b.mk("t", lambda e, q=q, kvg=kvg, k=k:
                              e.matmul(pav,
                                       Vn[:, q + k, kvg * 128:(kvg + 1) * 128],
                                       PT_sb[:, k * 512:(k + 1) * 512],
                                       start=(k == 0), stop=(k == 8)),
                              waits=w0)
                st["av_mark"] = pm
                av_evac = b.mk("v", lambda e, kvg=kvg: e.tensor_copy(
                    OT[:, kvg * 512:(kvg + 1) * 512], pav),
                    waits=(("pe", pm), ("pe", st["wo_mark"])))

            def prelude(cx):
                for hh in (0, 1):
                    emit_scores(cx, hh)
                    emit_m(cx, hh)
                    emit_exp(cx, hh)
                emit_rn(cx, 0)

            def body(cx):
                q = cx["q"]
                b.dma("y", "dR0", xres,
                      x_pad[HALO + q * 128:HALO + (q + 1) * 128, :],
                      waits=syncw([("dve", st["xres_used"])]))
                st["xres_n"] += 1
                for hh in range(2, 16):
                    emit_scores(cx, hh)
                    emit_m(cx, hh)
                    emit_exp(cx, hh)
                    emit_rn(cx, hh - 1)
                    if hh == 2 and st.get("pending_ln1"):
                        st["pending_ln1"]()
                        st["pending_ln1"] = None
                    emit_trans(cx, hh - 2)
                    emit_ev(cx, hh - 2)
                    if (hh - 2) % 4 == 3:
                        emit_av(cx, (hh - 2) // 4)
                emit_rn(cx, 15)
                for hh in (14, 15):
                    emit_trans(cx, hh)
                    emit_ev(cx, hh)
                    if hh % 4 == 3:
                        emit_av(cx, hh // 4)

            def wo_block(cx):
                q = cx["q"]
                for nch in range(4):
                    par = nch % 2
                    pwo_v = ppt_f32 if par == 0 else pav
                    pm = None
                    for h in range(16):
                        w0 = ()
                        if h == 0:
                            w0 = [("dve", av_evac), ("dve", wo_evac[par])]
                            if nch == 0:
                                w0.append(cx["ev_main"][15])
                            if wo_wait["v"]:
                                w0.append(("dK", wo_wait["v"]))
                                wo_wait["v"] = 0
                        pm = b.mk("t", lambda e, nch=nch, h=h, pwo_v=pwo_v:
                                  e.matmul(pwo_v,
                                           OT[:, h * 128:(h + 1) * 128],
                                           WoSB[:, h, nch * 512:(nch + 1) * 512],
                                           start=(h == 0), stop=(h == 15)),
                                  waits=w0)
                    if nch == 3:
                        st["wo_mark"] = pm
                    wo_evac[par] = b.mk("v", lambda e, nch=nch, pwo_v=pwo_v:
                                        e.tensor_add(h_sb[:, nch * 512:(nch + 1) * 512],
                                                     pwo_v,
                                                     xres[:, nch * 512:(nch + 1) * 512]),
                                        waits=(("pe", pm),
                                               ("dR0", 16 * st["xres_n"]),
                                               ("dW2", 16 * st["x2_spill"])))
                st["xres_used"] = M["dve"]

                def emit_ln1(q=q):
                    bs = 0
                    for i4 in range(4):
                        bs = b.mk("v", lambda e, i4=i4: e.bn_stats(
                            bnst[:, 6 * i4:6 * i4 + 6], h_sb[:, 512 * i4:512 * (i4 + 1)]))
                    ag = b.mk("v", lambda e: e.bn_aggr(mv, bnst), waits=(("dve", bs),))
                    ve = b.mk("v", lambda e: e.tensor_scalar_add(veps, mv[:, 1:2], EPS),
                              waits=(("dve", ag),))
                    lm = b.mk("s", lambda e: e.activation(lnv, veps, AF.Ln),
                              waits=(("dve", ve),))
                    re = b.mk("s", lambda e: e.activation(rstd, lnv, AF.Exp, scale=-0.5),
                              waits=(("act", lm),))
                    bmv = b.mk("v", lambda e: e.tensor_scalar(
                        bm, mv[:, 0:1], rstd, -1.0, ALU.mult, ALU.mult),
                        waits=(("act", re),))
                    ln_done = b.mk("v", lambda e: e.tensor_scalar(
                        x2f, h_sb, rstd, bm, ALU.mult, ALU.add),
                        waits=(("dW2", 16 * st["x2_spill"]), ("dve", bmv)))
                    st["x2_spill"] += 1
                    b.dma("y", "dW2", x2_hbm[q * 128:(q + 1) * 128, :], x2f,
                          waits=(("dve", ln_done),))
                    cpy = b.mk("v", lambda e: e.tensor_copy(x2b, x2f),
                               waits=(("dX2a", 16 * st["nX2"]),))
                    st["nX2"] += 1
                    b.dma("y", "dX2a",
                          x2bf_hbm[q * 128:(q + 1) * 128, :], x2b,
                          waits=(("dve", cpy),))
                st["pending_ln1"] = emit_ln1

            cur = mk_ctx(0)
            prelude(cur)
            for qi in range(8):
                nxt = mk_ctx(qi + 1) if qi < 7 else None
                body(cur)
                if nxt is not None:
                    prelude(nxt)
                wo_block(cur)
                cur = nxt
            if st.get("pending_ln1"):
                st["pending_ln1"]()
                st["pending_ln1"] = None


def emit_ffn(nc, b, st, env):
    M = b.M
    sg_sb, su_sb = env["sg_sb"], env["su_sb"]
    x2_hbm, x2bf_hbm, out = env["x2_hbm"], env["x2bf_hbm"], env["out"]
    wg, wu, wd = env["wg"], env["wu"], env["wd"]

    fl_dma = (b.floors("t", "v", "s") +
              [("dW2", 16 * st["x2_spill"]), ("dX2a", 16 * st["nX2"]),
               ("dR0", 16 * st["xres_n"])])
    fl_pe = b.floors("v", "s")
    first = {"g": True, "y": True, "t": True}

    def gatew(kind, extra=()):
        w = list(extra)
        if first[kind]:
            w += fl_dma if kind in ("g", "y") else fl_pe
            first[kind] = False
        return w

    with ExitStack() as fs:
        fuse = fs.enter_context(nc.sbuf_tensor("fuse", [128, 64, 512], BF16)).ap()
        x2T = fs.enter_context(nc.sbuf_tensor("x2T", [128, 16, 512], BF16)).ap()
        wgb = [fs.enter_context(nc.sbuf_tensor(f"wgb{i}", [128, 4096], BF16)).ap()
               for i in range(2)]
        wub = [fs.enter_context(nc.sbuf_tensor(f"wub{i}", [128, 4096], BF16)).ap()
               for i in range(2)]
        wdb = [fs.enter_context(nc.sbuf_tensor(f"wdb{i}", [128, 2048], BF16)).ap()
               for i in range(4)]
        x2r = [fs.enter_context(nc.sbuf_tensor(f"x2r{i}", [128, DM], F32)).ap()
               for i in range(4)]
        sscr = [fs.enter_context(nc.sbuf_tensor(f"sscr{i}", [128, 512], F32)).ap()
                for i in range(2)]
        uscr = [fs.enter_context(nc.sbuf_tensor(f"uscr{i}", [128, 512], F32)).ap()
                for i in range(2)]
        ftiny = fs.enter_context(nc.sbuf_tensor("ftiny", [128, 40], F32)).ap()
        sdb = fs.enter_context(nc.sbuf_tensor("sdb", [128, DM], F32)).ap()
        sd_sb = fs.enter_context(nc.sbuf_tensor("sd_sb", [1, DM], F32)).ap()

        sdl = b.dma("y", "dK", sd_sb, env["sd_in"], waits=gatew("y"))
        with nc.psum_tensor("psdb", [128, 512], F32) as psdb_h:
            psdb = psdb_h.ap()
            v2 = 0
            for n in range(4):
                m2 = b.mk("t", lambda e, n=n: e.matmul(
                    psdb, env["ones_f"], sd_sb[:, 512 * n:512 * (n + 1)],
                    start=True, stop=True),
                    waits=gatew("t", [("dK", sdl)]) if n == 0 else
                    ((("dve", v2),) if v2 else ()))
                v2 = b.mk("v", lambda e, n=n: e.tensor_copy(
                    sdb[:, 512 * n:512 * (n + 1)], psdb),
                    waits=(("pe", m2),))
        sdb_done = v2

        bnst = ftiny[:, 0:24]
        mv = ftiny[:, 32:34]
        veps = ftiny[:, 34:35]
        lnv = ftiny[:, 35:36]
        rstd = ftiny[:, 36:37]
        bm = ftiny[:, 37:38]

        nG = {0: 0, 1: 0}
        nU = {0: 0, 1: 0}
        nD = {0: 0, 1: 0, 2: 0, 3: 0}
        nR = {0: st["xres_n"], 1: 0}
        wg_free = [0, 0]
        wu_free = [0, 0]
        wd_free = [0, 0, 0, 0]
        silu_mark = [0, 0]
        upsc_mark = [0, 0]
        fuse_mark = [0, 0]
        gate_done = 0
        down_done_pe = 0
        prev_evac = [0, 0]
        out_n = {0: 0, 1: 0}
        norm_mark = [0, 0]
        x2r_used = [0, 0, 0, 0]
        wd_pre = [0]

        all_spills = (("dW2", 16 * st["x2_spill"]),
                      ("dX2a", 16 * st["nX2"]))

        gu_pre = [0]

        def emit_gu_load(icb):
            bp = icb % 2
            b.dma("g", f"dG{bp}", wgb[bp], wg[icb * 128:(icb + 1) * 128, :],
                  waits=gatew("g", [("pe", wg_free[bp])]))
            nG[bp] += 1
            b.dma("g", f"dU{bp}", wub[bp], wu[icb * 128:(icb + 1) * 128, :],
                  waits=[("pe", wu_free[bp])])
            nU[bp] += 1

        def emit_wd_load(nch, ibp):
            bp = ibp % 4
            b.dma("g", f"dD{bp}", wdb[bp],
                  wd[(nch * 16 + ibp) * 128:(nch * 16 + ibp + 1) * 128, :],
                  waits=[("pe", wd_free[bp])])
            nD[bp] += 1

        def emit_x2t(f, extra):
            w = gatew("y", extra)
            for dm in range(16):
                b.dma("y", "dXT", x2T[:, dm, :],
                      x2bf_hbm[f * 512:(f + 1) * 512, dm * 128:(dm + 1) * 128],
                      waits=w if dm == 0 else (), transpose=True)
            return ("dXT", M["dXT"])

        x2t_next = None
        for f in range(4):
            x2t_ready = emit_x2t(0, list(all_spills)) if f == 0 else x2t_next

            with ExitStack() as gs:
                pg = [gs.enter_context(nc.psum_tensor(f"pg{f}_{i}", [128, 512], F32)).ap()
                      for i in range(2)]
                pu = [gs.enter_context(nc.psum_tensor(f"pu{f}_{i}", [128, 512], F32)).ap()
                      for i in range(2)]
                for icg in range(64):
                    icb = icg // 2
                    par = icg % 2
                    boff = (icg % 2) * 128
                    bp = icb % 2
                    if icg % 2 == 0 and icb >= gu_pre[0]:
                        emit_gu_load(icb)
                    if icg in (48, 52, 56, 60):
                        emit_wd_load(0, (icg - 48) // 4)
                    pm = None
                    for dm in range(16):
                        w0 = ()
                        if dm == 0:
                            w0 = gatew("t", [(f"dG{bp}", 16 * nG[bp]), x2t_ready,
                                             ("act", silu_mark[par]),
                                             ("dve", prev_evac[0]),
                                             ("dve", prev_evac[1]),
                                             ("dve", sdb_done)]
                                       if icg == 0 else
                                       [(f"dG{bp}", 16 * nG[bp]),
                                        ("act", silu_mark[par])])
                        pm = b.mk("t", lambda e, dm=dm, par=par, bp=bp, boff=boff:
                                  e.matmul(pg[par],
                                           wgb[bp][:, dm * 256 + boff:dm * 256 + boff + 128],
                                           x2T[:, dm, :],
                                           start=(dm == 0), stop=(dm == 15)),
                                  waits=w0)
                    gm = pm
                    for dm in range(16):
                        w0 = ()
                        if dm == 0:
                            w0 = ((f"dU{bp}", 16 * nU[bp]),
                                  ("act", upsc_mark[par]))
                        pm = b.mk("t", lambda e, dm=dm, par=par, bp=bp, boff=boff:
                                  e.matmul(pu[par],
                                           wub[bp][:, dm * 256 + boff:dm * 256 + boff + 128],
                                           x2T[:, dm, :],
                                           start=(dm == 0), stop=(dm == 15)),
                                  waits=w0)
                    wg_free[bp] = pm
                    wu_free[bp] = pm
                    silu_mark[par] = b.mk("s", lambda e, par=par, icg=icg:
                                          e.activation(sscr[par], pg[par], AF.Silu,
                                                       scale=sg_sb[:, icg:icg + 1]),
                                          waits=(("pe", gm),
                                                 ("dve", fuse_mark[par])))
                    upsc_mark[par] = b.mk("s", lambda e, par=par, icg=icg:
                                          e.activation(uscr[par], pu[par], AF.Copy,
                                                       scale=su_sb[:, icg:icg + 1]),
                                          waits=(("pe", pm),))
                    w0 = [("act", upsc_mark[par])]
                    if icg <= 1:
                        w0.append(("pe", down_done_pe))
                    fuse_mark[par] = b.mk("v", lambda e, par=par, icg=icg:
                                          e.tensor_mul(fuse[:, icg, :],
                                                       sscr[par], uscr[par]),
                                          waits=w0)
                gate_done = M["pe"]
                fuse_all = max(fuse_mark)
                gu_pre[0] = 0
                wd_pre[0] = 4

            if f < 3:
                x2t_next = emit_x2t(f + 1, [("pe", gate_done)])

            with ExitStack() as ds:
                pd = [[ds.enter_context(
                    nc.psum_tensor(f"pdn{f}_{g}_{t}", [128, 512], F32)).ap()
                    for t in range(4)] for g in range(2)]
                for tt in range(4):
                    tq = f * 4 + tt
                    par = tq % 2
                    xw = [(f"dO{'ab'[par]}", 16 * out_n[par])]
                    if tq == 0:
                        xw.append(("dW2", 16 * st["x2_spill"]))
                    b.dma("y", f"dR{par}", x2r[tt],
                          x2_hbm[tq * 128:(tq + 1) * 128, :], waits=xw)
                    nR[par] += 1
                sweep_evac = {}
                for nch in range(4):
                    grp = nch % 2
                    last_mm = {}
                    for ic in range(64):
                        ibp = ic // 4
                        bp = ibp % 4
                        if ic % 4 == 0 and not (nch == 0 and ibp < wd_pre[0]):
                            emit_wd_load(nch, ibp)
                            if f < 3 and ic == 16 and nch in (2, 3):
                                emit_gu_load(0 if nch == 2 else 1)
                        roff = ((ic // 2) % 2) * 1024 + (ic % 2) * 512
                        for tt in range(4):
                            w0 = []
                            if ic == 0 and tt == 0:
                                w0 = [(f"dD{bp}", 16 * nD[bp])]
                                if nch == 0:
                                    w0.append(("dve", fuse_all))
                                if nch >= 2:
                                    w0.append(("dve", sweep_evac[nch - 2]))
                            elif ic % 4 == 0 and tt == 0:
                                w0 = [(f"dD{bp}", 16 * nD[bp])]
                            pm = b.mk("t", lambda e, ic=ic, bp=bp, tt=tt,
                                      grp=grp, roff=roff:
                                      e.matmul(pd[grp][tt],
                                               fuse[:, ic, tt * 128:(tt + 1) * 128],
                                               wdb[bp][:, roff:roff + 512],
                                               start=(ic == 0), stop=(ic == 63)),
                                      waits=w0)
                            last_mm[tt] = pm
                        if ic % 4 == 3:
                            wd_free[bp] = M["pe"]
                    ev = 0
                    for tt in range(4):
                        par = tt % 2
                        w0 = [("pe", last_mm[tt])]
                        b.mk("v", lambda e, tt=tt, nch=nch, grp=grp:
                             e.tensor_mul(sscr[tt % 2],
                                          pd[grp][tt],
                                          sdb[:, nch * 512:(nch + 1) * 512]),
                             waits=w0)
                        w1 = []
                        if nch == 0 and tt < 2:
                            w1 = [(f"dR{par}", 16 * nR[par])]
                        ev = b.mk("v", lambda e, tt=tt, nch=nch:
                                  e.tensor_add(x2r[tt][:, nch * 512:(nch + 1) * 512],
                                               x2r[tt][:, nch * 512:(nch + 1) * 512],
                                               sscr[tt % 2]),
                                  waits=w1)
                    sweep_evac[nch] = ev
                down_done_pe = M["pe"]
                prev_evac[0] = sweep_evac[2]
                prev_evac[1] = sweep_evac[3]
                for tt in range(4):
                    tq = f * 4 + tt
                    par = tq % 2
                    bs = 0
                    for i4 in range(4):
                        bs = b.mk("v", lambda e, i4=i4, tt=tt: e.bn_stats(
                            bnst[:, 6 * i4:6 * i4 + 6],
                            x2r[tt][:, 512 * i4:512 * (i4 + 1)]))
                    ag = b.mk("v", lambda e: e.bn_aggr(mv, bnst),
                              waits=(("dve", bs),))
                    ve = b.mk("v", lambda e: e.tensor_scalar_add(
                        veps, mv[:, 1:2], EPS), waits=(("dve", ag),))
                    sq = b.mk("s", lambda e: e.activation(lnv, veps, AF.Sqrt),
                              waits=(("dve", ve),))
                    rc = b.mk("v", lambda e: e.reciprocal(rstd, lnv),
                              waits=(("act", sq),))
                    bmv = b.mk("v", lambda e: e.tensor_scalar(
                        bm, mv[:, 0:1], rstd, -1.0, ALU.mult, ALU.mult),
                        waits=(("dve", rc),))
                    norm_mark[par] = b.mk("v", lambda e, tt=tt: e.tensor_scalar(
                        x2r[tt], x2r[tt], rstd, bm, ALU.mult, ALU.add),
                        waits=(("dve", bmv),))
                    out_n[par] += 1
                    b.dma("y", f"dO{'ab'[par]}",
                          out[tq * 128:(tq + 1) * 128, :], x2r[tt],
                          waits=(("dve", norm_mark[par]),))
            gu_pre[0] = 2 if f < 3 else 0


_NC_CACHE = {}


def _get_nc():
    if "nc" not in _NC_CACHE:
        _NC_CACHE["nc"] = build_nc()
    return _NC_CACHE["nc"]


def kernel(L0_attn_hidden_states, L0_attn_Wq, L0_attn_Wk, L0_attn_Wv,
           L0_attn_Wo, L0_ffn_w_gate, L0_ffn_w_up, L0_ffn_w_down,
           L0_ffn_s_gate, L0_ffn_s_up, L0_ffn_s_down):
    x = np.ascontiguousarray(np.asarray(L0_attn_hidden_states, dtype=np.float32))
    wq = np.ascontiguousarray(
        np.asarray(L0_attn_Wq, dtype=np.float32)
        .reshape(16, 128, 16, 128).transpose(2, 1, 0, 3).reshape(2048, 2048))
    wk = np.ascontiguousarray(
        np.asarray(L0_attn_Wk, dtype=np.float32)
        .reshape(16, 128, 4, 128).transpose(2, 1, 0, 3).reshape(512, 2048))
    wv = np.ascontiguousarray(np.asarray(L0_attn_Wv, dtype=np.float32))
    wo = np.ascontiguousarray(np.asarray(L0_attn_Wo, dtype=np.float32))
    wg = np.ascontiguousarray(
        np.asarray(L0_ffn_w_gate, dtype=np.int8)
        .reshape(16, 128, 32, 256).transpose(2, 1, 0, 3).reshape(4096, 4096))
    wu = np.ascontiguousarray(
        np.asarray(L0_ffn_w_up, dtype=np.int8)
        .reshape(16, 128, 32, 256).transpose(2, 1, 0, 3).reshape(4096, 4096))
    wd = np.ascontiguousarray(
        np.asarray(L0_ffn_w_down, dtype=np.int8)
        .reshape(16, 2, 2, 128, 4, 512).transpose(4, 0, 3, 1, 2, 5)
        .reshape(8192, 2048))
    sg = np.ascontiguousarray(np.asarray(L0_ffn_s_gate, dtype=np.float32).reshape(64, 128))
    su = np.ascontiguousarray(np.asarray(L0_ffn_s_up, dtype=np.float32).reshape(64, 128))
    sd = np.ascontiguousarray(np.asarray(L0_ffn_s_down, dtype=np.float32).reshape(1, DM))

    B, S = 2, 8192
    in_maps = []
    for c in range(8):
        bb, s0 = c // 4, (c % 4) * T
        if s0 == 0:
            halo = np.zeros((HALO, DM), dtype=np.float32)
            kv = np.full((1, HALO), NEG, dtype=ml_dtypes.bfloat16)
        else:
            halo = x[bb, s0 - HALO:s0]
            kv = np.zeros((1, HALO), dtype=ml_dtypes.bfloat16)
        x_pad = np.ascontiguousarray(np.concatenate([halo, x[bb, s0:s0 + T]], axis=0))
        in_maps.append({
            "x_pad": x_pad, "kval": kv,
            "wq": wq, "wk": wk, "wv": wv, "wo": wo,
            "wg": wg, "wu": wu, "wd": wd,
            "sg_in": sg, "su_in": su, "sd_in": sd,
        })

    nc = _get_nc()
    res = run_bass_kernel_spmd(nc, in_maps, core_ids=list(range(8)))

    outp = np.empty((B, S, DM), dtype=np.float32)
    for c in range(8):
        bb, s0 = c // 4, (c % 4) * T
        outp[bb, s0:s0 + T] = res.results[c]["out"]
    return outp


# version 28
# speedup vs baseline: 1.0084x; 1.0084x over previous
import numpy as np
import ml_dtypes
from contextlib import ExitStack

import concourse.bass as bass
import concourse.mybir as mybir
from concourse.bass_utils import run_bass_kernel_spmd

F32 = mybir.dt.float32
BF16 = mybir.dt.bfloat16
I8 = mybir.dt.int8
AF = mybir.ActivationFunctionType
ALU = mybir.AluOpType
AX = mybir.AxisListType

DM = 2048
NH = 16
KVH = 4
INTER = 8192
T = 2048
HALO = 1024
TLOC = T + HALO
SCALE = 0.08838834764831845
EPS = 1e-5
NEG = -1.0e30

SEM_NAMES = (
    "pe", "dve", "act", "pool",
    "dK",
    "dXH0", "dXH1", "dXH2",
    "dXT",
    "dQ0", "dQ1",
    "dKW0", "dKW1", "dKW2", "dKW3",
    "dG0", "dG1",
    "dU0", "dU1",
    "dD0", "dD1", "dD2", "dD3",
    "dR0", "dR1",
    "dW2",
    "dX2a", "dX2b",
    "dOa", "dOb",
)


class Bld:
    def __init__(self, nc, ctx):
        self.nc = nc
        self.prog = {k: [] for k in ("t", "v", "s", "g", "y")}
        self.M = {}
        self.sems = {}
        for n in SEM_NAMES:
            self.sems[n] = ctx.enter_context(nc.semaphore(n))
            self.M[n] = 0

    def add(self, eng, build, waits=(), incs=()):
        waits = tuple((s, v) for (s, v) in waits if v > 0)
        for s, v in waits:
            assert v <= self.M[s], f"wait {s}>={v} but only {self.M[s]} issued"
        incs = tuple(incs)
        sems = self.sems

        def thunk(e, _b=build, _w=waits, _i=incs):
            for s, v in _w:
                e.wait_ge(sems[s], v)
            ins = _b(e)
            for s, a in _i:
                ins.then_inc(sems[s], a)
        self.prog[eng].append(thunk)
        for s, a in incs:
            self.M[s] += a

    def mk(self, eng, build, waits=()):
        c = {"t": "pe", "v": "dve", "s": "act", "g": "pool"}[eng]
        self.add(eng, build, waits, incs=((c, 1),))
        return self.M[c]

    def dma(self, eng, sem, out_ap, in_ap, waits=(), transpose=False):
        if transpose:
            self.add(eng, lambda e, o=out_ap, i=in_ap:
                     e.dma_start(o, i, transpose=True), waits,
                     incs=((sem, 16),))
        else:
            self.add(eng, lambda e, o=out_ap, i=in_ap: e.dma_start(o, i),
                     waits, incs=((sem, 16),))
        return self.M[sem]

    def floors(self, *engines):
        m = {"t": "pe", "v": "dve", "s": "act", "g": "pool"}
        return [(m[e], self.M[m[e]]) for e in engines]


def build_nc():
    nc = bass.Bass("TRN2", target_bir_lowering=False, debug=False)

    env = {}
    env["x_pad"] = nc.dram_tensor("x_pad", [TLOC, DM], F32, kind="ExternalInput").ap()
    env["kval"] = nc.dram_tensor("kval", [1, HALO], BF16, kind="ExternalInput").ap()
    env["wq"] = nc.dram_tensor("wq", [DM, DM], F32, kind="ExternalInput").ap()
    env["wk"] = nc.dram_tensor("wk", [512, DM], F32, kind="ExternalInput").ap()
    env["wv"] = nc.dram_tensor("wv", [DM, 512], F32, kind="ExternalInput").ap()
    env["wo"] = nc.dram_tensor("wo", [DM, DM], F32, kind="ExternalInput").ap()
    env["wg"] = nc.dram_tensor("wg", [4096, 4096], I8, kind="ExternalInput").ap()
    env["wu"] = nc.dram_tensor("wu", [4096, 4096], I8, kind="ExternalInput").ap()
    env["wd"] = nc.dram_tensor("wd", [8192, 2048], I8, kind="ExternalInput").ap()
    env["sg_in"] = nc.dram_tensor("sg_in", [64, 128], F32, kind="ExternalInput").ap()
    env["su_in"] = nc.dram_tensor("su_in", [64, 128], F32, kind="ExternalInput").ap()
    env["sd_in"] = nc.dram_tensor("sd_in", [1, DM], F32, kind="ExternalInput").ap()

    env["xbf_hbm"] = nc.dram_tensor("xbf_hbm", [TLOC, DM], BF16, kind="Internal").ap()
    env["x2_hbm"] = nc.dram_tensor("x2_hbm", [T, DM], F32, kind="Internal").ap()
    env["x2bf_hbm"] = nc.dram_tensor("x2bf_hbm", [T, DM], BF16, kind="Internal").ap()
    env["out"] = nc.dram_tensor("out", [T, DM], F32, kind="ExternalOutput").ap()

    ctx = ExitStack()
    with ctx:
        b = Bld(nc, ctx)
        M = b.M

        env["ident_bf"] = ctx.enter_context(nc.sbuf_tensor("ident_bf", [128, 128], BF16)).ap()
        env["identf"] = ctx.enter_context(nc.sbuf_tensor("identf", [128, 128], F32)).ap()
        env["mask2"] = ctx.enter_context(nc.sbuf_tensor("mask2", [128, 256], BF16)).ap()
        env["ones_bf"] = ctx.enter_context(nc.sbuf_tensor("ones_bf", [1, 128], BF16)).ap()
        env["ones_f"] = ctx.enter_context(nc.sbuf_tensor("ones_f", [1, 128], F32)).ap()
        env["kval_sb"] = ctx.enter_context(nc.sbuf_tensor("kval_sb", [1, HALO], BF16)).ap()
        env["sg_sb"] = ctx.enter_context(nc.sbuf_tensor("sg_sb", [128, 64], F32)).ap()
        env["su_sb"] = ctx.enter_context(nc.sbuf_tensor("su_sb", [128, 64], F32)).ap()

        ident_bf, identf, mask2 = env["ident_bf"], env["identf"], env["mask2"]
        ones_bf, ones_f = env["ones_bf"], env["ones_f"]

        b.add("g", lambda e: e.memset(ident_bf, 0.0))
        b.add("g", lambda e: e.affine_select(out=ident_bf, in_=ident_bf,
              compare_op=ALU.not_equal, fill=1.0, base=0,
              pattern=[[-1, 128]], channel_multiplier=1))
        b.add("g", lambda e: e.memset(identf, 0.0))
        b.add("g", lambda e: e.affine_select(out=identf, in_=identf,
              compare_op=ALU.not_equal, fill=1.0, base=0,
              pattern=[[-1, 128]], channel_multiplier=1))
        b.add("g", lambda e: e.memset(mask2, 0.0))
        b.add("g", lambda e: e.affine_select(out=mask2[:, 0:128], in_=mask2[:, 0:128],
              compare_op=ALU.is_ge, fill=NEG, base=0,
              pattern=[[-1, 128]], channel_multiplier=1))
        b.add("g", lambda e: e.affine_select(out=mask2[:, 128:256], in_=mask2[:, 128:256],
              compare_op=ALU.is_ge, fill=NEG, base=0,
              pattern=[[1, 128]], channel_multiplier=-1))
        b.add("g", lambda e: e.memset(ones_bf, 1.0))
        env["consts_done"] = b.mk("g", lambda e: e.memset(ones_f, 1.0))

        def emit_cast(c):
            b.dma("g", f"dXH{c}", env["xbf_hbm"][c * 1024:(c + 1) * 1024, :],
                  env["x_pad"][c * 1024:(c + 1) * 1024, :])
        env["emit_cast"] = emit_cast

        env["k_loads"] = b.dma("y", "dK", env["kval_sb"], env["kval"])

        st = {
            "pq_free": [0, 0], "pv_free": [0, 0],
            "wq_free": [0, 0], "wk_free": [0, 0, 0, 0],
            "nQ": {0: 0, 1: 0}, "nKW": {0: 0, 1: 0, 2: 0, 3: 0},
            "qkv_ready": 0,
            "xres_n": 0, "xres_used": 0,
            "x2_spill": 0, "nX2": 0,
            "wo_mark": 0, "av_mark": 0,
            "xTp_used": 0,
            "scales_done": 0,
        }

        with ExitStack() as sup:
            env["KT"] = sup.enter_context(nc.sbuf_tensor("KT", [128, KVH, TLOC], BF16)).ap()
            env["Vn"] = sup.enter_context(nc.sbuf_tensor("Vn", [128, 24, 512], BF16)).ap()
            for half in (0, 1):
                emit_half(nc, b, st, half, env)

        emit_ffn(nc, b, st, env)

        b.add("y", lambda e: e.nop(),
              waits=(("dOa", M["dOa"]), ("dOb", M["dOb"])))

        with nc.Block() as block:
            @block.tensor
            def _(t):
                for f in b.prog["t"]:
                    f(t)

            @block.vector
            def _(v):
                for f in b.prog["v"]:
                    f(v)

            @block.scalar
            def _(s):
                for f in b.prog["s"]:
                    f(s)

            @block.gpsimd
            def _(g):
                for f in b.prog["g"]:
                    f(g)

            @block.sync
            def _(y):
                for f in b.prog["y"]:
                    f(y)

    return nc


def emit_half(nc, b, st, half, env):
    M = b.M
    KT, Vn = env["KT"], env["Vn"]
    ident_bf, identf = env["ident_bf"], env["identf"]
    mask2, ones_bf, ones_f = env["mask2"], env["ones_bf"], env["ones_f"]
    kval_sb = env["kval_sb"]
    sg_sb, su_sb = env["sg_sb"], env["su_sb"]
    x_pad, xbf_hbm = env["x_pad"], env["xbf_hbm"]
    wq, wk, wv, wo = env["wq"], env["wk"], env["wv"], env["wo"]
    x2_hbm, x2bf_hbm = env["x2_hbm"], env["x2bf_hbm"]

    chunks = [1, 0] if half == 0 else [2]
    fl_pe = b.floors("v", "s") if half == 1 else []
    fl_dma = b.floors("t", "v", "s") if half == 1 else []
    first_pool = [True]
    first_sync = [True]
    first_pe = [True]

    def poolw(extra=()):
        w = list(extra)
        if first_pool[0]:
            w += fl_dma
            first_pool[0] = False
        return w

    def syncw(extra=()):
        w = list(extra)
        if first_sync[0]:
            w += fl_dma
            first_sync[0] = False
        return w

    def pew(extra=()):
        w = list(extra)
        if first_pe[0]:
            w += fl_pe
            first_pe[0] = False
        return w

    with ExitStack() as hs:
        QT = hs.enter_context(nc.sbuf_tensor(f"QT{half}", [128, NH, 1024], BF16)).ap()

        with ExitStack() as qs:
            if half == 0:
                env["emit_cast"](1)
                xT_for = {c: qs.enter_context(
                    nc.sbuf_tensor(f"xT{c}", [128, 16, 1024], BF16)).ap()
                    for c in (0, 1)}
            else:
                xT_for = {2: qs.enter_context(
                    nc.sbuf_tensor("xT2", [128, 16, 1024], BF16)).ap()}
            WvSB = qs.enter_context(nc.sbuf_tensor(f"WvSB{half}", [128, 16, 512], BF16)).ap()
            wq_buf = [qs.enter_context(nc.sbuf_tensor(f"wqb{half}_{i}", [128, 2048], BF16)).ap()
                      for i in range(2)]
            wk_buf = [qs.enter_context(nc.sbuf_tensor(f"wkb{half}_{i}", [128, 2048], BF16)).ap()
                      for i in range(4)]
            pq = [qs.enter_context(nc.psum_tensor(f"pq{half}_{i}", [128, 1024], F32)).ap()
                  for i in range(2)]
            pv = [qs.enter_context(nc.psum_tensor(f"pv{half}_{i}", [128, 512], F32)).ap()
                  for i in range(2)]

            if half == 0:
                sgt_tmp = qs.enter_context(nc.sbuf_tensor("sgt_tmp", [64, 256], F32)).ap()
                b.dma("y", "dK", sgt_tmp[:, 0:128], env["sg_in"])
                k_loads2 = b.dma("y", "dK", sgt_tmp[:, 128:256], env["su_in"])
                m = b.mk("t", lambda e: e.transpose(
                    pq[0][:, 0:64], sgt_tmp[:, 0:128], identf[0:64, 0:64]),
                    waits=pew([("dK", k_loads2), ("pool", env["consts_done"])]))
                m = b.mk("t", lambda e: e.transpose(
                    pq[0][:, 64:128], sgt_tmp[:, 128:256], identf[0:64, 0:64]))
                v1 = b.mk("v", lambda e: e.tensor_copy(sg_sb, pq[0][:, 0:64]),
                          waits=(("pe", m),))
                v1 = b.mk("v", lambda e: e.tensor_copy(su_sb, pq[0][:, 64:128]))
                st["pq_free"][0] = v1
                st["pq_free"][1] = v1

            wv_loads = 0
            for c in chunks:
                xTa = xT_for[c]
                first_chunk = (c == chunks[0])
                w = [(f"dXH{c}", 16)]
                if half == 1 and c == 2:
                    w += fl_dma + [("dW2", 16 * st["x2_spill"]),
                                   ("dX2a", 16 * st["nX2"]),
                                   ("dR0", 16 * st["xres_n"])]
                xt_base = M["dXT"]
                for dm in range(16):
                    b.dma("s", "dXT", xTa[:, dm, :],
                          xbf_hbm[c * 1024:(c + 1) * 1024, dm * 128:(dm + 1) * 128],
                          waits=w if dm == 0 else (), transpose=True)
                xt_ready = ("dXT", M["dXT"])

                for kh in range(KVH):
                    b.dma("g", f"dKW{kh}", wk_buf[kh],
                          wk[kh * 128:(kh + 1) * 128, :],
                          waits=poolw([("pe", st["wk_free"][kh])]))
                    st["nKW"][kh] += 1
                if first_chunk:
                    for dm in range(16):
                        wv_loads = b.dma("g", "dK", WvSB[:, dm, :],
                                         wv[dm * 128:(dm + 1) * 128, :],
                                         waits=poolw([xt_ready]) if dm == 0
                                         else poolw())
                for kh in range(KVH):
                    par = kh % 2
                    pm = None
                    for dm in range(16):
                        for nb in range(2):
                            w0 = []
                            if dm == 0 and nb == 0:
                                w0 = pew([(f"dKW{kh}", 16 * st["nKW"][kh]),
                                          ("dve", st["pq_free"][par])])
                                if not (first_chunk and kh == 0):
                                    w0.append(xt_ready)
                            if first_chunk and kh == 0 and nb == 0:
                                w0 = list(w0) + [("dXT", xt_base + 16 * (dm + 1))]
                            pm = b.mk("t", lambda e, dm=dm, nb=nb, kh=kh, par=par, xTa=xTa:
                                      e.matmul(pq[par][:, nb * 512:(nb + 1) * 512],
                                               wk_buf[kh][:, dm * 128:(dm + 1) * 128],
                                               xTa[:, dm, nb * 512:(nb + 1) * 512],
                                               start=(dm == 0), stop=(dm == 15)),
                                      waits=w0)
                    st["wk_free"][kh] = pm
                    ev = b.mk("v", lambda e, kh=kh, c=c, par=par:
                              e.tensor_copy(KT[:, kh, c * 1024:(c + 1) * 1024], pq[par]),
                              waits=(("pe", pm),))
                    st["pq_free"][par] = ev
                if half == 0 and c == 1:
                    for qh in (0, 1):
                        par = qh % 2
                        b.dma("g", f"dQ{par}", wq_buf[par],
                              wq[qh * 128:(qh + 1) * 128, :],
                              waits=poolw([("pe", st["wq_free"][par])]))
                        st["nQ"][par] += 1
                    st["q_pre"] = 2
                    env["emit_cast"](0)

                for tt in range(8):
                    par = tt % 2
                    pm = None
                    for dm in range(16):
                        w0 = ()
                        if dm == 0:
                            w0 = pew([("dK", wv_loads), xt_ready,
                                      ("dve", st["pv_free"][par])])
                        pm = b.mk("t", lambda e, tt=tt, dm=dm, par=par, xTa=xTa:
                                  e.matmul(pv[par],
                                           xTa[:, dm, tt * 128:(tt + 1) * 128],
                                           WvSB[:, dm, :],
                                           start=(dm == 0), stop=(dm == 15)),
                                  waits=w0)
                    ev = b.mk("v", lambda e, tt=tt, c=c, par=par:
                              e.tensor_copy(Vn[:, c * 8 + tt, :], pv[par]),
                              waits=(("pe", pm),))
                    st["pv_free"][par] = ev

                if c == half + 1:
                    for qh in range(NH):
                        par = qh % 2
                        if qh >= st.get("q_pre", 0):
                            b.dma("g", f"dQ{par}", wq_buf[par],
                                  wq[qh * 128:(qh + 1) * 128, :],
                                  waits=poolw([("pe", st["wq_free"][par])]))
                            st["nQ"][par] += 1
                        pm = None
                        for dm in range(16):
                            for nb in range(2):
                                w0 = ()
                                if dm == 0 and nb == 0:
                                    w0 = pew([(f"dQ{par}", 16 * st["nQ"][par]),
                                              xt_ready, ("dve", st["pq_free"][par])])
                                pm = b.mk("t", lambda e, qh=qh, dm=dm, nb=nb, par=par, xTa=xTa:
                                          e.matmul(pq[par][:, nb * 512:(nb + 1) * 512],
                                                   wq_buf[par][:, dm * 128:(dm + 1) * 128],
                                                   xTa[:, dm, nb * 512:(nb + 1) * 512],
                                                   start=(dm == 0), stop=(dm == 15)),
                                          waits=w0)
                        st["wq_free"][par] = pm
                        ev = b.mk("v", lambda e, qh=qh, par=par:
                                  e.tensor_scalar_mul(QT[:, qh, :], pq[par], SCALE),
                                  waits=(("pe", pm),))
                        st["pq_free"][par] = ev
                    st["q_pre"] = 0

            st["qkv_ready"] = M["dve"]

        with ExitStack() as ats:
            WoSB = ats.enter_context(nc.sbuf_tensor(f"WoSB{half}", [128, 16, DM], BF16)).ap()
            OT = ats.enter_context(nc.sbuf_tensor(f"OT{half}", [128, NH * 128], BF16)).ap()
            P_sb = [ats.enter_context(nc.sbuf_tensor(f"P{half}_{i}", [128, 1152], BF16)).ap()
                    for i in range(2)]
            Pn_sb = [ats.enter_context(nc.sbuf_tensor(f"Pn{half}_{i}", [128, 1152], BF16)).ap()
                     for i in range(2)]
            PT_sb = ats.enter_context(nc.sbuf_tensor(f"PT{half}", [128, 4608], BF16)).ap()
            h_sb = ats.enter_context(nc.sbuf_tensor(f"h{half}", [128, DM], F32)).ap()
            x2f = ats.enter_context(nc.sbuf_tensor(f"x2f{half}", [128, DM], F32)).ap()
            x2b = ats.enter_context(nc.sbuf_tensor(f"x2b{half}", [128, DM], BF16)).ap()
            xres = ats.enter_context(nc.sbuf_tensor(f"xres{half}", [128, DM], F32)).ap()
            tiny = ats.enter_context(nc.sbuf_tensor(f"tiny{half}", [128, 40], F32)).ap()
            psx = [ats.enter_context(nc.psum_tensor(f"psx{half}_{i}", [128, 1536], F32)).ap()
                   for i in range(2)]
            ppt = ats.enter_context(nc.psum_tensor(f"ppt{half}", [128, 1024], BF16)).ap()
            pav = ats.enter_context(nc.psum_tensor(f"pav{half}", [128, 512], F32)).ap()
            ppt_f32 = ppt.bitcast(F32)
            pav_bf = pav.bitcast(BF16)

            mx = [tiny[:, 0 + i:1 + i] for i in range(2)]
            mxb = [tiny[:, 2 + i:3 + i] for i in range(2)]
            ssum = [tiny[:, 4 + i:5 + i] for i in range(2)]
            rcp = [tiny[:, 6 + i:7 + i] for i in range(2)]
            bnst = tiny[:, 8:32]
            mv = tiny[:, 32:34]
            veps = tiny[:, 34:35]
            lnv = tiny[:, 35:36]
            rstd = tiny[:, 36:37]
            bm = tiny[:, 37:38]

            proj_pe = M["pe"]
            wo_loads = 0
            for h in range(16):
                wo_loads = b.dma("g", "dK", WoSB[:, h, :],
                                 wo[h * 128:(h + 1) * 128, :],
                                 waits=poolw([("pe", proj_pe)]) if h == 0
                                 else poolw())
            if half == 0:
                env["emit_cast"](2)
            wo_wait = {"v": wo_loads}


            exp_done = [0, 0]
            pn_used = [0, 0]
            pnorm = [0, 0]
            av_evac = 0
            wo_evac = [0, 0]
            ln_done = 0
            x2b_done = 0

            PT4 = PT_sb.rearrange("p (k h t) -> p k h t", h=4, t=128)
            ppt8 = ppt.rearrange("p (k t) -> p k t", t=128)

            def mk_ctx(qi):
                return {"qi": qi, "q": half * 8 + qi,
                        "sc": {}, "norm": {}, "trans": {},
                        "ev_main": {-1: ("act", 0)}, "ev_t8": {-1: 0}}

            def emit_scores(cx, hh):
                q, qi = cx["q"], cx["qi"]
                p = hh % 2
                ps = psx[p]
                w0 = pew([("act", exp_done[p]), ("dve", st["qkv_ready"])])
                widths = (512, 512, 128)
                pm = None
                for nb in range(3):
                    w = widths[nb]
                    k0 = q * 128 + nb * 512
                    r1w = max(0, min(HALO - k0, w)) if q <= 7 else 0
                    has_mask = nb != 1
                    pm = b.mk("t", lambda e, hh=hh, nb=nb, w=w, k0=k0, qi=qi,
                              ps=ps, alone=(r1w == 0 and not has_mask):
                              e.matmul(ps[:, nb * 512:nb * 512 + w],
                                       QT[:, hh, qi * 128:(qi + 1) * 128],
                                       KT[:, hh // 4, k0:k0 + w],
                                       start=True, stop=alone),
                              waits=w0)
                    w0 = ()
                    if r1w:
                        pm = b.mk("t", lambda e, nb=nb, rw=r1w, k0=k0, ps=ps,
                                  alone=(not has_mask):
                                  e.matmul(ps[:, nb * 512:nb * 512 + rw],
                                           ones_bf, kval_sb[:, k0:k0 + rw],
                                           start=False, stop=alone))
                    if has_mask:
                        moff = 0 if nb == 0 else 128
                        poff = 0 if nb == 0 else 1024
                        pm = b.mk("t", lambda e, ps=ps, moff=moff, poff=poff:
                                  e.matmul(ps[:, poff:poff + 128],
                                           mask2[:, moff:moff + 128],
                                           ident_bf,
                                           start=False, stop=True))
                    if nb == 0:
                        cx["sc"][(hh, "a")] = pm
                cx["sc"][hh] = pm

            def emit_m(cx, hh):
                p = hh % 2
                ps = psx[p]
                b.mk("v", lambda e, ps=ps, p=p: e.reduce_max(
                    mxb[p], ps[:, 0:512], axis=AX.X, negate=True),
                    waits=(("pe", cx["sc"][(hh, "a")]),))
                b.mk("v", lambda e, ps=ps, p=p: e.reduce_max(
                    mx[p], ps[:, 512:1152], axis=AX.X, negate=True),
                    waits=(("pe", cx["sc"][hh]),))
                cx["sc"][(hh, "m")] = b.mk("v", lambda e, p=p: e.tensor_tensor(
                    mx[p], mx[p], mxb[p], ALU.min))

            def emit_exp(cx, hh):
                p = hh % 2
                exp_done[p] = b.mk("s", lambda e, p=p: e.activation(
                    P_sb[p], psx[p][:, 0:1152], AF.Exp,
                    bias=mx[p], scale=1.0, accum_out=ssum[p]),
                    waits=(("dve", cx["sc"][(hh, "m")]), ("dve", pnorm[p])))

            def emit_rn(cx, hh):
                p = hh % 2
                rcpm = b.mk("v", lambda e, p=p: e.reciprocal(rcp[p], ssum[p]),
                            waits=(("act", exp_done[p]),))
                pnorm[p] = b.mk("v", lambda e, p=p: e.tensor_scalar_mul(
                    Pn_sb[p], P_sb[p], rcp[p]),
                    waits=(("pe", pn_used[p]), ("dve", rcpm)))
                cx["norm"][hh] = pnorm[p]

            def emit_trans(cx, hh):
                nonlocal av_evac
                p = hh % 2
                j = hh % 4
                w0 = [("dve", cx["norm"][hh]), cx["ev_main"][hh - 1],
                      ("dve", cx["ev_t8"][hh - 1])]
                if hh == 0:
                    w0 += [("dve", wo_evac[0]), ("dve", wo_evac[1])]
                tm = None
                for k in range(8):
                    tm = b.mk("t", lambda e, k=k, p=p: e.transpose(
                        ppt[:, k * 128:(k + 1) * 128],
                        Pn_sb[p][:, k * 128:(k + 1) * 128], ident_bf),
                        waits=w0 if k == 0 else ())
                    w0 = ()
                w8 = [("dve", av_evac)] if j == 0 else []
                tm8 = b.mk("t", lambda e, p=p: e.transpose(
                    pav_bf[:, 0:128], Pn_sb[p][:, 1024:1152], ident_bf),
                    waits=w8)
                pn_used[p] = tm8
                cx["trans"][hh] = (tm, tm8)

            def emit_ev(cx, hh):
                tm, tm8 = cx["trans"][hh]
                j = hh % 4
                w0 = [("pe", tm)]
                if j == 0:
                    w0.append(("pe", st["av_mark"]))
                if hh % 2 == 0:
                    cx["ev_main"][hh] = ("act", b.mk("s", lambda e, j=j: e.copy(
                        PT4[:, 0:8, j, :], ppt8), waits=w0))
                else:
                    cx["ev_main"][hh] = ("dve", b.mk("v", lambda e, j=j: e.tensor_copy(
                        PT4[:, 0:8, j, :], ppt8), waits=w0))
                cx["ev_t8"][hh] = b.mk("v", lambda e, j=j: e.tensor_copy(
                    PT4[:, 8, j, :], pav_bf[:, 0:128]),
                    waits=(("pe", tm8),))

            def emit_av(cx, kvg):
                nonlocal av_evac
                q = cx["q"]
                pm = None
                for k in range(9):
                    w0 = ()
                    if k == 0:
                        w0 = (cx["ev_main"][kvg * 4 + 3],
                              ("dve", cx["ev_t8"][kvg * 4 + 3]),
                              ("dve", av_evac))
                    pm = b.mk("t", lambda e, q=q, kvg=kvg, k=k:
                              e.matmul(pav,
                                       Vn[:, q + k, kvg * 128:(kvg + 1) * 128],
                                       PT_sb[:, k * 512:(k + 1) * 512],
                                       start=(k == 0), stop=(k == 8)),
                              waits=w0)
                st["av_mark"] = pm
                av_evac = b.mk("v", lambda e, kvg=kvg: e.tensor_copy(
                    OT[:, kvg * 512:(kvg + 1) * 512], pav),
                    waits=(("pe", pm), ("pe", st["wo_mark"])))

            def prelude(cx):
                for hh in (0, 1):
                    emit_scores(cx, hh)
                    emit_m(cx, hh)
                    emit_exp(cx, hh)
                emit_rn(cx, 0)

            def body(cx):
                q = cx["q"]
                b.dma("y", "dR0", xres,
                      x_pad[HALO + q * 128:HALO + (q + 1) * 128, :],
                      waits=syncw([("dve", st["xres_used"])]))
                st["xres_n"] += 1
                for hh in range(2, 16):
                    emit_scores(cx, hh)
                    emit_m(cx, hh)
                    emit_exp(cx, hh)
                    emit_rn(cx, hh - 1)
                    if hh == 2 and st.get("pending_ln1"):
                        st["pending_ln1"]()
                        st["pending_ln1"] = None
                    emit_trans(cx, hh - 2)
                    emit_ev(cx, hh - 2)
                    if (hh - 2) % 4 == 3:
                        emit_av(cx, (hh - 2) // 4)
                emit_rn(cx, 15)
                for hh in (14, 15):
                    emit_trans(cx, hh)
                    emit_ev(cx, hh)
                    if hh % 4 == 3:
                        emit_av(cx, hh // 4)

            def wo_block(cx):
                q = cx["q"]
                for nch in range(4):
                    par = nch % 2
                    pwo_v = ppt_f32 if par == 0 else pav
                    pm = None
                    for h in range(16):
                        w0 = ()
                        if h == 0:
                            w0 = [("dve", av_evac), ("dve", wo_evac[par])]
                            if nch == 0:
                                w0.append(cx["ev_main"][15])
                            if wo_wait["v"]:
                                w0.append(("dK", wo_wait["v"]))
                                wo_wait["v"] = 0
                        pm = b.mk("t", lambda e, nch=nch, h=h, pwo_v=pwo_v:
                                  e.matmul(pwo_v,
                                           OT[:, h * 128:(h + 1) * 128],
                                           WoSB[:, h, nch * 512:(nch + 1) * 512],
                                           start=(h == 0), stop=(h == 15)),
                                  waits=w0)
                    if nch == 3:
                        st["wo_mark"] = pm
                    wo_evac[par] = b.mk("v", lambda e, nch=nch, pwo_v=pwo_v:
                                        e.tensor_add(h_sb[:, nch * 512:(nch + 1) * 512],
                                                     pwo_v,
                                                     xres[:, nch * 512:(nch + 1) * 512]),
                                        waits=(("pe", pm),
                                               ("dR0", 16 * st["xres_n"])))
                st["xres_used"] = M["dve"]

                def emit_ln1(q=q):
                    bs = 0
                    for i4 in range(4):
                        bs = b.mk("v", lambda e, i4=i4: e.bn_stats(
                            bnst[:, 6 * i4:6 * i4 + 6], h_sb[:, 512 * i4:512 * (i4 + 1)]))
                    ag = b.mk("v", lambda e: e.bn_aggr(mv, bnst), waits=(("dve", bs),))
                    ve = b.mk("v", lambda e: e.tensor_scalar_add(veps, mv[:, 1:2], EPS),
                              waits=(("dve", ag),))
                    lm = b.mk("s", lambda e: e.activation(lnv, veps, AF.Ln),
                              waits=(("dve", ve),))
                    re = b.mk("s", lambda e: e.activation(rstd, lnv, AF.Exp, scale=-0.5),
                              waits=(("act", lm),))
                    bmv = b.mk("v", lambda e: e.tensor_scalar(
                        bm, mv[:, 0:1], rstd, -1.0, ALU.mult, ALU.mult),
                        waits=(("act", re),))
                    ln_done = b.mk("v", lambda e: e.tensor_scalar(
                        x2f, h_sb, rstd, bm, ALU.mult, ALU.add),
                        waits=(("dW2", 16 * st["x2_spill"]), ("dve", bmv)))
                    st["x2_spill"] += 1
                    b.dma("y", "dW2", x2_hbm[q * 128:(q + 1) * 128, :], x2f,
                          waits=(("dve", ln_done),))
                    cpy = b.mk("v", lambda e: e.tensor_copy(x2b, x2f),
                               waits=(("dX2a", 16 * st["nX2"]),))
                    st["nX2"] += 1
                    b.dma("y", "dX2a",
                          x2bf_hbm[q * 128:(q + 1) * 128, :], x2b,
                          waits=(("dve", cpy),))
                st["pending_ln1"] = emit_ln1

            cur = mk_ctx(0)
            prelude(cur)
            for qi in range(8):
                nxt = mk_ctx(qi + 1) if qi < 7 else None
                body(cur)
                if nxt is not None:
                    prelude(nxt)
                wo_block(cur)
                cur = nxt
            if st.get("pending_ln1"):
                st["pending_ln1"]()
                st["pending_ln1"] = None


def emit_ffn(nc, b, st, env):
    M = b.M
    sg_sb, su_sb = env["sg_sb"], env["su_sb"]
    x2_hbm, x2bf_hbm, out = env["x2_hbm"], env["x2bf_hbm"], env["out"]
    wg, wu, wd = env["wg"], env["wu"], env["wd"]

    fl_dma = b.floors("t", "v", "s")
    fl_pe = b.floors("v", "s")
    first = {"g": True, "y": True, "t": True}

    def gatew(kind, extra=()):
        w = list(extra)
        if first[kind]:
            w += fl_dma if kind in ("g", "y") else fl_pe
            first[kind] = False
        return w

    with ExitStack() as fs:
        fuse = fs.enter_context(nc.sbuf_tensor("fuse", [128, 64, 512], BF16)).ap()
        x2T = fs.enter_context(nc.sbuf_tensor("x2T", [128, 16, 512], BF16)).ap()
        wgb = [fs.enter_context(nc.sbuf_tensor(f"wgb{i}", [128, 4096], BF16)).ap()
               for i in range(2)]
        wub = [fs.enter_context(nc.sbuf_tensor(f"wub{i}", [128, 4096], BF16)).ap()
               for i in range(2)]
        wdb = [fs.enter_context(nc.sbuf_tensor(f"wdb{i}", [128, 2048], BF16)).ap()
               for i in range(4)]
        x2r = [fs.enter_context(nc.sbuf_tensor(f"x2r{i}", [128, DM], F32)).ap()
               for i in range(4)]
        sscr = [fs.enter_context(nc.sbuf_tensor(f"sscr{i}", [128, 512], F32)).ap()
                for i in range(2)]
        uscr = [fs.enter_context(nc.sbuf_tensor(f"uscr{i}", [128, 512], F32)).ap()
                for i in range(2)]
        ftiny = fs.enter_context(nc.sbuf_tensor("ftiny", [128, 40], F32)).ap()
        sdb = fs.enter_context(nc.sbuf_tensor("sdb", [128, DM], F32)).ap()
        sd_sb = fs.enter_context(nc.sbuf_tensor("sd_sb", [1, DM], F32)).ap()

        sdl = b.dma("y", "dK", sd_sb, env["sd_in"], waits=gatew("y"))
        with nc.psum_tensor("psdb", [128, 512], F32) as psdb_h:
            psdb = psdb_h.ap()
            v2 = 0
            for n in range(4):
                m2 = b.mk("t", lambda e, n=n: e.matmul(
                    psdb, env["ones_f"], sd_sb[:, 512 * n:512 * (n + 1)],
                    start=True, stop=True),
                    waits=gatew("t", [("dK", sdl)]) if n == 0 else
                    ((("dve", v2),) if v2 else ()))
                v2 = b.mk("v", lambda e, n=n: e.tensor_copy(
                    sdb[:, 512 * n:512 * (n + 1)], psdb),
                    waits=(("pe", m2),))
        sdb_done = v2

        bnst = ftiny[:, 0:24]
        mv = ftiny[:, 32:34]
        veps = ftiny[:, 34:35]
        lnv = ftiny[:, 35:36]
        rstd = ftiny[:, 36:37]
        bm = ftiny[:, 37:38]

        nG = {0: 0, 1: 0}
        nU = {0: 0, 1: 0}
        nD = {0: 0, 1: 0, 2: 0, 3: 0}
        nR = {0: st["xres_n"], 1: 0}
        wg_free = [0, 0]
        wu_free = [0, 0]
        wd_free = [0, 0, 0, 0]
        silu_mark = [0, 0]
        upsc_mark = [0, 0]
        fuse_mark = [0, 0]
        gate_done = 0
        down_done_pe = 0
        prev_evac = [0, 0]
        out_n = {0: 0, 1: 0}
        norm_mark = [0, 0]
        x2r_used = [0, 0, 0, 0]
        wd_pre = [0]

        all_spills = (("dW2", 16 * st["x2_spill"]),
                      ("dX2a", 16 * st["nX2"]))

        gu_pre = [0]

        def emit_gu_load(icb):
            bp = icb % 2
            b.dma("g", f"dG{bp}", wgb[bp], wg[icb * 128:(icb + 1) * 128, :],
                  waits=gatew("g", [("pe", wg_free[bp])]))
            nG[bp] += 1
            b.dma("g", f"dU{bp}", wub[bp], wu[icb * 128:(icb + 1) * 128, :],
                  waits=[("pe", wu_free[bp])])
            nU[bp] += 1

        def emit_wd_load(nch, ibp):
            bp = ibp % 4
            b.dma("g", f"dD{bp}", wdb[bp],
                  wd[(nch * 16 + ibp) * 128:(nch * 16 + ibp + 1) * 128, :],
                  waits=[("pe", wd_free[bp])])
            nD[bp] += 1

        def emit_x2t(f, extra):
            w = gatew("y", extra)
            for dm in range(16):
                b.dma("y", "dXT", x2T[:, dm, :],
                      x2bf_hbm[f * 512:(f + 1) * 512, dm * 128:(dm + 1) * 128],
                      waits=w if dm == 0 else (), transpose=True)
            return ("dXT", M["dXT"])

        x2t_next = None
        for f in range(4):
            x2t_ready = emit_x2t(0, list(all_spills)) if f == 0 else x2t_next

            with ExitStack() as gs:
                pg = [gs.enter_context(nc.psum_tensor(f"pg{f}_{i}", [128, 512], F32)).ap()
                      for i in range(2)]
                pu = [gs.enter_context(nc.psum_tensor(f"pu{f}_{i}", [128, 512], F32)).ap()
                      for i in range(2)]
                for icg in range(64):
                    icb = icg // 2
                    par = icg % 2
                    boff = (icg % 2) * 128
                    bp = icb % 2
                    if icg % 2 == 0 and icb >= gu_pre[0]:
                        emit_gu_load(icb)
                    if icg in (48, 52, 56, 60):
                        emit_wd_load(0, (icg - 48) // 4)
                    pm = None
                    for dm in range(16):
                        w0 = ()
                        if dm == 0:
                            w0 = gatew("t", [(f"dG{bp}", 16 * nG[bp]), x2t_ready,
                                             ("act", silu_mark[par]),
                                             ("dve", prev_evac[0]),
                                             ("dve", prev_evac[1]),
                                             ("dve", sdb_done)]
                                       if icg == 0 else
                                       [(f"dG{bp}", 16 * nG[bp]),
                                        ("act", silu_mark[par])])
                        pm = b.mk("t", lambda e, dm=dm, par=par, bp=bp, boff=boff:
                                  e.matmul(pg[par],
                                           wgb[bp][:, dm * 256 + boff:dm * 256 + boff + 128],
                                           x2T[:, dm, :],
                                           start=(dm == 0), stop=(dm == 15)),
                                  waits=w0)
                    gm = pm
                    for dm in range(16):
                        w0 = ()
                        if dm == 0:
                            w0 = ((f"dU{bp}", 16 * nU[bp]),
                                  ("act", upsc_mark[par]))
                        pm = b.mk("t", lambda e, dm=dm, par=par, bp=bp, boff=boff:
                                  e.matmul(pu[par],
                                           wub[bp][:, dm * 256 + boff:dm * 256 + boff + 128],
                                           x2T[:, dm, :],
                                           start=(dm == 0), stop=(dm == 15)),
                                  waits=w0)
                    wg_free[bp] = pm
                    wu_free[bp] = pm
                    silu_mark[par] = b.mk("s", lambda e, par=par, icg=icg:
                                          e.activation(sscr[par], pg[par], AF.Silu,
                                                       scale=sg_sb[:, icg:icg + 1]),
                                          waits=(("pe", gm),
                                                 ("dve", fuse_mark[par])))
                    upsc_mark[par] = b.mk("s", lambda e, par=par, icg=icg:
                                          e.activation(uscr[par], pu[par], AF.Copy,
                                                       scale=su_sb[:, icg:icg + 1]),
                                          waits=(("pe", pm),))
                    w0 = [("act", upsc_mark[par])]
                    if icg <= 1:
                        w0.append(("pe", down_done_pe))
                    fuse_mark[par] = b.mk("v", lambda e, par=par, icg=icg:
                                          e.tensor_mul(fuse[:, icg, :],
                                                       sscr[par], uscr[par]),
                                          waits=w0)
                gate_done = M["pe"]
                fuse_all = max(fuse_mark)
                gu_pre[0] = 0
                wd_pre[0] = 4

            if f < 3:
                x2t_next = emit_x2t(f + 1, [("pe", gate_done)])

            with ExitStack() as ds:
                pd = [[ds.enter_context(
                    nc.psum_tensor(f"pdn{f}_{g}_{t}", [128, 512], F32)).ap()
                    for t in range(4)] for g in range(2)]
                for tt in range(4):
                    tq = f * 4 + tt
                    par = tq % 2
                    xw = [(f"dO{'ab'[par]}", 16 * out_n[par])]
                    if tq == 0:
                        xw.append(("dW2", 16 * st["x2_spill"]))
                    b.dma("y", f"dR{par}", x2r[tt],
                          x2_hbm[tq * 128:(tq + 1) * 128, :], waits=xw)
                    nR[par] += 1
                sweep_evac = {}
                for nch in range(4):
                    grp = nch % 2
                    last_mm = {}
                    for ic in range(64):
                        ibp = ic // 4
                        bp = ibp % 4
                        if ic % 4 == 0 and not (nch == 0 and ibp < wd_pre[0]):
                            emit_wd_load(nch, ibp)
                            if f < 3 and ic == 16 and nch in (2, 3):
                                emit_gu_load(0 if nch == 2 else 1)
                        roff = ((ic // 2) % 2) * 1024 + (ic % 2) * 512
                        for tt in range(4):
                            w0 = []
                            if ic == 0 and tt == 0:
                                w0 = [(f"dD{bp}", 16 * nD[bp])]
                                if nch == 0:
                                    w0.append(("dve", fuse_all))
                                if nch >= 2:
                                    w0.append(("dve", sweep_evac[nch - 2]))
                            elif ic % 4 == 0 and tt == 0:
                                w0 = [(f"dD{bp}", 16 * nD[bp])]
                            pm = b.mk("t", lambda e, ic=ic, bp=bp, tt=tt,
                                      grp=grp, roff=roff:
                                      e.matmul(pd[grp][tt],
                                               fuse[:, ic, tt * 128:(tt + 1) * 128],
                                               wdb[bp][:, roff:roff + 512],
                                               start=(ic == 0), stop=(ic == 63)),
                                      waits=w0)
                            last_mm[tt] = pm
                        if ic % 4 == 3:
                            wd_free[bp] = M["pe"]
                    ev = 0
                    for tt in range(4):
                        par = tt % 2
                        w0 = [("pe", last_mm[tt])]
                        b.mk("v", lambda e, tt=tt, nch=nch, grp=grp:
                             e.tensor_mul(sscr[tt % 2],
                                          pd[grp][tt],
                                          sdb[:, nch * 512:(nch + 1) * 512]),
                             waits=w0)
                        w1 = []
                        if nch == 0 and tt < 2:
                            w1 = [(f"dR{par}", 16 * nR[par])]
                        ev = b.mk("v", lambda e, tt=tt, nch=nch:
                                  e.tensor_add(x2r[tt][:, nch * 512:(nch + 1) * 512],
                                               x2r[tt][:, nch * 512:(nch + 1) * 512],
                                               sscr[tt % 2]),
                                  waits=w1)
                    sweep_evac[nch] = ev
                down_done_pe = M["pe"]
                prev_evac[0] = sweep_evac[2]
                prev_evac[1] = sweep_evac[3]
                for tt in range(4):
                    tq = f * 4 + tt
                    par = tq % 2
                    bs = 0
                    for i4 in range(4):
                        bs = b.mk("v", lambda e, i4=i4, tt=tt: e.bn_stats(
                            bnst[:, 6 * i4:6 * i4 + 6],
                            x2r[tt][:, 512 * i4:512 * (i4 + 1)]))
                    ag = b.mk("v", lambda e: e.bn_aggr(mv, bnst),
                              waits=(("dve", bs),))
                    ve = b.mk("v", lambda e: e.tensor_scalar_add(
                        veps, mv[:, 1:2], EPS), waits=(("dve", ag),))
                    sq = b.mk("s", lambda e: e.activation(lnv, veps, AF.Sqrt),
                              waits=(("dve", ve),))
                    rc = b.mk("v", lambda e: e.reciprocal(rstd, lnv),
                              waits=(("act", sq),))
                    bmv = b.mk("v", lambda e: e.tensor_scalar(
                        bm, mv[:, 0:1], rstd, -1.0, ALU.mult, ALU.mult),
                        waits=(("dve", rc),))
                    norm_mark[par] = b.mk("v", lambda e, tt=tt: e.tensor_scalar(
                        x2r[tt], x2r[tt], rstd, bm, ALU.mult, ALU.add),
                        waits=(("dve", bmv),))
                    out_n[par] += 1
                    b.dma("y", f"dO{'ab'[par]}",
                          out[tq * 128:(tq + 1) * 128, :], x2r[tt],
                          waits=(("dve", norm_mark[par]),))
            gu_pre[0] = 2 if f < 3 else 0


_NC_CACHE = {}


def _get_nc():
    if "nc" not in _NC_CACHE:
        _NC_CACHE["nc"] = build_nc()
    return _NC_CACHE["nc"]


def kernel(L0_attn_hidden_states, L0_attn_Wq, L0_attn_Wk, L0_attn_Wv,
           L0_attn_Wo, L0_ffn_w_gate, L0_ffn_w_up, L0_ffn_w_down,
           L0_ffn_s_gate, L0_ffn_s_up, L0_ffn_s_down):
    x = np.ascontiguousarray(np.asarray(L0_attn_hidden_states, dtype=np.float32))
    wq = np.ascontiguousarray(
        np.asarray(L0_attn_Wq, dtype=np.float32)
        .reshape(16, 128, 16, 128).transpose(2, 1, 0, 3).reshape(2048, 2048))
    wk = np.ascontiguousarray(
        np.asarray(L0_attn_Wk, dtype=np.float32)
        .reshape(16, 128, 4, 128).transpose(2, 1, 0, 3).reshape(512, 2048))
    wv = np.ascontiguousarray(np.asarray(L0_attn_Wv, dtype=np.float32))
    wo = np.ascontiguousarray(np.asarray(L0_attn_Wo, dtype=np.float32))
    wg = np.ascontiguousarray(
        np.asarray(L0_ffn_w_gate, dtype=np.int8)
        .reshape(16, 128, 32, 256).transpose(2, 1, 0, 3).reshape(4096, 4096))
    wu = np.ascontiguousarray(
        np.asarray(L0_ffn_w_up, dtype=np.int8)
        .reshape(16, 128, 32, 256).transpose(2, 1, 0, 3).reshape(4096, 4096))
    wd = np.ascontiguousarray(
        np.asarray(L0_ffn_w_down, dtype=np.int8)
        .reshape(16, 2, 2, 128, 4, 512).transpose(4, 0, 3, 1, 2, 5)
        .reshape(8192, 2048))
    sg = np.ascontiguousarray(np.asarray(L0_ffn_s_gate, dtype=np.float32).reshape(64, 128))
    su = np.ascontiguousarray(np.asarray(L0_ffn_s_up, dtype=np.float32).reshape(64, 128))
    sd = np.ascontiguousarray(np.asarray(L0_ffn_s_down, dtype=np.float32).reshape(1, DM))

    B, S = 2, 8192
    in_maps = []
    for c in range(8):
        bb, s0 = c // 4, (c % 4) * T
        if s0 == 0:
            halo = np.zeros((HALO, DM), dtype=np.float32)
            kv = np.full((1, HALO), NEG, dtype=ml_dtypes.bfloat16)
        else:
            halo = x[bb, s0 - HALO:s0]
            kv = np.zeros((1, HALO), dtype=ml_dtypes.bfloat16)
        x_pad = np.ascontiguousarray(np.concatenate([halo, x[bb, s0:s0 + T]], axis=0))
        in_maps.append({
            "x_pad": x_pad, "kval": kv,
            "wq": wq, "wk": wk, "wv": wv, "wo": wo,
            "wg": wg, "wu": wu, "wd": wd,
            "sg_in": sg, "su_in": su, "sd_in": sd,
        })

    nc = _get_nc()
    res = run_bass_kernel_spmd(nc, in_maps, core_ids=list(range(8)))

    outp = np.empty((B, S, DM), dtype=np.float32)
    for c in range(8):
        bb, s0 = c // 4, (c % 4) * T
        outp[bb, s0:s0 + T] = res.results[c]["out"]
    return outp


# version 29
# speedup vs baseline: 1.0105x; 1.0021x over previous
import numpy as np
import ml_dtypes
from contextlib import ExitStack

import concourse.bass as bass
import concourse.mybir as mybir
from concourse.bass_utils import run_bass_kernel_spmd

F32 = mybir.dt.float32
BF16 = mybir.dt.bfloat16
I8 = mybir.dt.int8
AF = mybir.ActivationFunctionType
ALU = mybir.AluOpType
AX = mybir.AxisListType

DM = 2048
NH = 16
KVH = 4
INTER = 8192
T = 2048
HALO = 1024
TLOC = T + HALO
SCALE = 0.08838834764831845
EPS = 1e-5
NEG = -1.0e30

SEM_NAMES = (
    "pe", "dve", "act", "pool",
    "dK",
    "dXH0", "dXH1", "dXH2",
    "dXT",
    "dQ0", "dQ1",
    "dKW0", "dKW1", "dKW2", "dKW3",
    "dG0", "dG1",
    "dU0", "dU1",
    "dD0", "dD1", "dD2", "dD3",
    "dR0", "dR1",
    "dW2",
    "dX2a", "dX2b",
    "dOa", "dOb",
)


class Bld:
    def __init__(self, nc, ctx):
        self.nc = nc
        self.prog = {k: [] for k in ("t", "v", "s", "g", "y")}
        self.M = {}
        self.sems = {}
        for n in SEM_NAMES:
            self.sems[n] = ctx.enter_context(nc.semaphore(n))
            self.M[n] = 0

    def add(self, eng, build, waits=(), incs=()):
        waits = tuple((s, v) for (s, v) in waits if v > 0)
        for s, v in waits:
            assert v <= self.M[s], f"wait {s}>={v} but only {self.M[s]} issued"
        incs = tuple(incs)
        sems = self.sems

        def thunk(e, _b=build, _w=waits, _i=incs):
            for s, v in _w:
                e.wait_ge(sems[s], v)
            ins = _b(e)
            for s, a in _i:
                ins.then_inc(sems[s], a)
        self.prog[eng].append(thunk)
        for s, a in incs:
            self.M[s] += a

    def mk(self, eng, build, waits=()):
        c = {"t": "pe", "v": "dve", "s": "act", "g": "pool"}[eng]
        self.add(eng, build, waits, incs=((c, 1),))
        return self.M[c]

    def dma(self, eng, sem, out_ap, in_ap, waits=(), transpose=False):
        if transpose:
            self.add(eng, lambda e, o=out_ap, i=in_ap:
                     e.dma_start(o, i, transpose=True), waits,
                     incs=((sem, 16),))
        else:
            self.add(eng, lambda e, o=out_ap, i=in_ap: e.dma_start(o, i),
                     waits, incs=((sem, 16),))
        return self.M[sem]

    def floors(self, *engines):
        m = {"t": "pe", "v": "dve", "s": "act", "g": "pool"}
        return [(m[e], self.M[m[e]]) for e in engines]


def build_nc():
    nc = bass.Bass("TRN2", target_bir_lowering=False, debug=False)

    env = {}
    env["x_pad"] = nc.dram_tensor("x_pad", [TLOC, DM], F32, kind="ExternalInput").ap()
    env["kval"] = nc.dram_tensor("kval", [1, HALO], BF16, kind="ExternalInput").ap()
    env["wq"] = nc.dram_tensor("wq", [DM, DM], F32, kind="ExternalInput").ap()
    env["wk"] = nc.dram_tensor("wk", [512, DM], F32, kind="ExternalInput").ap()
    env["wv"] = nc.dram_tensor("wv", [DM, 512], F32, kind="ExternalInput").ap()
    env["wo"] = nc.dram_tensor("wo", [DM, DM], F32, kind="ExternalInput").ap()
    env["wg"] = nc.dram_tensor("wg", [4096, 4096], I8, kind="ExternalInput").ap()
    env["wu"] = nc.dram_tensor("wu", [4096, 4096], I8, kind="ExternalInput").ap()
    env["wd"] = nc.dram_tensor("wd", [8192, 2048], I8, kind="ExternalInput").ap()
    env["sg_in"] = nc.dram_tensor("sg_in", [64, 128], F32, kind="ExternalInput").ap()
    env["su_in"] = nc.dram_tensor("su_in", [64, 128], F32, kind="ExternalInput").ap()
    env["sd_in"] = nc.dram_tensor("sd_in", [1, DM], F32, kind="ExternalInput").ap()

    env["xbf_hbm"] = nc.dram_tensor("xbf_hbm", [TLOC, DM], BF16, kind="Internal").ap()
    env["x2_hbm"] = nc.dram_tensor("x2_hbm", [T, DM], F32, kind="Internal").ap()
    env["x2bf_hbm"] = nc.dram_tensor("x2bf_hbm", [T, DM], BF16, kind="Internal").ap()
    env["out"] = nc.dram_tensor("out", [T, DM], F32, kind="ExternalOutput").ap()

    ctx = ExitStack()
    with ctx:
        b = Bld(nc, ctx)
        M = b.M

        env["ident_bf"] = ctx.enter_context(nc.sbuf_tensor("ident_bf", [128, 128], BF16)).ap()
        env["identf"] = ctx.enter_context(nc.sbuf_tensor("identf", [128, 128], F32)).ap()
        env["mask2"] = ctx.enter_context(nc.sbuf_tensor("mask2", [128, 256], BF16)).ap()
        env["ones_bf"] = ctx.enter_context(nc.sbuf_tensor("ones_bf", [1, 128], BF16)).ap()
        env["ones_f"] = ctx.enter_context(nc.sbuf_tensor("ones_f", [1, 128], F32)).ap()
        env["kval_sb"] = ctx.enter_context(nc.sbuf_tensor("kval_sb", [1, HALO], BF16)).ap()
        env["sg_sb"] = ctx.enter_context(nc.sbuf_tensor("sg_sb", [128, 64], F32)).ap()
        env["su_sb"] = ctx.enter_context(nc.sbuf_tensor("su_sb", [128, 64], F32)).ap()

        ident_bf, identf, mask2 = env["ident_bf"], env["identf"], env["mask2"]
        ones_bf, ones_f = env["ones_bf"], env["ones_f"]

        b.add("g", lambda e: e.memset(ident_bf, 0.0))
        b.add("g", lambda e: e.affine_select(out=ident_bf, in_=ident_bf,
              compare_op=ALU.not_equal, fill=1.0, base=0,
              pattern=[[-1, 128]], channel_multiplier=1))
        b.add("g", lambda e: e.memset(identf, 0.0))
        b.add("g", lambda e: e.affine_select(out=identf, in_=identf,
              compare_op=ALU.not_equal, fill=1.0, base=0,
              pattern=[[-1, 128]], channel_multiplier=1))
        b.add("g", lambda e: e.memset(mask2, 0.0))
        b.add("g", lambda e: e.affine_select(out=mask2[:, 0:128], in_=mask2[:, 0:128],
              compare_op=ALU.is_ge, fill=NEG, base=0,
              pattern=[[-1, 128]], channel_multiplier=1))
        b.add("g", lambda e: e.affine_select(out=mask2[:, 128:256], in_=mask2[:, 128:256],
              compare_op=ALU.is_ge, fill=NEG, base=0,
              pattern=[[1, 128]], channel_multiplier=-1))
        b.add("g", lambda e: e.memset(ones_bf, 1.0))
        env["consts_done"] = b.mk("g", lambda e: e.memset(ones_f, 1.0))

        def emit_cast(c):
            b.dma("g", f"dXH{c}", env["xbf_hbm"][c * 1024:(c + 1) * 1024, :],
                  env["x_pad"][c * 1024:(c + 1) * 1024, :])
        env["emit_cast"] = emit_cast

        env["k_loads"] = b.dma("y", "dK", env["kval_sb"], env["kval"])

        st = {
            "pq_free": [0, 0], "pv_free": [0, 0],
            "wq_free": [0, 0], "wk_free": [0, 0, 0, 0],
            "nQ": {0: 0, 1: 0}, "nKW": {0: 0, 1: 0, 2: 0, 3: 0},
            "qkv_ready": 0,
            "xres_n": 0, "xres_used": 0,
            "x2_spill": 0, "nX2": 0,
            "wo_mark": 0, "av_mark": 0,
            "xTp_used": 0,
            "scales_done": 0,
        }

        with ExitStack() as sup:
            env["KT"] = sup.enter_context(nc.sbuf_tensor("KT", [128, KVH, TLOC], BF16)).ap()
            env["Vn"] = sup.enter_context(nc.sbuf_tensor("Vn", [128, 24, 512], BF16)).ap()
            for half in (0, 1):
                emit_half(nc, b, st, half, env)

        emit_ffn(nc, b, st, env)

        b.add("y", lambda e: e.nop(),
              waits=(("dOa", M["dOa"]), ("dOb", M["dOb"])))

        with nc.Block() as block:
            @block.tensor
            def _(t):
                for f in b.prog["t"]:
                    f(t)

            @block.vector
            def _(v):
                for f in b.prog["v"]:
                    f(v)

            @block.scalar
            def _(s):
                for f in b.prog["s"]:
                    f(s)

            @block.gpsimd
            def _(g):
                for f in b.prog["g"]:
                    f(g)

            @block.sync
            def _(y):
                for f in b.prog["y"]:
                    f(y)

    return nc


def emit_half(nc, b, st, half, env):
    M = b.M
    KT, Vn = env["KT"], env["Vn"]
    ident_bf, identf = env["ident_bf"], env["identf"]
    mask2, ones_bf, ones_f = env["mask2"], env["ones_bf"], env["ones_f"]
    kval_sb = env["kval_sb"]
    sg_sb, su_sb = env["sg_sb"], env["su_sb"]
    x_pad, xbf_hbm = env["x_pad"], env["xbf_hbm"]
    wq, wk, wv, wo = env["wq"], env["wk"], env["wv"], env["wo"]
    x2_hbm, x2bf_hbm = env["x2_hbm"], env["x2bf_hbm"]

    chunks = [1, 0] if half == 0 else [2]
    fl_pe = b.floors("v", "s") if half == 1 else []
    fl_dma = b.floors("t", "v", "s") if half == 1 else []
    first_pool = [True]
    first_sync = [True]
    first_pe = [True]

    def poolw(extra=()):
        w = list(extra)
        if first_pool[0]:
            w += fl_dma
            first_pool[0] = False
        return w

    def syncw(extra=()):
        w = list(extra)
        if first_sync[0]:
            w += fl_dma
            first_sync[0] = False
        return w

    def pew(extra=()):
        w = list(extra)
        if first_pe[0]:
            w += fl_pe
            first_pe[0] = False
        return w

    with ExitStack() as hs:
        QT = hs.enter_context(nc.sbuf_tensor(f"QT{half}", [128, NH, 1024], BF16)).ap()

        with ExitStack() as qs:
            if half == 0:
                env["emit_cast"](1)
                xT_for = {c: qs.enter_context(
                    nc.sbuf_tensor(f"xT{c}", [128, 16, 1024], BF16)).ap()
                    for c in (0, 1)}
            else:
                xT_for = {2: qs.enter_context(
                    nc.sbuf_tensor("xT2", [128, 16, 1024], BF16)).ap()}
            WvSB = qs.enter_context(nc.sbuf_tensor(f"WvSB{half}", [128, 16, 512], BF16)).ap()
            wq_buf = [qs.enter_context(nc.sbuf_tensor(f"wqb{half}_{i}", [128, 2048], BF16)).ap()
                      for i in range(2)]
            wk_buf = [qs.enter_context(nc.sbuf_tensor(f"wkb{half}_{i}", [128, 2048], BF16)).ap()
                      for i in range(4)]
            pq = [qs.enter_context(nc.psum_tensor(f"pq{half}_{i}", [128, 1024], F32)).ap()
                  for i in range(2)]
            pv = [qs.enter_context(nc.psum_tensor(f"pv{half}_{i}", [128, 512], F32)).ap()
                  for i in range(2)]

            if half == 0:
                sgt_tmp = qs.enter_context(nc.sbuf_tensor("sgt_tmp", [64, 256], F32)).ap()
                b.dma("y", "dK", sgt_tmp[:, 0:128], env["sg_in"])
                k_loads2 = b.dma("y", "dK", sgt_tmp[:, 128:256], env["su_in"])
                m = b.mk("t", lambda e: e.transpose(
                    pq[0][:, 0:64], sgt_tmp[:, 0:128], identf[0:64, 0:64]),
                    waits=pew([("dK", k_loads2), ("pool", env["consts_done"])]))
                m = b.mk("t", lambda e: e.transpose(
                    pq[0][:, 64:128], sgt_tmp[:, 128:256], identf[0:64, 0:64]))
                v1 = b.mk("v", lambda e: e.tensor_copy(sg_sb, pq[0][:, 0:64]),
                          waits=(("pe", m),))
                v1 = b.mk("v", lambda e: e.tensor_copy(su_sb, pq[0][:, 64:128]))
                st["pq_free"][0] = v1
                st["pq_free"][1] = v1

            wv_loads = 0
            for c in chunks:
                xTa = xT_for[c]
                first_chunk = (c == chunks[0])
                w = [(f"dXH{c}", 16)]
                if half == 1 and c == 2:
                    w += fl_dma + [("dW2", 16 * st["x2_spill"]),
                                   ("dX2a", 16 * st["nX2"]),
                                   ("dR0", 16 * st["xres_n"])]
                xt_base = M["dXT"]
                for dm in range(16):
                    b.dma("s", "dXT", xTa[:, dm, :],
                          xbf_hbm[c * 1024:(c + 1) * 1024, dm * 128:(dm + 1) * 128],
                          waits=w if dm == 0 else (), transpose=True)
                xt_ready = ("dXT", M["dXT"])

                for kh in range(KVH):
                    b.dma("g", f"dKW{kh}", wk_buf[kh],
                          wk[kh * 128:(kh + 1) * 128, :],
                          waits=poolw([("pe", st["wk_free"][kh])]))
                    st["nKW"][kh] += 1
                if first_chunk:
                    for dm in range(16):
                        wv_loads = b.dma("g", "dK", WvSB[:, dm, :],
                                         wv[dm * 128:(dm + 1) * 128, :],
                                         waits=poolw([xt_ready]) if dm == 0
                                         else poolw())
                for kh in range(KVH):
                    par = kh % 2
                    pm = None
                    for dm in range(16):
                        for nb in range(2):
                            w0 = []
                            if dm == 0 and nb == 0:
                                w0 = pew([(f"dKW{kh}", 16 * st["nKW"][kh]),
                                          ("dve", st["pq_free"][par])])
                                if not (first_chunk and kh == 0):
                                    w0.append(xt_ready)
                            if first_chunk and kh == 0 and nb == 0:
                                w0 = list(w0) + [("dXT", xt_base + 16 * (dm + 1))]
                            pm = b.mk("t", lambda e, dm=dm, nb=nb, kh=kh, par=par, xTa=xTa:
                                      e.matmul(pq[par][:, nb * 512:(nb + 1) * 512],
                                               wk_buf[kh][:, dm * 128:(dm + 1) * 128],
                                               xTa[:, dm, nb * 512:(nb + 1) * 512],
                                               start=(dm == 0), stop=(dm == 15)),
                                      waits=w0)
                    st["wk_free"][kh] = pm
                    ev = b.mk("v", lambda e, kh=kh, c=c, par=par:
                              e.tensor_copy(KT[:, kh, c * 1024:(c + 1) * 1024], pq[par]),
                              waits=(("pe", pm),))
                    st["pq_free"][par] = ev
                if half == 0 and c == 1:
                    for qh in (0, 1):
                        par = qh % 2
                        b.dma("g", f"dQ{par}", wq_buf[par],
                              wq[qh * 128:(qh + 1) * 128, :],
                              waits=poolw([("pe", st["wq_free"][par])]))
                        st["nQ"][par] += 1
                    st["q_pre"] = 2
                    env["emit_cast"](0)

                for tt in range(8):
                    par = tt % 2
                    pm = None
                    for dm in range(16):
                        w0 = ()
                        if dm == 0:
                            w0 = pew([("dK", wv_loads), xt_ready,
                                      ("dve", st["pv_free"][par])])
                        pm = b.mk("t", lambda e, tt=tt, dm=dm, par=par, xTa=xTa:
                                  e.matmul(pv[par],
                                           xTa[:, dm, tt * 128:(tt + 1) * 128],
                                           WvSB[:, dm, :],
                                           start=(dm == 0), stop=(dm == 15)),
                                  waits=w0)
                    ev = b.mk("v", lambda e, tt=tt, c=c, par=par:
                              e.tensor_copy(Vn[:, c * 8 + tt, :], pv[par]),
                              waits=(("pe", pm),))
                    st["pv_free"][par] = ev

                if c == half + 1:
                    for qh in range(NH):
                        par = qh % 2
                        if qh >= st.get("q_pre", 0):
                            b.dma("g", f"dQ{par}", wq_buf[par],
                                  wq[qh * 128:(qh + 1) * 128, :],
                                  waits=poolw([("pe", st["wq_free"][par])]))
                            st["nQ"][par] += 1
                        pm = None
                        for dm in range(16):
                            for nb in range(2):
                                w0 = ()
                                if dm == 0 and nb == 0:
                                    w0 = pew([(f"dQ{par}", 16 * st["nQ"][par]),
                                              xt_ready, ("dve", st["pq_free"][par])])
                                pm = b.mk("t", lambda e, qh=qh, dm=dm, nb=nb, par=par, xTa=xTa:
                                          e.matmul(pq[par][:, nb * 512:(nb + 1) * 512],
                                                   wq_buf[par][:, dm * 128:(dm + 1) * 128],
                                                   xTa[:, dm, nb * 512:(nb + 1) * 512],
                                                   start=(dm == 0), stop=(dm == 15)),
                                          waits=w0)
                        st["wq_free"][par] = pm
                        ev = b.mk("v", lambda e, qh=qh, par=par:
                                  e.tensor_scalar_mul(QT[:, qh, :], pq[par], SCALE),
                                  waits=(("pe", pm),))
                        st["pq_free"][par] = ev
                    st["q_pre"] = 0

            st["qkv_ready"] = M["dve"]

        with ExitStack() as ats:
            WoSB = ats.enter_context(nc.sbuf_tensor(f"WoSB{half}", [128, 16, DM], BF16)).ap()
            OT = ats.enter_context(nc.sbuf_tensor(f"OT{half}", [128, NH * 128], BF16)).ap()
            P_sb = [ats.enter_context(nc.sbuf_tensor(f"P{half}_{i}", [128, 1152], BF16)).ap()
                    for i in range(2)]
            Pn_sb = [ats.enter_context(nc.sbuf_tensor(f"Pn{half}_{i}", [128, 1152], BF16)).ap()
                     for i in range(2)]
            PT_sb = ats.enter_context(nc.sbuf_tensor(f"PT{half}", [128, 9216], BF16)).ap()
            h_sb = ats.enter_context(nc.sbuf_tensor(f"h{half}", [128, DM], F32)).ap()
            x2f = ats.enter_context(nc.sbuf_tensor(f"x2f{half}", [128, DM], F32)).ap()
            x2b = ats.enter_context(nc.sbuf_tensor(f"x2b{half}", [128, DM], BF16)).ap()
            xres = ats.enter_context(nc.sbuf_tensor(f"xres{half}", [128, DM], F32)).ap()
            tiny = ats.enter_context(nc.sbuf_tensor(f"tiny{half}", [128, 40], F32)).ap()
            psx = [ats.enter_context(nc.psum_tensor(f"psx{half}_{i}", [128, 1536], F32)).ap()
                   for i in range(2)]
            ppt = ats.enter_context(nc.psum_tensor(f"ppt{half}", [128, 1024], BF16)).ap()
            pav = ats.enter_context(nc.psum_tensor(f"pav{half}", [128, 512], F32)).ap()
            ppt_f32 = ppt.bitcast(F32)
            pav_bf = pav.bitcast(BF16)

            mx = [tiny[:, 0 + i:1 + i] for i in range(2)]
            mxb = [tiny[:, 2 + i:3 + i] for i in range(2)]
            ssum = [tiny[:, 4 + i:5 + i] for i in range(2)]
            rcp = [tiny[:, 6 + i:7 + i] for i in range(2)]
            bnst = tiny[:, 8:32]
            mv = tiny[:, 32:34]
            veps = tiny[:, 34:35]
            lnv = tiny[:, 35:36]
            rstd = tiny[:, 36:37]
            bm = tiny[:, 37:38]

            proj_pe = M["pe"]
            wo_loads = 0
            for h in range(16):
                wo_loads = b.dma("g", "dK", WoSB[:, h, :],
                                 wo[h * 128:(h + 1) * 128, :],
                                 waits=poolw([("pe", proj_pe)]) if h == 0
                                 else poolw())
            if half == 0:
                env["emit_cast"](2)
            wo_wait = {"v": wo_loads}


            exp_done = [0, 0]
            pn_used = [0, 0]
            pnorm = [0, 0]
            av_evac = 0
            wo_evac = [0, 0]
            ln_done = 0
            x2b_done = 0

            PT4 = PT_sb.rearrange("p (k h t) -> p k h t", h=8, t=128)
            ppt8 = ppt.rearrange("p (k t) -> p k t", t=128)

            def mk_ctx(qi):
                return {"qi": qi, "q": half * 8 + qi,
                        "sc": {}, "norm": {}, "trans": {},
                        "ev_main": {-1: ("act", 0)}, "ev_t8": {-1: 0}}

            def emit_scores(cx, hh):
                q, qi = cx["q"], cx["qi"]
                p = hh % 2
                ps = psx[p]
                w0 = pew([("act", exp_done[p]), ("dve", st["qkv_ready"])])
                widths = (512, 512, 128)
                pm = None
                for nb in range(3):
                    w = widths[nb]
                    k0 = q * 128 + nb * 512
                    r1w = max(0, min(HALO - k0, w)) if q <= 7 else 0
                    has_mask = nb != 1
                    pm = b.mk("t", lambda e, hh=hh, nb=nb, w=w, k0=k0, qi=qi,
                              ps=ps, alone=(r1w == 0 and not has_mask):
                              e.matmul(ps[:, nb * 512:nb * 512 + w],
                                       QT[:, hh, qi * 128:(qi + 1) * 128],
                                       KT[:, hh // 4, k0:k0 + w],
                                       start=True, stop=alone),
                              waits=w0)
                    w0 = ()
                    if r1w:
                        pm = b.mk("t", lambda e, nb=nb, rw=r1w, k0=k0, ps=ps,
                                  alone=(not has_mask):
                                  e.matmul(ps[:, nb * 512:nb * 512 + rw],
                                           ones_bf, kval_sb[:, k0:k0 + rw],
                                           start=False, stop=alone))
                    if has_mask:
                        moff = 0 if nb == 0 else 128
                        poff = 0 if nb == 0 else 1024
                        pm = b.mk("t", lambda e, ps=ps, moff=moff, poff=poff:
                                  e.matmul(ps[:, poff:poff + 128],
                                           mask2[:, moff:moff + 128],
                                           ident_bf,
                                           start=False, stop=True))
                    if nb == 0:
                        cx["sc"][(hh, "a")] = pm
                cx["sc"][hh] = pm

            def emit_m(cx, hh):
                p = hh % 2
                ps = psx[p]
                b.mk("v", lambda e, ps=ps, p=p: e.reduce_max(
                    mxb[p], ps[:, 0:512], axis=AX.X, negate=True),
                    waits=(("pe", cx["sc"][(hh, "a")]),))
                b.mk("v", lambda e, ps=ps, p=p: e.reduce_max(
                    mx[p], ps[:, 512:1152], axis=AX.X, negate=True),
                    waits=(("pe", cx["sc"][hh]),))
                cx["sc"][(hh, "m")] = b.mk("v", lambda e, p=p: e.tensor_tensor(
                    mx[p], mx[p], mxb[p], ALU.min))

            def emit_exp(cx, hh):
                p = hh % 2
                exp_done[p] = b.mk("s", lambda e, p=p: e.activation(
                    P_sb[p], psx[p][:, 0:1152], AF.Exp,
                    bias=mx[p], scale=1.0, accum_out=ssum[p]),
                    waits=(("dve", cx["sc"][(hh, "m")]), ("dve", pnorm[p])))

            def emit_rn(cx, hh):
                p = hh % 2
                rcpm = b.mk("v", lambda e, p=p: e.reciprocal(rcp[p], ssum[p]),
                            waits=(("act", exp_done[p]),))
                pnorm[p] = b.mk("v", lambda e, p=p: e.tensor_scalar_mul(
                    Pn_sb[p], P_sb[p], rcp[p]),
                    waits=(("pe", pn_used[p]), ("dve", rcpm)))
                cx["norm"][hh] = pnorm[p]

            def emit_trans(cx, hh):
                nonlocal av_evac
                p = hh % 2
                j = hh % 4
                w0 = [("dve", cx["norm"][hh]), cx["ev_main"][hh - 1],
                      ("dve", cx["ev_t8"][hh - 1])]
                if hh == 0:
                    w0 += [("dve", wo_evac[0]), ("dve", wo_evac[1])]
                tm = None
                for k in range(8):
                    tm = b.mk("t", lambda e, k=k, p=p: e.transpose(
                        ppt[:, k * 128:(k + 1) * 128],
                        Pn_sb[p][:, k * 128:(k + 1) * 128], ident_bf),
                        waits=w0 if k == 0 else ())
                    w0 = ()
                w8 = [("dve", av_evac)]
                tm8 = b.mk("t", lambda e, p=p: e.transpose(
                    pav_bf[:, 0:128], Pn_sb[p][:, 1024:1152], ident_bf),
                    waits=w8)
                pn_used[p] = tm8
                cx["trans"][hh] = (tm, tm8)

            def emit_ev(cx, hh):
                tm, tm8 = cx["trans"][hh]
                j = hh % 8
                w0 = [("pe", tm)]
                if j % 4 == 0:
                    w0.append(("pe", st["av_mark"]))
                if hh % 2 == 0:
                    cx["ev_main"][hh] = ("act", b.mk("s", lambda e, j=j: e.copy(
                        PT4[:, 0:8, j, :], ppt8), waits=w0))
                else:
                    cx["ev_main"][hh] = ("dve", b.mk("v", lambda e, j=j: e.tensor_copy(
                        PT4[:, 0:8, j, :], ppt8), waits=w0))
                cx["ev_t8"][hh] = b.mk("v", lambda e, j=j: e.tensor_copy(
                    PT4[:, 8, j, :], pav_bf[:, 0:128]),
                    waits=(("pe", tm8),))

            def emit_av(cx, kvg):
                nonlocal av_evac
                q = cx["q"]
                pm = None
                hoff = (kvg % 2) * 512
                for k in range(9):
                    w0 = ()
                    if k == 0:
                        w0 = (cx["ev_main"][kvg * 4 + 3],
                              ("dve", cx["ev_t8"][min(kvg * 4 + 5, 15)]),
                              ("dve", av_evac))
                    pm = b.mk("t", lambda e, q=q, kvg=kvg, k=k, hoff=hoff:
                              e.matmul(pav,
                                       Vn[:, q + k, kvg * 128:(kvg + 1) * 128],
                                       PT_sb[:, k * 1024 + hoff:k * 1024 + hoff + 512],
                                       start=(k == 0), stop=(k == 8)),
                              waits=w0)
                st["av_mark"] = pm
                av_evac = b.mk("v", lambda e, kvg=kvg: e.tensor_copy(
                    OT[:, kvg * 512:(kvg + 1) * 512], pav),
                    waits=(("pe", pm), ("pe", st["wo_mark"])))

            def prelude(cx):
                for hh in (0, 1):
                    emit_scores(cx, hh)
                    emit_m(cx, hh)
                    emit_exp(cx, hh)
                emit_rn(cx, 0)

            def body(cx):
                q = cx["q"]
                b.dma("y", "dR0", xres,
                      x_pad[HALO + q * 128:HALO + (q + 1) * 128, :],
                      waits=syncw([("dve", st["xres_used"])]))
                st["xres_n"] += 1
                for hh in range(2, 16):
                    emit_scores(cx, hh)
                    emit_m(cx, hh)
                    emit_exp(cx, hh)
                    emit_rn(cx, hh - 1)
                    if hh == 2 and st.get("pending_ln1"):
                        st["pending_ln1"]()
                        st["pending_ln1"] = None
                    emit_trans(cx, hh - 2)
                    emit_ev(cx, hh - 2)
                    if hh >= 7 and (hh - 7) % 4 == 0:
                        emit_av(cx, (hh - 7) // 4)
                emit_rn(cx, 15)
                for hh in (14, 15):
                    emit_trans(cx, hh)
                    emit_ev(cx, hh)
                    if hh % 4 == 3:
                        emit_av(cx, hh // 4)

            def wo_block(cx):
                q = cx["q"]
                for nch in range(4):
                    par = nch % 2
                    pwo_v = ppt_f32 if par == 0 else pav
                    pm = None
                    for h in range(16):
                        w0 = ()
                        if h == 0:
                            w0 = [("dve", av_evac), ("dve", wo_evac[par])]
                            if nch == 0:
                                w0.append(cx["ev_main"][15])
                            if wo_wait["v"]:
                                w0.append(("dK", wo_wait["v"]))
                                wo_wait["v"] = 0
                        pm = b.mk("t", lambda e, nch=nch, h=h, pwo_v=pwo_v:
                                  e.matmul(pwo_v,
                                           OT[:, h * 128:(h + 1) * 128],
                                           WoSB[:, h, nch * 512:(nch + 1) * 512],
                                           start=(h == 0), stop=(h == 15)),
                                  waits=w0)
                    if nch == 3:
                        st["wo_mark"] = pm
                    wo_evac[par] = b.mk("v", lambda e, nch=nch, pwo_v=pwo_v:
                                        e.tensor_add(h_sb[:, nch * 512:(nch + 1) * 512],
                                                     pwo_v,
                                                     xres[:, nch * 512:(nch + 1) * 512]),
                                        waits=(("pe", pm),
                                               ("dR0", 16 * st["xres_n"])))
                st["xres_used"] = M["dve"]

                def emit_ln1(q=q):
                    bs = 0
                    for i4 in range(4):
                        bs = b.mk("v", lambda e, i4=i4: e.bn_stats(
                            bnst[:, 6 * i4:6 * i4 + 6], h_sb[:, 512 * i4:512 * (i4 + 1)]))
                    ag = b.mk("v", lambda e: e.bn_aggr(mv, bnst), waits=(("dve", bs),))
                    ve = b.mk("v", lambda e: e.tensor_scalar_add(veps, mv[:, 1:2], EPS),
                              waits=(("dve", ag),))
                    lm = b.mk("s", lambda e: e.activation(lnv, veps, AF.Ln),
                              waits=(("dve", ve),))
                    re = b.mk("s", lambda e: e.activation(rstd, lnv, AF.Exp, scale=-0.5),
                              waits=(("act", lm),))
                    bmv = b.mk("v", lambda e: e.tensor_scalar(
                        bm, mv[:, 0:1], rstd, -1.0, ALU.mult, ALU.mult),
                        waits=(("act", re),))
                    ln_done = b.mk("v", lambda e: e.tensor_scalar(
                        x2f, h_sb, rstd, bm, ALU.mult, ALU.add),
                        waits=(("dW2", 16 * st["x2_spill"]), ("dve", bmv)))
                    st["x2_spill"] += 1
                    b.dma("y", "dW2", x2_hbm[q * 128:(q + 1) * 128, :], x2f,
                          waits=(("dve", ln_done),))
                    cpy = b.mk("v", lambda e: e.tensor_copy(x2b, x2f),
                               waits=(("dX2a", 16 * st["nX2"]),))
                    st["nX2"] += 1
                    b.dma("y", "dX2a",
                          x2bf_hbm[q * 128:(q + 1) * 128, :], x2b,
                          waits=(("dve", cpy),))
                st["pending_ln1"] = emit_ln1

            cur = mk_ctx(0)
            prelude(cur)
            for qi in range(8):
                nxt = mk_ctx(qi + 1) if qi < 7 else None
                body(cur)
                if nxt is not None:
                    prelude(nxt)
                wo_block(cur)
                cur = nxt
            if st.get("pending_ln1"):
                st["pending_ln1"]()
                st["pending_ln1"] = None


def emit_ffn(nc, b, st, env):
    M = b.M
    sg_sb, su_sb = env["sg_sb"], env["su_sb"]
    x2_hbm, x2bf_hbm, out = env["x2_hbm"], env["x2bf_hbm"], env["out"]
    wg, wu, wd = env["wg"], env["wu"], env["wd"]

    fl_dma = b.floors("t", "v", "s")
    fl_pe = b.floors("v", "s")
    first = {"g": True, "y": True, "t": True}

    def gatew(kind, extra=()):
        w = list(extra)
        if first[kind]:
            w += fl_dma if kind in ("g", "y") else fl_pe
            first[kind] = False
        return w

    with ExitStack() as fs:
        fuse = fs.enter_context(nc.sbuf_tensor("fuse", [128, 64, 512], BF16)).ap()
        x2T = fs.enter_context(nc.sbuf_tensor("x2T", [128, 16, 512], BF16)).ap()
        wgb = [fs.enter_context(nc.sbuf_tensor(f"wgb{i}", [128, 4096], BF16)).ap()
               for i in range(2)]
        wub = [fs.enter_context(nc.sbuf_tensor(f"wub{i}", [128, 4096], BF16)).ap()
               for i in range(2)]
        wdb = [fs.enter_context(nc.sbuf_tensor(f"wdb{i}", [128, 2048], BF16)).ap()
               for i in range(4)]
        x2r = [fs.enter_context(nc.sbuf_tensor(f"x2r{i}", [128, DM], F32)).ap()
               for i in range(4)]
        sscr = [fs.enter_context(nc.sbuf_tensor(f"sscr{i}", [128, 512], F32)).ap()
                for i in range(2)]
        uscr = [fs.enter_context(nc.sbuf_tensor(f"uscr{i}", [128, 512], F32)).ap()
                for i in range(2)]
        ftiny = fs.enter_context(nc.sbuf_tensor("ftiny", [128, 40], F32)).ap()
        sdb = fs.enter_context(nc.sbuf_tensor("sdb", [128, DM], F32)).ap()
        sd_sb = fs.enter_context(nc.sbuf_tensor("sd_sb", [1, DM], F32)).ap()

        sdl = b.dma("y", "dK", sd_sb, env["sd_in"], waits=gatew("y"))
        with nc.psum_tensor("psdb", [128, 512], F32) as psdb_h:
            psdb = psdb_h.ap()
            v2 = 0
            for n in range(4):
                m2 = b.mk("t", lambda e, n=n: e.matmul(
                    psdb, env["ones_f"], sd_sb[:, 512 * n:512 * (n + 1)],
                    start=True, stop=True),
                    waits=gatew("t", [("dK", sdl)]) if n == 0 else
                    ((("dve", v2),) if v2 else ()))
                v2 = b.mk("v", lambda e, n=n: e.tensor_copy(
                    sdb[:, 512 * n:512 * (n + 1)], psdb),
                    waits=(("pe", m2),))
        sdb_done = v2

        bnst = ftiny[:, 0:24]
        mv = ftiny[:, 32:34]
        veps = ftiny[:, 34:35]
        lnv = ftiny[:, 35:36]
        rstd = ftiny[:, 36:37]
        bm = ftiny[:, 37:38]

        nG = {0: 0, 1: 0}
        nU = {0: 0, 1: 0}
        nD = {0: 0, 1: 0, 2: 0, 3: 0}
        nR = {0: st["xres_n"], 1: 0}
        wg_free = [0, 0]
        wu_free = [0, 0]
        wd_free = [0, 0, 0, 0]
        silu_mark = [0, 0]
        upsc_mark = [0, 0]
        fuse_mark = [0, 0]
        gate_done = 0
        down_done_pe = 0
        prev_evac = [0, 0]
        out_n = {0: 0, 1: 0}
        norm_mark = [0, 0]
        x2r_used = [0, 0, 0, 0]
        wd_pre = [0]

        all_spills = (("dW2", 16 * st["x2_spill"]),
                      ("dX2a", 16 * st["nX2"]))

        gu_pre = [0]

        def emit_gu_load(icb):
            bp = icb % 2
            b.dma("g", f"dG{bp}", wgb[bp], wg[icb * 128:(icb + 1) * 128, :],
                  waits=gatew("g", [("pe", wg_free[bp])]))
            nG[bp] += 1
            b.dma("g", f"dU{bp}", wub[bp], wu[icb * 128:(icb + 1) * 128, :],
                  waits=[("pe", wu_free[bp])])
            nU[bp] += 1

        def emit_wd_load(nch, ibp):
            bp = ibp % 4
            b.dma("g", f"dD{bp}", wdb[bp],
                  wd[(nch * 16 + ibp) * 128:(nch * 16 + ibp + 1) * 128, :],
                  waits=[("pe", wd_free[bp])])
            nD[bp] += 1

        def emit_x2t(f, extra):
            w = gatew("y", extra)
            for dm in range(16):
                b.dma("y", "dXT", x2T[:, dm, :],
                      x2bf_hbm[f * 512:(f + 1) * 512, dm * 128:(dm + 1) * 128],
                      waits=w if dm == 0 else (), transpose=True)
            return ("dXT", M["dXT"])

        x2t_next = None
        for f in range(4):
            x2t_ready = emit_x2t(0, list(all_spills)) if f == 0 else x2t_next

            with ExitStack() as gs:
                pg = [gs.enter_context(nc.psum_tensor(f"pg{f}_{i}", [128, 512], F32)).ap()
                      for i in range(2)]
                pu = [gs.enter_context(nc.psum_tensor(f"pu{f}_{i}", [128, 512], F32)).ap()
                      for i in range(2)]
                for icg in range(64):
                    icb = icg // 2
                    par = icg % 2
                    boff = (icg % 2) * 128
                    bp = icb % 2
                    if icg % 2 == 0 and icb >= gu_pre[0]:
                        emit_gu_load(icb)
                    if icg in (48, 52, 56, 60):
                        emit_wd_load(0, (icg - 48) // 4)
                    pm = None
                    for dm in range(16):
                        w0 = ()
                        if dm == 0:
                            w0 = gatew("t", [(f"dG{bp}", 16 * nG[bp]), x2t_ready,
                                             ("act", silu_mark[par]),
                                             ("dve", prev_evac[0]),
                                             ("dve", prev_evac[1]),
                                             ("dve", sdb_done)]
                                       if icg == 0 else
                                       [(f"dG{bp}", 16 * nG[bp]),
                                        ("act", silu_mark[par])])
                        pm = b.mk("t", lambda e, dm=dm, par=par, bp=bp, boff=boff:
                                  e.matmul(pg[par],
                                           wgb[bp][:, dm * 256 + boff:dm * 256 + boff + 128],
                                           x2T[:, dm, :],
                                           start=(dm == 0), stop=(dm == 15)),
                                  waits=w0)
                    gm = pm
                    for dm in range(16):
                        w0 = ()
                        if dm == 0:
                            w0 = ((f"dU{bp}", 16 * nU[bp]),
                                  ("act", upsc_mark[par]))
                        pm = b.mk("t", lambda e, dm=dm, par=par, bp=bp, boff=boff:
                                  e.matmul(pu[par],
                                           wub[bp][:, dm * 256 + boff:dm * 256 + boff + 128],
                                           x2T[:, dm, :],
                                           start=(dm == 0), stop=(dm == 15)),
                                  waits=w0)
                    wg_free[bp] = pm
                    wu_free[bp] = pm
                    silu_mark[par] = b.mk("s", lambda e, par=par, icg=icg:
                                          e.activation(sscr[par], pg[par], AF.Silu,
                                                       scale=sg_sb[:, icg:icg + 1]),
                                          waits=(("pe", gm),
                                                 ("dve", fuse_mark[par])))
                    upsc_mark[par] = b.mk("s", lambda e, par=par, icg=icg:
                                          e.activation(uscr[par], pu[par], AF.Copy,
                                                       scale=su_sb[:, icg:icg + 1]),
                                          waits=(("pe", pm),))
                    w0 = [("act", upsc_mark[par])]
                    if icg <= 1:
                        w0.append(("pe", down_done_pe))
                    fuse_mark[par] = b.mk("v", lambda e, par=par, icg=icg:
                                          e.tensor_mul(fuse[:, icg, :],
                                                       sscr[par], uscr[par]),
                                          waits=w0)
                gate_done = M["pe"]
                fuse_all = max(fuse_mark)
                gu_pre[0] = 0
                wd_pre[0] = 4

            if f < 3:
                x2t_next = emit_x2t(f + 1, [("pe", gate_done)])

            with ExitStack() as ds:
                pd = [[ds.enter_context(
                    nc.psum_tensor(f"pdn{f}_{g}_{t}", [128, 512], F32)).ap()
                    for t in range(4)] for g in range(2)]
                for tt in range(4):
                    tq = f * 4 + tt
                    par = tq % 2
                    xw = [(f"dO{'ab'[par]}", 16 * out_n[par])]
                    if tq == 0:
                        xw.append(("dW2", 16 * st["x2_spill"]))
                    b.dma("y", f"dR{par}", x2r[tt],
                          x2_hbm[tq * 128:(tq + 1) * 128, :], waits=xw)
                    nR[par] += 1
                sweep_evac = {}
                for nch in range(4):
                    grp = nch % 2
                    last_mm = {}
                    for ic in range(64):
                        ibp = ic // 4
                        bp = ibp % 4
                        if ic % 4 == 0 and not (nch == 0 and ibp < wd_pre[0]):
                            emit_wd_load(nch, ibp)
                            if f < 3 and ic == 16 and nch in (2, 3):
                                emit_gu_load(0 if nch == 2 else 1)
                        roff = ((ic // 2) % 2) * 1024 + (ic % 2) * 512
                        for tt in range(4):
                            w0 = []
                            if ic == 0 and tt == 0:
                                w0 = [(f"dD{bp}", 16 * nD[bp])]
                                if nch == 0:
                                    w0.append(("dve", fuse_all))
                                if nch >= 2:
                                    w0.append(("dve", sweep_evac[nch - 2]))
                            elif ic % 4 == 0 and tt == 0:
                                w0 = [(f"dD{bp}", 16 * nD[bp])]
                            pm = b.mk("t", lambda e, ic=ic, bp=bp, tt=tt,
                                      grp=grp, roff=roff:
                                      e.matmul(pd[grp][tt],
                                               fuse[:, ic, tt * 128:(tt + 1) * 128],
                                               wdb[bp][:, roff:roff + 512],
                                               start=(ic == 0), stop=(ic == 63)),
                                      waits=w0)
                            last_mm[tt] = pm
                        if ic % 4 == 3:
                            wd_free[bp] = M["pe"]
                    ev = 0
                    for tt in range(4):
                        par = tt % 2
                        w0 = [("pe", last_mm[tt])]
                        b.mk("v", lambda e, tt=tt, nch=nch, grp=grp:
                             e.tensor_mul(sscr[tt % 2],
                                          pd[grp][tt],
                                          sdb[:, nch * 512:(nch + 1) * 512]),
                             waits=w0)
                        w1 = []
                        if nch == 0 and tt < 2:
                            w1 = [(f"dR{par}", 16 * nR[par])]
                        ev = b.mk("v", lambda e, tt=tt, nch=nch:
                                  e.tensor_add(x2r[tt][:, nch * 512:(nch + 1) * 512],
                                               x2r[tt][:, nch * 512:(nch + 1) * 512],
                                               sscr[tt % 2]),
                                  waits=w1)
                    sweep_evac[nch] = ev
                down_done_pe = M["pe"]
                prev_evac[0] = sweep_evac[2]
                prev_evac[1] = sweep_evac[3]
                for tt in range(4):
                    tq = f * 4 + tt
                    par = tq % 2
                    bs = 0
                    for i4 in range(4):
                        bs = b.mk("v", lambda e, i4=i4, tt=tt: e.bn_stats(
                            bnst[:, 6 * i4:6 * i4 + 6],
                            x2r[tt][:, 512 * i4:512 * (i4 + 1)]))
                    ag = b.mk("v", lambda e: e.bn_aggr(mv, bnst),
                              waits=(("dve", bs),))
                    ve = b.mk("v", lambda e: e.tensor_scalar_add(
                        veps, mv[:, 1:2], EPS), waits=(("dve", ag),))
                    sq = b.mk("s", lambda e: e.activation(lnv, veps, AF.Sqrt),
                              waits=(("dve", ve),))
                    rc = b.mk("v", lambda e: e.reciprocal(rstd, lnv),
                              waits=(("act", sq),))
                    bmv = b.mk("v", lambda e: e.tensor_scalar(
                        bm, mv[:, 0:1], rstd, -1.0, ALU.mult, ALU.mult),
                        waits=(("dve", rc),))
                    norm_mark[par] = b.mk("v", lambda e, tt=tt: e.tensor_scalar(
                        x2r[tt], x2r[tt], rstd, bm, ALU.mult, ALU.add),
                        waits=(("dve", bmv),))
                    out_n[par] += 1
                    b.dma("y", f"dO{'ab'[par]}",
                          out[tq * 128:(tq + 1) * 128, :], x2r[tt],
                          waits=(("dve", norm_mark[par]),))
            gu_pre[0] = 2 if f < 3 else 0


_NC_CACHE = {}


def _get_nc():
    if "nc" not in _NC_CACHE:
        _NC_CACHE["nc"] = build_nc()
    return _NC_CACHE["nc"]


def kernel(L0_attn_hidden_states, L0_attn_Wq, L0_attn_Wk, L0_attn_Wv,
           L0_attn_Wo, L0_ffn_w_gate, L0_ffn_w_up, L0_ffn_w_down,
           L0_ffn_s_gate, L0_ffn_s_up, L0_ffn_s_down):
    x = np.ascontiguousarray(np.asarray(L0_attn_hidden_states, dtype=np.float32))
    wq = np.ascontiguousarray(
        np.asarray(L0_attn_Wq, dtype=np.float32)
        .reshape(16, 128, 16, 128).transpose(2, 1, 0, 3).reshape(2048, 2048))
    wk = np.ascontiguousarray(
        np.asarray(L0_attn_Wk, dtype=np.float32)
        .reshape(16, 128, 4, 128).transpose(2, 1, 0, 3).reshape(512, 2048))
    wv = np.ascontiguousarray(np.asarray(L0_attn_Wv, dtype=np.float32))
    wo = np.ascontiguousarray(np.asarray(L0_attn_Wo, dtype=np.float32))
    wg = np.ascontiguousarray(
        np.asarray(L0_ffn_w_gate, dtype=np.int8)
        .reshape(16, 128, 32, 256).transpose(2, 1, 0, 3).reshape(4096, 4096))
    wu = np.ascontiguousarray(
        np.asarray(L0_ffn_w_up, dtype=np.int8)
        .reshape(16, 128, 32, 256).transpose(2, 1, 0, 3).reshape(4096, 4096))
    wd = np.ascontiguousarray(
        np.asarray(L0_ffn_w_down, dtype=np.int8)
        .reshape(16, 2, 2, 128, 4, 512).transpose(4, 0, 3, 1, 2, 5)
        .reshape(8192, 2048))
    sg = np.ascontiguousarray(np.asarray(L0_ffn_s_gate, dtype=np.float32).reshape(64, 128))
    su = np.ascontiguousarray(np.asarray(L0_ffn_s_up, dtype=np.float32).reshape(64, 128))
    sd = np.ascontiguousarray(np.asarray(L0_ffn_s_down, dtype=np.float32).reshape(1, DM))

    B, S = 2, 8192
    in_maps = []
    for c in range(8):
        bb, s0 = c // 4, (c % 4) * T
        if s0 == 0:
            halo = np.zeros((HALO, DM), dtype=np.float32)
            kv = np.full((1, HALO), NEG, dtype=ml_dtypes.bfloat16)
        else:
            halo = x[bb, s0 - HALO:s0]
            kv = np.zeros((1, HALO), dtype=ml_dtypes.bfloat16)
        x_pad = np.ascontiguousarray(np.concatenate([halo, x[bb, s0:s0 + T]], axis=0))
        in_maps.append({
            "x_pad": x_pad, "kval": kv,
            "wq": wq, "wk": wk, "wv": wv, "wo": wo,
            "wg": wg, "wu": wu, "wd": wd,
            "sg_in": sg, "su_in": su, "sd_in": sd,
        })

    nc = _get_nc()
    res = run_bass_kernel_spmd(nc, in_maps, core_ids=list(range(8)))

    outp = np.empty((B, S, DM), dtype=np.float32)
    for c in range(8):
        bb, s0 = c // 4, (c % 4) * T
        outp[bb, s0:s0 + T] = res.results[c]["out"]
    return outp


# version 31
# speedup vs baseline: 1.0246x; 1.0140x over previous
import numpy as np
import ml_dtypes
from contextlib import ExitStack

import concourse.bass as bass
import concourse.mybir as mybir
from concourse.bass_utils import run_bass_kernel_spmd

F32 = mybir.dt.float32
BF16 = mybir.dt.bfloat16
I8 = mybir.dt.int8
AF = mybir.ActivationFunctionType
ALU = mybir.AluOpType
AX = mybir.AxisListType

DM = 2048
NH = 16
KVH = 4
INTER = 8192
T = 2048
HALO = 1024
TLOC = T + HALO
SCALE = 0.08838834764831845
EPS = 1e-5
NEG = -1.0e30

SEM_NAMES = (
    "pe", "dve", "act", "pool",
    "dK",
    "dXH0", "dXH1", "dXH2",
    "dXT",
    "dQ0", "dQ1",
    "dKW0", "dKW1", "dKW2", "dKW3",
    "dG0", "dG1",
    "dU0", "dU1",
    "dD0", "dD1", "dD2", "dD3",
    "dR0", "dR1",
    "dW2",
    "dX2a", "dX2b",
    "dOa", "dOb",
)


class Bld:
    def __init__(self, nc, ctx):
        self.nc = nc
        self.prog = {k: [] for k in ("t", "v", "s", "g", "y")}
        self.M = {}
        self.sems = {}
        for n in SEM_NAMES:
            self.sems[n] = ctx.enter_context(nc.semaphore(n))
            self.M[n] = 0

    def add(self, eng, build, waits=(), incs=()):
        waits = tuple((s, v) for (s, v) in waits if v > 0)
        for s, v in waits:
            assert v <= self.M[s], f"wait {s}>={v} but only {self.M[s]} issued"
        incs = tuple(incs)
        sems = self.sems

        def thunk(e, _b=build, _w=waits, _i=incs):
            for s, v in _w:
                e.wait_ge(sems[s], v)
            ins = _b(e)
            for s, a in _i:
                ins.then_inc(sems[s], a)
        self.prog[eng].append(thunk)
        for s, a in incs:
            self.M[s] += a

    def mk(self, eng, build, waits=()):
        c = {"t": "pe", "v": "dve", "s": "act", "g": "pool"}[eng]
        self.add(eng, build, waits, incs=((c, 1),))
        return self.M[c]

    def dma(self, eng, sem, out_ap, in_ap, waits=(), transpose=False):
        if transpose:
            self.add(eng, lambda e, o=out_ap, i=in_ap:
                     e.dma_start(o, i, transpose=True), waits,
                     incs=((sem, 16),))
        else:
            self.add(eng, lambda e, o=out_ap, i=in_ap: e.dma_start(o, i),
                     waits, incs=((sem, 16),))
        return self.M[sem]

    def floors(self, *engines):
        m = {"t": "pe", "v": "dve", "s": "act", "g": "pool"}
        return [(m[e], self.M[m[e]]) for e in engines]


def build_nc():
    nc = bass.Bass("TRN2", target_bir_lowering=False, debug=False)

    env = {}
    env["x_pad"] = nc.dram_tensor("x_pad", [TLOC, DM], F32, kind="ExternalInput").ap()
    env["kval"] = nc.dram_tensor("kval", [1, HALO], BF16, kind="ExternalInput").ap()
    env["wq"] = nc.dram_tensor("wq", [DM, DM], F32, kind="ExternalInput").ap()
    env["wk"] = nc.dram_tensor("wk", [512, DM], F32, kind="ExternalInput").ap()
    env["wv"] = nc.dram_tensor("wv", [DM, 512], F32, kind="ExternalInput").ap()
    env["wo"] = nc.dram_tensor("wo", [DM, DM], F32, kind="ExternalInput").ap()
    env["wg"] = nc.dram_tensor("wg", [4096, 4096], I8, kind="ExternalInput").ap()
    env["wu"] = nc.dram_tensor("wu", [4096, 4096], I8, kind="ExternalInput").ap()
    env["wd"] = nc.dram_tensor("wd", [8192, 2048], I8, kind="ExternalInput").ap()
    env["sg_in"] = nc.dram_tensor("sg_in", [64, 128], F32, kind="ExternalInput").ap()
    env["su_in"] = nc.dram_tensor("su_in", [64, 128], F32, kind="ExternalInput").ap()
    env["sd_in"] = nc.dram_tensor("sd_in", [1, DM], F32, kind="ExternalInput").ap()

    env["xbf_hbm"] = nc.dram_tensor("xbf_hbm", [TLOC, DM], BF16, kind="Internal").ap()
    env["x2_hbm"] = nc.dram_tensor("x2_hbm", [T, DM], F32, kind="Internal").ap()
    env["x2bf_hbm"] = nc.dram_tensor("x2bf_hbm", [T, DM], BF16, kind="Internal").ap()
    env["out"] = nc.dram_tensor("out", [T, DM], F32, kind="ExternalOutput").ap()

    ctx = ExitStack()
    with ctx:
        b = Bld(nc, ctx)
        M = b.M

        env["ident_bf"] = ctx.enter_context(nc.sbuf_tensor("ident_bf", [128, 128], BF16)).ap()
        env["identf"] = ctx.enter_context(nc.sbuf_tensor("identf", [128, 128], F32)).ap()
        env["mask2"] = ctx.enter_context(nc.sbuf_tensor("mask2", [128, 256], BF16)).ap()
        env["ones_bf"] = ctx.enter_context(nc.sbuf_tensor("ones_bf", [1, 128], BF16)).ap()
        env["ones_f"] = ctx.enter_context(nc.sbuf_tensor("ones_f", [1, 128], F32)).ap()
        env["kval_sb"] = ctx.enter_context(nc.sbuf_tensor("kval_sb", [1, HALO], BF16)).ap()
        env["sg_sb"] = ctx.enter_context(nc.sbuf_tensor("sg_sb", [128, 64], F32)).ap()
        env["su_sb"] = ctx.enter_context(nc.sbuf_tensor("su_sb", [128, 64], F32)).ap()

        ident_bf, identf, mask2 = env["ident_bf"], env["identf"], env["mask2"]
        ones_bf, ones_f = env["ones_bf"], env["ones_f"]

        b.add("g", lambda e: e.memset(ident_bf, 0.0))
        b.add("g", lambda e: e.affine_select(out=ident_bf, in_=ident_bf,
              compare_op=ALU.not_equal, fill=1.0, base=0,
              pattern=[[-1, 128]], channel_multiplier=1))
        b.add("g", lambda e: e.memset(identf, 0.0))
        b.add("g", lambda e: e.affine_select(out=identf, in_=identf,
              compare_op=ALU.not_equal, fill=1.0, base=0,
              pattern=[[-1, 128]], channel_multiplier=1))
        b.add("g", lambda e: e.memset(mask2, 0.0))
        b.add("g", lambda e: e.affine_select(out=mask2[:, 0:128], in_=mask2[:, 0:128],
              compare_op=ALU.is_ge, fill=NEG, base=0,
              pattern=[[-1, 128]], channel_multiplier=1))
        b.add("g", lambda e: e.affine_select(out=mask2[:, 128:256], in_=mask2[:, 128:256],
              compare_op=ALU.is_ge, fill=NEG, base=0,
              pattern=[[1, 128]], channel_multiplier=-1))
        b.add("g", lambda e: e.memset(ones_bf, 1.0))
        env["consts_done"] = b.mk("g", lambda e: e.memset(ones_f, 1.0))

        def emit_cast(c):
            b.dma("g", f"dXH{c}", env["xbf_hbm"][c * 1024:(c + 1) * 1024, :],
                  env["x_pad"][c * 1024:(c + 1) * 1024, :])
        env["emit_cast"] = emit_cast

        env["k_loads"] = b.dma("y", "dK", env["kval_sb"], env["kval"])

        st = {
            "pq_free": [0, 0], "pv_free": [0, 0],
            "wq_free": [0, 0], "wk_free": [0, 0, 0, 0],
            "nQ": {0: 0, 1: 0}, "nKW": {0: 0, 1: 0, 2: 0, 3: 0},
            "qkv_ready": 0,
            "xres_n": 0, "xres_used": 0,
            "x2_spill": 0, "nX2": 0,
            "wo_mark": 0, "av_mark": 0,
            "xTp_used": 0,
            "scales_done": 0,
        }

        with ExitStack() as sup:
            env["KT"] = sup.enter_context(nc.sbuf_tensor("KT", [128, KVH, TLOC], BF16)).ap()
            env["Vn"] = sup.enter_context(nc.sbuf_tensor("Vn", [128, 24, 512], BF16)).ap()
            for half in (0, 1):
                emit_half(nc, b, st, half, env)

        emit_ffn(nc, b, st, env)

        b.add("y", lambda e: e.nop(),
              waits=(("dOa", M["dOa"]), ("dOb", M["dOb"])))

        with nc.Block() as block:
            @block.tensor
            def _(t):
                for f in b.prog["t"]:
                    f(t)

            @block.vector
            def _(v):
                for f in b.prog["v"]:
                    f(v)

            @block.scalar
            def _(s):
                for f in b.prog["s"]:
                    f(s)

            @block.gpsimd
            def _(g):
                for f in b.prog["g"]:
                    f(g)

            @block.sync
            def _(y):
                for f in b.prog["y"]:
                    f(y)

    return nc


def emit_half(nc, b, st, half, env):
    M = b.M
    KT, Vn = env["KT"], env["Vn"]
    ident_bf, identf = env["ident_bf"], env["identf"]
    mask2, ones_bf, ones_f = env["mask2"], env["ones_bf"], env["ones_f"]
    kval_sb = env["kval_sb"]
    sg_sb, su_sb = env["sg_sb"], env["su_sb"]
    x_pad, xbf_hbm = env["x_pad"], env["xbf_hbm"]
    wq, wk, wv, wo = env["wq"], env["wk"], env["wv"], env["wo"]
    x2_hbm, x2bf_hbm = env["x2_hbm"], env["x2bf_hbm"]

    chunks = [1, 0] if half == 0 else [2]
    fl_pe = b.floors("v", "s") if half == 1 else []
    fl_dma = b.floors("t", "v", "s") if half == 1 else []
    first_pool = [True]
    first_sync = [True]
    first_pe = [True]

    def poolw(extra=()):
        w = list(extra)
        if first_pool[0]:
            w += fl_dma
            first_pool[0] = False
        return w

    def syncw(extra=()):
        w = list(extra)
        if first_sync[0]:
            w += fl_dma
            first_sync[0] = False
        return w

    def pew(extra=()):
        w = list(extra)
        if first_pe[0]:
            w += fl_pe
            first_pe[0] = False
        return w

    with ExitStack() as hs:
        QT = hs.enter_context(nc.sbuf_tensor(f"QT{half}", [128, NH, 1024], BF16)).ap()

        with ExitStack() as qs:
            if half == 0:
                env["emit_cast"](1)
                xT_for = {c: qs.enter_context(
                    nc.sbuf_tensor(f"xT{c}", [128, 16, 1024], BF16)).ap()
                    for c in (0, 1)}
            else:
                xT_for = {2: qs.enter_context(
                    nc.sbuf_tensor("xT2", [128, 16, 1024], BF16)).ap()}
            WvSB = qs.enter_context(nc.sbuf_tensor(f"WvSB{half}", [128, 16, 512], BF16)).ap()
            wq_buf = [qs.enter_context(nc.sbuf_tensor(f"wqb{half}_{i}", [128, 2048], BF16)).ap()
                      for i in range(2)]
            wk_buf = [qs.enter_context(nc.sbuf_tensor(f"wkb{half}_{i}", [128, 2048], BF16)).ap()
                      for i in range(4)]
            pq = [qs.enter_context(nc.psum_tensor(f"pq{half}_{i}", [128, 1024], F32)).ap()
                  for i in range(2)]
            pv = [qs.enter_context(nc.psum_tensor(f"pv{half}_{i}", [128, 512], F32)).ap()
                  for i in range(2)]

            if half == 0:
                sgt_tmp = qs.enter_context(nc.sbuf_tensor("sgt_tmp", [64, 256], F32)).ap()
                b.dma("y", "dK", sgt_tmp[:, 0:128], env["sg_in"])
                k_loads2 = b.dma("y", "dK", sgt_tmp[:, 128:256], env["su_in"])
                m = b.mk("t", lambda e: e.transpose(
                    pq[0][:, 0:64], sgt_tmp[:, 0:128], identf[0:64, 0:64]),
                    waits=pew([("dK", k_loads2), ("pool", env["consts_done"])]))
                m = b.mk("t", lambda e: e.transpose(
                    pq[0][:, 64:128], sgt_tmp[:, 128:256], identf[0:64, 0:64]))
                v1 = b.mk("v", lambda e: e.tensor_copy(sg_sb, pq[0][:, 0:64]),
                          waits=(("pe", m),))
                v1 = b.mk("v", lambda e: e.tensor_copy(su_sb, pq[0][:, 64:128]))
                st["pq_free"][0] = v1
                st["pq_free"][1] = v1

            wv_loads = 0
            for c in chunks:
                xTa = xT_for[c]
                first_chunk = (c == chunks[0])
                w = [(f"dXH{c}", 16)]
                if half == 1 and c == 2:
                    w += fl_dma + [("dW2", 16 * st["x2_spill"]),
                                   ("dX2a", 16 * st["nX2"]),
                                   ("dR0", 16 * st["xres_n"])]
                xt_base = M["dXT"]
                for dm in range(16):
                    b.dma("s", "dXT", xTa[:, dm, :],
                          xbf_hbm[c * 1024:(c + 1) * 1024, dm * 128:(dm + 1) * 128],
                          waits=w if dm == 0 else (), transpose=True)
                xt_ready = ("dXT", M["dXT"])

                for kh in range(KVH):
                    b.dma("g", f"dKW{kh}", wk_buf[kh],
                          wk[kh * 128:(kh + 1) * 128, :],
                          waits=poolw([("pe", st["wk_free"][kh])]))
                    st["nKW"][kh] += 1
                if first_chunk:
                    for dm in range(16):
                        wv_loads = b.dma("g", "dK", WvSB[:, dm, :],
                                         wv[dm * 128:(dm + 1) * 128, :],
                                         waits=poolw([xt_ready]) if dm == 0
                                         else poolw())
                for kh in range(KVH):
                    par = kh % 2
                    pm = None
                    for dm in range(16):
                        for nb in range(2):
                            w0 = []
                            if dm == 0 and nb == 0:
                                w0 = pew([(f"dKW{kh}", 16 * st["nKW"][kh]),
                                          ("dve", st["pq_free"][par])])
                                if not (first_chunk and kh == 0):
                                    w0.append(xt_ready)
                            if first_chunk and kh == 0 and nb == 0:
                                w0 = list(w0) + [("dXT", xt_base + 16 * (dm + 1))]
                            pm = b.mk("t", lambda e, dm=dm, nb=nb, kh=kh, par=par, xTa=xTa:
                                      e.matmul(pq[par][:, nb * 512:(nb + 1) * 512],
                                               wk_buf[kh][:, dm * 128:(dm + 1) * 128],
                                               xTa[:, dm, nb * 512:(nb + 1) * 512],
                                               start=(dm == 0), stop=(dm == 15)),
                                      waits=w0)
                    st["wk_free"][kh] = pm
                    ev = b.mk("v", lambda e, kh=kh, c=c, par=par:
                              e.tensor_copy(KT[:, kh, c * 1024:(c + 1) * 1024], pq[par]),
                              waits=(("pe", pm),))
                    st["pq_free"][par] = ev
                if half == 0 and c == 1:
                    for qh in (0, 1):
                        par = qh % 2
                        b.dma("g", f"dQ{par}", wq_buf[par],
                              wq[qh * 128:(qh + 1) * 128, :],
                              waits=poolw([("pe", st["wq_free"][par])]))
                        st["nQ"][par] += 1
                    st["q_pre"] = 2
                    env["emit_cast"](0)

                for tt in range(8):
                    par = tt % 2
                    pm = None
                    for dm in range(16):
                        w0 = ()
                        if dm == 0:
                            w0 = pew([("dK", wv_loads), xt_ready,
                                      ("dve", st["pv_free"][par])])
                        pm = b.mk("t", lambda e, tt=tt, dm=dm, par=par, xTa=xTa:
                                  e.matmul(pv[par],
                                           xTa[:, dm, tt * 128:(tt + 1) * 128],
                                           WvSB[:, dm, :],
                                           start=(dm == 0), stop=(dm == 15)),
                                  waits=w0)
                    ev = b.mk("v", lambda e, tt=tt, c=c, par=par:
                              e.tensor_copy(Vn[:, c * 8 + tt, :], pv[par]),
                              waits=(("pe", pm),))
                    st["pv_free"][par] = ev

                if c == half + 1:
                    for qh in range(NH):
                        par = qh % 2
                        if qh >= st.get("q_pre", 0):
                            b.dma("g", f"dQ{par}", wq_buf[par],
                                  wq[qh * 128:(qh + 1) * 128, :],
                                  waits=poolw([("pe", st["wq_free"][par])]))
                            st["nQ"][par] += 1
                        pm = None
                        for dm in range(16):
                            for nb in range(2):
                                w0 = ()
                                if dm == 0 and nb == 0:
                                    w0 = pew([(f"dQ{par}", 16 * st["nQ"][par]),
                                              xt_ready, ("dve", st["pq_free"][par])])
                                pm = b.mk("t", lambda e, qh=qh, dm=dm, nb=nb, par=par, xTa=xTa:
                                          e.matmul(pq[par][:, nb * 512:(nb + 1) * 512],
                                                   wq_buf[par][:, dm * 128:(dm + 1) * 128],
                                                   xTa[:, dm, nb * 512:(nb + 1) * 512],
                                                   start=(dm == 0), stop=(dm == 15)),
                                          waits=w0)
                        st["wq_free"][par] = pm
                        ev = b.mk("v", lambda e, qh=qh, par=par:
                                  e.tensor_scalar_mul(QT[:, qh, :], pq[par], SCALE),
                                  waits=(("pe", pm),))
                        st["pq_free"][par] = ev
                    st["q_pre"] = 0

            st["qkv_ready"] = M["dve"]

        with ExitStack() as ats:
            WoSB = ats.enter_context(nc.sbuf_tensor(f"WoSB{half}", [128, 16, DM], BF16)).ap()
            OT = ats.enter_context(nc.sbuf_tensor(f"OT{half}", [128, NH * 128], BF16)).ap()
            P_sb = [ats.enter_context(nc.sbuf_tensor(f"P{half}_{i}", [128, 1152], BF16)).ap()
                    for i in range(2)]
            Pn_sb = [ats.enter_context(nc.sbuf_tensor(f"Pn{half}_{i}", [128, 1152], BF16)).ap()
                     for i in range(2)]
            PT_sb = ats.enter_context(nc.sbuf_tensor(f"PT{half}", [128, 4608], BF16)).ap()
            h_sb = ats.enter_context(nc.sbuf_tensor(f"h{half}", [128, DM], F32)).ap()
            x2f = ats.enter_context(nc.sbuf_tensor(f"x2f{half}", [128, DM], F32)).ap()
            x2b = ats.enter_context(nc.sbuf_tensor(f"x2b{half}", [128, DM], BF16)).ap()
            xres = ats.enter_context(nc.sbuf_tensor(f"xres{half}", [128, DM], F32)).ap()
            tiny = ats.enter_context(nc.sbuf_tensor(f"tiny{half}", [128, 40], F32)).ap()
            psx = [ats.enter_context(nc.psum_tensor(f"psx{half}_{i}", [128, 1536], F32)).ap()
                   for i in range(2)]
            ppt = ats.enter_context(nc.psum_tensor(f"ppt{half}", [128, 1024], BF16)).ap()
            pav = ats.enter_context(nc.psum_tensor(f"pav{half}", [128, 512], F32)).ap()
            ppt_f32 = ppt.bitcast(F32)
            pav_bf = pav.bitcast(BF16)

            mx = [tiny[:, 0 + i:1 + i] for i in range(2)]
            mxb = [tiny[:, 2 + i:3 + i] for i in range(2)]
            ssum = [tiny[:, 4 + i:5 + i] for i in range(2)]
            rcp = [tiny[:, 6 + i:7 + i] for i in range(2)]
            bnst = tiny[:, 8:32]
            mv = tiny[:, 32:34]
            veps = tiny[:, 34:35]
            lnv = tiny[:, 35:36]
            rstd = tiny[:, 36:37]
            bm = tiny[:, 37:38]

            proj_pe = M["pe"]
            wo_loads = 0
            for h in range(16):
                wo_loads = b.dma("g", "dK", WoSB[:, h, :],
                                 wo[h * 128:(h + 1) * 128, :],
                                 waits=poolw([("pe", proj_pe)]) if h == 0
                                 else poolw())
            if half == 0:
                env["emit_cast"](2)
            wo_wait = {"v": wo_loads}


            exp_done = [0, 0]
            pn_used = [0, 0]
            pnorm = [0, 0]
            av_evac = 0
            wo_evac = [0, 0]
            ln_done = 0
            x2b_done = 0

            PT4 = PT_sb.rearrange("p (k h t) -> p k h t", h=4, t=128)
            ppt8 = ppt.rearrange("p (k t) -> p k t", t=128)

            def mk_ctx(qi):
                return {"qi": qi, "q": half * 8 + qi,
                        "sc": {}, "norm": {}, "trans": {},
                        "ev_main": {-1: ("act", 0)}, "ev_t8": {-1: 0}}

            def emit_scores(cx, hh):
                q, qi = cx["q"], cx["qi"]
                p = hh % 2
                ps = psx[p]
                w0 = pew([("act", exp_done[p]), ("dve", st["qkv_ready"])])
                widths = (512, 512, 128)
                pm = None
                for nb in range(3):
                    w = widths[nb]
                    k0 = q * 128 + nb * 512
                    r1w = max(0, min(HALO - k0, w)) if q <= 7 else 0
                    has_mask = nb != 1
                    pm = b.mk("t", lambda e, hh=hh, nb=nb, w=w, k0=k0, qi=qi,
                              ps=ps, alone=(r1w == 0 and not has_mask):
                              e.matmul(ps[:, nb * 512:nb * 512 + w],
                                       QT[:, hh, qi * 128:(qi + 1) * 128],
                                       KT[:, hh // 4, k0:k0 + w],
                                       start=True, stop=alone),
                              waits=w0)
                    w0 = ()
                    if r1w:
                        pm = b.mk("t", lambda e, nb=nb, rw=r1w, k0=k0, ps=ps,
                                  alone=(not has_mask):
                                  e.matmul(ps[:, nb * 512:nb * 512 + rw],
                                           ones_bf, kval_sb[:, k0:k0 + rw],
                                           start=False, stop=alone))
                    if has_mask:
                        moff = 0 if nb == 0 else 128
                        poff = 0 if nb == 0 else 1024
                        pm = b.mk("t", lambda e, ps=ps, moff=moff, poff=poff:
                                  e.matmul(ps[:, poff:poff + 128],
                                           mask2[:, moff:moff + 128],
                                           ident_bf,
                                           start=False, stop=True))
                    if nb == 0:
                        cx["sc"][(hh, "a")] = pm
                cx["sc"][hh] = pm

            def emit_m(cx, hh):
                p = hh % 2
                ps = psx[p]
                b.mk("v", lambda e, ps=ps, p=p: e.reduce_max(
                    mxb[p], ps[:, 0:512], axis=AX.X, negate=True),
                    waits=(("pe", cx["sc"][(hh, "a")]),))
                b.mk("v", lambda e, ps=ps, p=p: e.reduce_max(
                    mx[p], ps[:, 512:1152], axis=AX.X, negate=True),
                    waits=(("pe", cx["sc"][hh]),))
                cx["sc"][(hh, "m")] = b.mk("v", lambda e, p=p: e.tensor_tensor(
                    mx[p], mx[p], mxb[p], ALU.min))

            def emit_exp(cx, hh):
                p = hh % 2
                exp_done[p] = b.mk("s", lambda e, p=p: e.activation(
                    P_sb[p], psx[p][:, 0:1152], AF.Exp,
                    bias=mx[p], scale=1.0, accum_out=ssum[p]),
                    waits=(("dve", cx["sc"][(hh, "m")]), ("dve", pnorm[p])))

            def emit_rn(cx, hh):
                p = hh % 2
                rcpm = b.mk("v", lambda e, p=p: e.reciprocal(rcp[p], ssum[p]),
                            waits=(("act", exp_done[p]),))
                pnorm[p] = b.mk("v", lambda e, p=p: e.tensor_scalar_mul(
                    Pn_sb[p], P_sb[p], rcp[p]),
                    waits=(("pe", pn_used[p]), ("dve", rcpm)))
                cx["norm"][hh] = pnorm[p]

            def emit_trans_main(cx, hh):
                p = hh % 2
                w0 = [("dve", cx["norm"][hh]), cx["ev_main"][hh - 1],
                      ("dve", cx["ev_t8"][hh - 1])]
                if hh == 0:
                    w0 += [("dve", wo_evac[0]), ("dve", wo_evac[1])]
                tm = None
                for k in range(8):
                    tm = b.mk("t", lambda e, k=k, p=p: e.transpose(
                        ppt[:, k * 128:(k + 1) * 128],
                        Pn_sb[p][:, k * 128:(k + 1) * 128], ident_bf),
                        waits=w0 if k == 0 else ())
                    w0 = ()
                cx["trans"][hh] = tm

            def emit_tm8(cx, hh):
                p = hh % 2
                j = hh % 4
                w8 = [("dve", av_evac)] if j == 0 else []
                tm8 = b.mk("t", lambda e, p=p: e.transpose(
                    pav_bf[:, 0:128], Pn_sb[p][:, 1024:1152], ident_bf),
                    waits=w8)
                pn_used[p] = tm8
                cx["trans"][(hh, 8)] = tm8

            def emit_ev(cx, hh):
                tm = cx["trans"][hh]
                tm8 = cx["trans"][(hh, 8)]
                j = hh % 4
                w0 = [("pe", tm)]
                if j == 0:
                    w0.append(("pe", st["av_mark"]))
                if hh % 2 == 0:
                    cx["ev_main"][hh] = ("act", b.mk("s", lambda e, j=j: e.copy(
                        PT4[:, 0:8, j, :], ppt8), waits=w0))
                else:
                    cx["ev_main"][hh] = ("dve", b.mk("v", lambda e, j=j: e.tensor_copy(
                        PT4[:, 0:8, j, :], ppt8), waits=w0))
                cx["ev_t8"][hh] = b.mk("v", lambda e, j=j: e.tensor_copy(
                    PT4[:, 8, j, :], pav_bf[:, 0:128]),
                    waits=(("pe", tm8),))

            def emit_av(cx, kvg):
                nonlocal av_evac
                q = cx["q"]
                pm = None
                for k in range(9):
                    w0 = ()
                    if k == 0:
                        w0 = (cx["ev_main"][kvg * 4 + 3],
                              ("dve", cx["ev_t8"][kvg * 4 + 3]),
                              ("dve", av_evac))
                    pm = b.mk("t", lambda e, q=q, kvg=kvg, k=k:
                              e.matmul(pav,
                                       Vn[:, q + k, kvg * 128:(kvg + 1) * 128],
                                       PT_sb[:, k * 512:(k + 1) * 512],
                                       start=(k == 0), stop=(k == 8)),
                              waits=w0)
                st["av_mark"] = pm
                av_evac = b.mk("v", lambda e, kvg=kvg: e.tensor_copy(
                    OT[:, kvg * 512:(kvg + 1) * 512], pav),
                    waits=(("pe", pm), ("pe", st["wo_mark"])))

            def prelude(cx):
                for hh in (0, 1):
                    emit_scores(cx, hh)
                    emit_m(cx, hh)
                    emit_exp(cx, hh)
                emit_rn(cx, 0)

            def body(cx):
                q = cx["q"]
                b.dma("y", "dR0", xres,
                      x_pad[HALO + q * 128:HALO + (q + 1) * 128, :],
                      waits=syncw([("dve", st["xres_used"])]))
                st["xres_n"] += 1
                for hh in range(2, 16):
                    emit_scores(cx, hh)
                    emit_m(cx, hh)
                    emit_exp(cx, hh)
                    emit_rn(cx, hh - 1)
                    if hh == 2 and st.get("pending_ln1"):
                        st["pending_ln1"]()
                        st["pending_ln1"] = None
                    emit_trans_main(cx, hh - 2)
                    if hh >= 6 and (hh - 6) % 4 == 0:
                        emit_av(cx, (hh - 6) // 4)
                    emit_tm8(cx, hh - 2)
                    emit_ev(cx, hh - 2)
                emit_rn(cx, 15)
                for hh in (14, 15):
                    emit_trans_main(cx, hh)
                    emit_tm8(cx, hh)
                    emit_ev(cx, hh)
                    if hh % 4 == 3:
                        emit_av(cx, hh // 4)

            def wo_block(cx):
                q = cx["q"]
                for nch in range(4):
                    par = nch % 2
                    pwo_v = ppt_f32 if par == 0 else pav
                    pm = None
                    for h in range(16):
                        w0 = ()
                        if h == 0:
                            w0 = [("dve", av_evac), ("dve", wo_evac[par])]
                            if nch == 0:
                                w0.append(cx["ev_main"][15])
                            if wo_wait["v"]:
                                w0.append(("dK", wo_wait["v"]))
                                wo_wait["v"] = 0
                        pm = b.mk("t", lambda e, nch=nch, h=h, pwo_v=pwo_v:
                                  e.matmul(pwo_v,
                                           OT[:, h * 128:(h + 1) * 128],
                                           WoSB[:, h, nch * 512:(nch + 1) * 512],
                                           start=(h == 0), stop=(h == 15)),
                                  waits=w0)
                    if nch == 3:
                        st["wo_mark"] = pm
                    wo_evac[par] = b.mk("v", lambda e, nch=nch, pwo_v=pwo_v:
                                        e.tensor_add(h_sb[:, nch * 512:(nch + 1) * 512],
                                                     pwo_v,
                                                     xres[:, nch * 512:(nch + 1) * 512]),
                                        waits=(("pe", pm),
                                               ("dR0", 16 * st["xres_n"])))
                st["xres_used"] = M["dve"]

                def emit_ln1(q=q):
                    bs = 0
                    for i4 in range(4):
                        bs = b.mk("v", lambda e, i4=i4: e.bn_stats(
                            bnst[:, 6 * i4:6 * i4 + 6], h_sb[:, 512 * i4:512 * (i4 + 1)]))
                    ag = b.mk("v", lambda e: e.bn_aggr(mv, bnst), waits=(("dve", bs),))
                    ve = b.mk("v", lambda e: e.tensor_scalar_add(veps, mv[:, 1:2], EPS),
                              waits=(("dve", ag),))
                    lm = b.mk("s", lambda e: e.activation(lnv, veps, AF.Ln),
                              waits=(("dve", ve),))
                    re = b.mk("s", lambda e: e.activation(rstd, lnv, AF.Exp, scale=-0.5),
                              waits=(("act", lm),))
                    bmv = b.mk("v", lambda e: e.tensor_scalar(
                        bm, mv[:, 0:1], rstd, -1.0, ALU.mult, ALU.mult),
                        waits=(("act", re),))
                    ln_done = b.mk("v", lambda e: e.tensor_scalar(
                        x2f, h_sb, rstd, bm, ALU.mult, ALU.add),
                        waits=(("dW2", 16 * st["x2_spill"]), ("dve", bmv)))
                    st["x2_spill"] += 1
                    b.dma("y", "dW2", x2_hbm[q * 128:(q + 1) * 128, :], x2f,
                          waits=(("dve", ln_done),))
                    cpy = b.mk("v", lambda e: e.tensor_copy(x2b, x2f),
                               waits=(("dX2a", 16 * st["nX2"]),))
                    st["nX2"] += 1
                    b.dma("y", "dX2a",
                          x2bf_hbm[q * 128:(q + 1) * 128, :], x2b,
                          waits=(("dve", cpy),))
                st["pending_ln1"] = emit_ln1

            cur = mk_ctx(0)
            prelude(cur)
            for qi in range(8):
                nxt = mk_ctx(qi + 1) if qi < 7 else None
                body(cur)
                if nxt is not None:
                    prelude(nxt)
                wo_block(cur)
                cur = nxt
            if st.get("pending_ln1"):
                st["pending_ln1"]()
                st["pending_ln1"] = None


def emit_ffn(nc, b, st, env):
    M = b.M
    sg_sb, su_sb = env["sg_sb"], env["su_sb"]
    x2_hbm, x2bf_hbm, out = env["x2_hbm"], env["x2bf_hbm"], env["out"]
    wg, wu, wd = env["wg"], env["wu"], env["wd"]

    fl_dma = b.floors("t", "v", "s")
    fl_pe = b.floors("v", "s")
    first = {"g": True, "y": True, "t": True}

    def gatew(kind, extra=()):
        w = list(extra)
        if first[kind]:
            w += fl_dma if kind in ("g", "y") else fl_pe
            first[kind] = False
        return w

    with ExitStack() as fs:
        fuse = fs.enter_context(nc.sbuf_tensor("fuse", [128, 64, 512], BF16)).ap()
        x2T = fs.enter_context(nc.sbuf_tensor("x2T", [128, 16, 512], BF16)).ap()
        wgb = [fs.enter_context(nc.sbuf_tensor(f"wgb{i}", [128, 4096], BF16)).ap()
               for i in range(2)]
        wub = [fs.enter_context(nc.sbuf_tensor(f"wub{i}", [128, 4096], BF16)).ap()
               for i in range(2)]
        wdb = [fs.enter_context(nc.sbuf_tensor(f"wdb{i}", [128, 2048], BF16)).ap()
               for i in range(4)]
        x2r = [fs.enter_context(nc.sbuf_tensor(f"x2r{i}", [128, DM], F32)).ap()
               for i in range(4)]
        sscr = [fs.enter_context(nc.sbuf_tensor(f"sscr{i}", [128, 512], F32)).ap()
                for i in range(2)]
        uscr = [fs.enter_context(nc.sbuf_tensor(f"uscr{i}", [128, 512], F32)).ap()
                for i in range(2)]
        ftiny = fs.enter_context(nc.sbuf_tensor("ftiny", [128, 40], F32)).ap()
        sdb = fs.enter_context(nc.sbuf_tensor("sdb", [128, DM], F32)).ap()
        sd_sb = fs.enter_context(nc.sbuf_tensor("sd_sb", [1, DM], F32)).ap()

        sdl = b.dma("y", "dK", sd_sb, env["sd_in"], waits=gatew("y"))
        with nc.psum_tensor("psdb", [128, 512], F32) as psdb_h:
            psdb = psdb_h.ap()
            v2 = 0
            for n in range(4):
                m2 = b.mk("t", lambda e, n=n: e.matmul(
                    psdb, env["ones_f"], sd_sb[:, 512 * n:512 * (n + 1)],
                    start=True, stop=True),
                    waits=gatew("t", [("dK", sdl)]) if n == 0 else
                    ((("dve", v2),) if v2 else ()))
                v2 = b.mk("v", lambda e, n=n: e.tensor_copy(
                    sdb[:, 512 * n:512 * (n + 1)], psdb),
                    waits=(("pe", m2),))
        sdb_done = v2

        bnst = ftiny[:, 0:24]
        mv = ftiny[:, 32:34]
        veps = ftiny[:, 34:35]
        lnv = ftiny[:, 35:36]
        rstd = ftiny[:, 36:37]
        bm = ftiny[:, 37:38]

        nG = {0: 0, 1: 0}
        nU = {0: 0, 1: 0}
        nD = {0: 0, 1: 0, 2: 0, 3: 0}
        nR = {0: st["xres_n"], 1: 0}
        wg_free = [0, 0]
        wu_free = [0, 0]
        wd_free = [0, 0, 0, 0]
        silu_mark = [0, 0]
        upsc_mark = [0, 0]
        fuse_mark = [0, 0]
        gate_done = 0
        down_done_pe = 0
        prev_evac = [0, 0]
        out_n = {0: 0, 1: 0}
        norm_mark = [0, 0]
        x2r_used = [0, 0, 0, 0]
        wd_pre = [0]

        all_spills = (("dW2", 16 * st["x2_spill"]),
                      ("dX2a", 16 * st["nX2"]))

        gu_pre = [0]

        def emit_gu_load(icb):
            bp = icb % 2
            b.dma("g", f"dG{bp}", wgb[bp], wg[icb * 128:(icb + 1) * 128, :],
                  waits=gatew("g", [("pe", wg_free[bp])]))
            nG[bp] += 1
            b.dma("g", f"dU{bp}", wub[bp], wu[icb * 128:(icb + 1) * 128, :],
                  waits=[("pe", wu_free[bp])])
            nU[bp] += 1

        def emit_wd_load(nch, ibp):
            bp = ibp % 4
            b.dma("g", f"dD{bp}", wdb[bp],
                  wd[(nch * 16 + ibp) * 128:(nch * 16 + ibp + 1) * 128, :],
                  waits=[("pe", wd_free[bp])])
            nD[bp] += 1

        def emit_x2t(f, extra):
            w = gatew("y", extra)
            for dm in range(16):
                b.dma("y", "dXT", x2T[:, dm, :],
                      x2bf_hbm[f * 512:(f + 1) * 512, dm * 128:(dm + 1) * 128],
                      waits=w if dm == 0 else (), transpose=True)
            return ("dXT", M["dXT"])

        x2t_next = None
        for f in range(4):
            x2t_ready = emit_x2t(0, list(all_spills)) if f == 0 else x2t_next

            with ExitStack() as gs:
                pg = [gs.enter_context(nc.psum_tensor(f"pg{f}_{i}", [128, 512], F32)).ap()
                      for i in range(2)]
                pu = [gs.enter_context(nc.psum_tensor(f"pu{f}_{i}", [128, 512], F32)).ap()
                      for i in range(2)]
                for icg in range(64):
                    icb = icg // 2
                    par = icg % 2
                    boff = (icg % 2) * 128
                    bp = icb % 2
                    if icg % 2 == 0 and icb >= gu_pre[0]:
                        emit_gu_load(icb)
                    if icg in (48, 52, 56, 60):
                        emit_wd_load(0, (icg - 48) // 4)
                    pm = None
                    for dm in range(16):
                        w0 = ()
                        if dm == 0:
                            w0 = gatew("t", [(f"dG{bp}", 16 * nG[bp]), x2t_ready,
                                             ("act", silu_mark[par]),
                                             ("dve", prev_evac[0]),
                                             ("dve", prev_evac[1]),
                                             ("dve", sdb_done)]
                                       if icg == 0 else
                                       [(f"dG{bp}", 16 * nG[bp]),
                                        ("act", silu_mark[par])])
                        pm = b.mk("t", lambda e, dm=dm, par=par, bp=bp, boff=boff:
                                  e.matmul(pg[par],
                                           wgb[bp][:, dm * 256 + boff:dm * 256 + boff + 128],
                                           x2T[:, dm, :],
                                           start=(dm == 0), stop=(dm == 15)),
                                  waits=w0)
                    gm = pm
                    for dm in range(16):
                        w0 = ()
                        if dm == 0:
                            w0 = ((f"dU{bp}", 16 * nU[bp]),
                                  ("act", upsc_mark[par]))
                        pm = b.mk("t", lambda e, dm=dm, par=par, bp=bp, boff=boff:
                                  e.matmul(pu[par],
                                           wub[bp][:, dm * 256 + boff:dm * 256 + boff + 128],
                                           x2T[:, dm, :],
                                           start=(dm == 0), stop=(dm == 15)),
                                  waits=w0)
                    wg_free[bp] = pm
                    wu_free[bp] = pm
                    silu_mark[par] = b.mk("s", lambda e, par=par, icg=icg:
                                          e.activation(sscr[par], pg[par], AF.Silu,
                                                       scale=sg_sb[:, icg:icg + 1]),
                                          waits=(("pe", gm),
                                                 ("dve", fuse_mark[par])))
                    upsc_mark[par] = b.mk("s", lambda e, par=par, icg=icg:
                                          e.activation(uscr[par], pu[par], AF.Copy,
                                                       scale=su_sb[:, icg:icg + 1]),
                                          waits=(("pe", pm),))
                    w0 = [("act", upsc_mark[par])]
                    if icg <= 1:
                        w0.append(("pe", down_done_pe))
                    fuse_mark[par] = b.mk("v", lambda e, par=par, icg=icg:
                                          e.tensor_mul(fuse[:, icg, :],
                                                       sscr[par], uscr[par]),
                                          waits=w0)
                gate_done = M["pe"]
                fuse_all = max(fuse_mark)
                gu_pre[0] = 0
                wd_pre[0] = 4

            if f < 3:
                x2t_next = emit_x2t(f + 1, [("pe", gate_done)])

            with ExitStack() as ds:
                pd = [[ds.enter_context(
                    nc.psum_tensor(f"pdn{f}_{g}_{t}", [128, 512], F32)).ap()
                    for t in range(4)] for g in range(2)]
                for tt in range(4):
                    tq = f * 4 + tt
                    par = tq % 2
                    xw = [(f"dO{'ab'[par]}", 16 * out_n[par])]
                    if tq == 0:
                        xw.append(("dW2", 16 * st["x2_spill"]))
                    b.dma("y", f"dR{par}", x2r[tt],
                          x2_hbm[tq * 128:(tq + 1) * 128, :], waits=xw)
                    nR[par] += 1
                sweep_evac = {}
                for nch in range(4):
                    grp = nch % 2
                    last_mm = {}
                    for ic in range(64):
                        ibp = ic // 4
                        bp = ibp % 4
                        if ic % 4 == 0 and not (nch == 0 and ibp < wd_pre[0]):
                            emit_wd_load(nch, ibp)
                            if f < 3 and ic == 16 and nch in (2, 3):
                                emit_gu_load(0 if nch == 2 else 1)
                        roff = ((ic // 2) % 2) * 1024 + (ic % 2) * 512
                        for tt in range(4):
                            w0 = []
                            if ic == 0 and tt == 0:
                                w0 = [(f"dD{bp}", 16 * nD[bp])]
                                if nch == 0:
                                    w0.append(("dve", fuse_all))
                                if nch >= 2:
                                    w0.append(("dve", sweep_evac[nch - 2]))
                            elif ic % 4 == 0 and tt == 0:
                                w0 = [(f"dD{bp}", 16 * nD[bp])]
                            pm = b.mk("t", lambda e, ic=ic, bp=bp, tt=tt,
                                      grp=grp, roff=roff:
                                      e.matmul(pd[grp][tt],
                                               fuse[:, ic, tt * 128:(tt + 1) * 128],
                                               wdb[bp][:, roff:roff + 512],
                                               start=(ic == 0), stop=(ic == 63)),
                                      waits=w0)
                            last_mm[tt] = pm
                        if ic % 4 == 3:
                            wd_free[bp] = M["pe"]
                    ev = 0
                    for tt in range(4):
                        par = tt % 2
                        w0 = [("pe", last_mm[tt])]
                        b.mk("v", lambda e, tt=tt, nch=nch, grp=grp:
                             e.tensor_mul(sscr[tt % 2],
                                          pd[grp][tt],
                                          sdb[:, nch * 512:(nch + 1) * 512]),
                             waits=w0)
                        w1 = []
                        if nch == 0 and tt < 2:
                            w1 = [(f"dR{par}", 16 * nR[par])]
                        ev = b.mk("v", lambda e, tt=tt, nch=nch:
                                  e.tensor_add(x2r[tt][:, nch * 512:(nch + 1) * 512],
                                               x2r[tt][:, nch * 512:(nch + 1) * 512],
                                               sscr[tt % 2]),
                                  waits=w1)
                    sweep_evac[nch] = ev
                down_done_pe = M["pe"]
                prev_evac[0] = sweep_evac[2]
                prev_evac[1] = sweep_evac[3]
                for tt in range(4):
                    tq = f * 4 + tt
                    par = tq % 2
                    bs = 0
                    for i4 in range(4):
                        bs = b.mk("v", lambda e, i4=i4, tt=tt: e.bn_stats(
                            bnst[:, 6 * i4:6 * i4 + 6],
                            x2r[tt][:, 512 * i4:512 * (i4 + 1)]))
                    ag = b.mk("v", lambda e: e.bn_aggr(mv, bnst),
                              waits=(("dve", bs),))
                    ve = b.mk("v", lambda e: e.tensor_scalar_add(
                        veps, mv[:, 1:2], EPS), waits=(("dve", ag),))
                    sq = b.mk("s", lambda e: e.activation(lnv, veps, AF.Sqrt),
                              waits=(("dve", ve),))
                    rc = b.mk("v", lambda e: e.reciprocal(rstd, lnv),
                              waits=(("act", sq),))
                    bmv = b.mk("v", lambda e: e.tensor_scalar(
                        bm, mv[:, 0:1], rstd, -1.0, ALU.mult, ALU.mult),
                        waits=(("dve", rc),))
                    norm_mark[par] = b.mk("v", lambda e, tt=tt: e.tensor_scalar(
                        x2r[tt], x2r[tt], rstd, bm, ALU.mult, ALU.add),
                        waits=(("dve", bmv),))
                    out_n[par] += 1
                    b.dma("y", f"dO{'ab'[par]}",
                          out[tq * 128:(tq + 1) * 128, :], x2r[tt],
                          waits=(("dve", norm_mark[par]),))
            gu_pre[0] = 2 if f < 3 else 0


_NC_CACHE = {}


def _get_nc():
    if "nc" not in _NC_CACHE:
        _NC_CACHE["nc"] = build_nc()
    return _NC_CACHE["nc"]


def kernel(L0_attn_hidden_states, L0_attn_Wq, L0_attn_Wk, L0_attn_Wv,
           L0_attn_Wo, L0_ffn_w_gate, L0_ffn_w_up, L0_ffn_w_down,
           L0_ffn_s_gate, L0_ffn_s_up, L0_ffn_s_down):
    x = np.ascontiguousarray(np.asarray(L0_attn_hidden_states, dtype=np.float32))
    wq = np.ascontiguousarray(
        np.asarray(L0_attn_Wq, dtype=np.float32)
        .reshape(16, 128, 16, 128).transpose(2, 1, 0, 3).reshape(2048, 2048))
    wk = np.ascontiguousarray(
        np.asarray(L0_attn_Wk, dtype=np.float32)
        .reshape(16, 128, 4, 128).transpose(2, 1, 0, 3).reshape(512, 2048))
    wv = np.ascontiguousarray(np.asarray(L0_attn_Wv, dtype=np.float32))
    wo = np.ascontiguousarray(np.asarray(L0_attn_Wo, dtype=np.float32))
    wg = np.ascontiguousarray(
        np.asarray(L0_ffn_w_gate, dtype=np.int8)
        .reshape(16, 128, 32, 256).transpose(2, 1, 0, 3).reshape(4096, 4096))
    wu = np.ascontiguousarray(
        np.asarray(L0_ffn_w_up, dtype=np.int8)
        .reshape(16, 128, 32, 256).transpose(2, 1, 0, 3).reshape(4096, 4096))
    wd = np.ascontiguousarray(
        np.asarray(L0_ffn_w_down, dtype=np.int8)
        .reshape(16, 2, 2, 128, 4, 512).transpose(4, 0, 3, 1, 2, 5)
        .reshape(8192, 2048))
    sg = np.ascontiguousarray(np.asarray(L0_ffn_s_gate, dtype=np.float32).reshape(64, 128))
    su = np.ascontiguousarray(np.asarray(L0_ffn_s_up, dtype=np.float32).reshape(64, 128))
    sd = np.ascontiguousarray(np.asarray(L0_ffn_s_down, dtype=np.float32).reshape(1, DM))

    B, S = 2, 8192
    in_maps = []
    for c in range(8):
        bb, s0 = c // 4, (c % 4) * T
        if s0 == 0:
            halo = np.zeros((HALO, DM), dtype=np.float32)
            kv = np.full((1, HALO), NEG, dtype=ml_dtypes.bfloat16)
        else:
            halo = x[bb, s0 - HALO:s0]
            kv = np.zeros((1, HALO), dtype=ml_dtypes.bfloat16)
        x_pad = np.ascontiguousarray(np.concatenate([halo, x[bb, s0:s0 + T]], axis=0))
        in_maps.append({
            "x_pad": x_pad, "kval": kv,
            "wq": wq, "wk": wk, "wv": wv, "wo": wo,
            "wg": wg, "wu": wu, "wd": wd,
            "sg_in": sg, "su_in": su, "sd_in": sd,
        })

    nc = _get_nc()
    res = run_bass_kernel_spmd(nc, in_maps, core_ids=list(range(8)))

    outp = np.empty((B, S, DM), dtype=np.float32)
    for c in range(8):
        bb, s0 = c // 4, (c % 4) * T
        outp[bb, s0:s0 + T] = res.results[c]["out"]
    return outp


# version 35
# speedup vs baseline: 1.0358x; 1.0110x over previous
import numpy as np
import ml_dtypes
from contextlib import ExitStack

import concourse.bass as bass
import concourse.mybir as mybir
from concourse.bass_utils import run_bass_kernel_spmd

F32 = mybir.dt.float32
BF16 = mybir.dt.bfloat16
I8 = mybir.dt.int8
AF = mybir.ActivationFunctionType
ALU = mybir.AluOpType
AX = mybir.AxisListType

DM = 2048
NH = 16
KVH = 4
INTER = 8192
T = 2048
HALO = 1024
TLOC = T + HALO
SCALE = 0.08838834764831845
EPS = 1e-5
NEG = -1.0e30

SEM_NAMES = (
    "pe", "dve", "act", "pool",
    "dK",
    "dXH0", "dXH1", "dXH2",
    "dXT",
    "dQ0", "dQ1",
    "dKW0", "dKW1", "dKW2", "dKW3",
    "dG0", "dG1",
    "dU0", "dU1",
    "dD0", "dD1", "dD2", "dD3",
    "dR0", "dR1",
    "dW2",
    "dX2a", "dX2b",
    "dOa", "dOb",
)


class Bld:
    def __init__(self, nc, ctx):
        self.nc = nc
        self.prog = {k: [] for k in ("t", "v", "s", "g", "y")}
        self.M = {}
        self.sems = {}
        for n in SEM_NAMES:
            self.sems[n] = ctx.enter_context(nc.semaphore(n))
            self.M[n] = 0

    def add(self, eng, build, waits=(), incs=()):
        waits = tuple((s, v) for (s, v) in waits if v > 0)
        for s, v in waits:
            assert v <= self.M[s], f"wait {s}>={v} but only {self.M[s]} issued"
        incs = tuple(incs)
        sems = self.sems

        def thunk(e, _b=build, _w=waits, _i=incs):
            for s, v in _w:
                e.wait_ge(sems[s], v)
            ins = _b(e)
            for s, a in _i:
                ins.then_inc(sems[s], a)
        self.prog[eng].append(thunk)
        for s, a in incs:
            self.M[s] += a

    def mk(self, eng, build, waits=()):
        c = {"t": "pe", "v": "dve", "s": "act", "g": "pool"}[eng]
        self.add(eng, build, waits, incs=((c, 1),))
        return self.M[c]

    def dma(self, eng, sem, out_ap, in_ap, waits=(), transpose=False):
        if transpose:
            self.add(eng, lambda e, o=out_ap, i=in_ap:
                     e.dma_start(o, i, transpose=True), waits,
                     incs=((sem, 16),))
        else:
            self.add(eng, lambda e, o=out_ap, i=in_ap: e.dma_start(o, i),
                     waits, incs=((sem, 16),))
        return self.M[sem]

    def floors(self, *engines):
        m = {"t": "pe", "v": "dve", "s": "act", "g": "pool"}
        return [(m[e], self.M[m[e]]) for e in engines]


def build_nc():
    nc = bass.Bass("TRN2", target_bir_lowering=False, debug=False)

    env = {}
    env["x_pad"] = nc.dram_tensor("x_pad", [TLOC, DM], F32, kind="ExternalInput").ap()
    env["kval"] = nc.dram_tensor("kval", [1, HALO], BF16, kind="ExternalInput").ap()
    env["wq"] = nc.dram_tensor("wq", [DM, DM], F32, kind="ExternalInput").ap()
    env["wk"] = nc.dram_tensor("wk", [512, DM], F32, kind="ExternalInput").ap()
    env["wv"] = nc.dram_tensor("wv", [DM, 512], F32, kind="ExternalInput").ap()
    env["wo"] = nc.dram_tensor("wo", [DM, DM], F32, kind="ExternalInput").ap()
    env["wg"] = nc.dram_tensor("wg", [4096, 4096], I8, kind="ExternalInput").ap()
    env["wu"] = nc.dram_tensor("wu", [4096, 4096], I8, kind="ExternalInput").ap()
    env["wd"] = nc.dram_tensor("wd", [8192, 2048], I8, kind="ExternalInput").ap()
    env["sg_in"] = nc.dram_tensor("sg_in", [64, 128], F32, kind="ExternalInput").ap()
    env["su_in"] = nc.dram_tensor("su_in", [64, 128], F32, kind="ExternalInput").ap()
    env["sd_in"] = nc.dram_tensor("sd_in", [1, DM], F32, kind="ExternalInput").ap()

    env["xbf_hbm"] = nc.dram_tensor("xbf_hbm", [TLOC, DM], BF16, kind="Internal").ap()
    env["x2_hbm"] = nc.dram_tensor("x2_hbm", [T, DM], F32, kind="Internal").ap()
    env["x2bf_hbm"] = nc.dram_tensor("x2bf_hbm", [T, DM], BF16, kind="Internal").ap()
    env["out"] = nc.dram_tensor("out", [T, DM], F32, kind="ExternalOutput").ap()

    ctx = ExitStack()
    with ctx:
        b = Bld(nc, ctx)
        M = b.M

        env["ident_bf"] = ctx.enter_context(nc.sbuf_tensor("ident_bf", [128, 128], BF16)).ap()
        env["identf"] = ctx.enter_context(nc.sbuf_tensor("identf", [128, 128], F32)).ap()
        env["mask2"] = ctx.enter_context(nc.sbuf_tensor("mask2", [128, 256], BF16)).ap()
        env["ones_bf"] = ctx.enter_context(nc.sbuf_tensor("ones_bf", [1, 128], BF16)).ap()
        env["ones_f"] = ctx.enter_context(nc.sbuf_tensor("ones_f", [1, 128], F32)).ap()
        env["kval_sb"] = ctx.enter_context(nc.sbuf_tensor("kval_sb", [1, HALO], BF16)).ap()
        env["sg_sb"] = ctx.enter_context(nc.sbuf_tensor("sg_sb", [128, 64], F32)).ap()
        env["su_sb"] = ctx.enter_context(nc.sbuf_tensor("su_sb", [128, 64], F32)).ap()

        ident_bf, identf, mask2 = env["ident_bf"], env["identf"], env["mask2"]
        ones_bf, ones_f = env["ones_bf"], env["ones_f"]

        b.add("g", lambda e: e.memset(ident_bf, 0.0))
        b.add("g", lambda e: e.affine_select(out=ident_bf, in_=ident_bf,
              compare_op=ALU.not_equal, fill=1.0, base=0,
              pattern=[[-1, 128]], channel_multiplier=1))
        b.add("g", lambda e: e.memset(identf, 0.0))
        b.add("g", lambda e: e.affine_select(out=identf, in_=identf,
              compare_op=ALU.not_equal, fill=1.0, base=0,
              pattern=[[-1, 128]], channel_multiplier=1))
        b.add("g", lambda e: e.memset(mask2, 0.0))
        b.add("g", lambda e: e.affine_select(out=mask2[:, 0:128], in_=mask2[:, 0:128],
              compare_op=ALU.is_ge, fill=NEG, base=0,
              pattern=[[-1, 128]], channel_multiplier=1))
        b.add("g", lambda e: e.affine_select(out=mask2[:, 128:256], in_=mask2[:, 128:256],
              compare_op=ALU.is_ge, fill=NEG, base=0,
              pattern=[[1, 128]], channel_multiplier=-1))
        b.add("g", lambda e: e.memset(ones_bf, 1.0))
        env["consts_done"] = b.mk("g", lambda e: e.memset(ones_f, 1.0))

        def emit_cast(c):
            b.dma("g", f"dXH{c}", env["xbf_hbm"][c * 1024:(c + 1) * 1024, :],
                  env["x_pad"][c * 1024:(c + 1) * 1024, :])
        env["emit_cast"] = emit_cast

        env["k_loads"] = b.dma("y", "dK", env["kval_sb"], env["kval"])

        st = {
            "pq_free": [0, 0], "pv_free": [0, 0],
            "wq_free": [0, 0], "wk_free": [0, 0, 0, 0],
            "nQ": {0: 0, 1: 0}, "nKW": {0: 0, 1: 0, 2: 0, 3: 0},
            "qkv_ready": 0,
            "xres_n": 0, "xres_used": 0,
            "x2_spill": 0, "nX2": 0,
            "wo_mark": 0, "av_mark": 0,
            "xTp_used": 0,
            "scales_done": 0,
        }

        with ExitStack() as sup:
            env["KT"] = sup.enter_context(nc.sbuf_tensor("KT", [128, KVH, TLOC], BF16)).ap()
            env["Vn"] = sup.enter_context(nc.sbuf_tensor("Vn", [128, 24, 512], BF16)).ap()
            for half in (0, 1):
                emit_half(nc, b, st, half, env)

        emit_ffn(nc, b, st, env)

        b.add("y", lambda e: e.nop(),
              waits=(("dOa", M["dOa"]), ("dOb", M["dOb"])))

        with nc.Block() as block:
            @block.tensor
            def _(t):
                for f in b.prog["t"]:
                    f(t)

            @block.vector
            def _(v):
                for f in b.prog["v"]:
                    f(v)

            @block.scalar
            def _(s):
                for f in b.prog["s"]:
                    f(s)

            @block.gpsimd
            def _(g):
                for f in b.prog["g"]:
                    f(g)

            @block.sync
            def _(y):
                for f in b.prog["y"]:
                    f(y)

    return nc


def emit_half(nc, b, st, half, env):
    M = b.M
    KT, Vn = env["KT"], env["Vn"]
    ident_bf, identf = env["ident_bf"], env["identf"]
    mask2, ones_bf, ones_f = env["mask2"], env["ones_bf"], env["ones_f"]
    kval_sb = env["kval_sb"]
    sg_sb, su_sb = env["sg_sb"], env["su_sb"]
    x_pad, xbf_hbm = env["x_pad"], env["xbf_hbm"]
    wq, wk, wv, wo = env["wq"], env["wk"], env["wv"], env["wo"]
    x2_hbm, x2bf_hbm = env["x2_hbm"], env["x2bf_hbm"]

    chunks = [1, 0] if half == 0 else [2]
    fl_pe = b.floors("v", "s") if half == 1 else []
    fl_dma = b.floors("t", "v", "s") if half == 1 else []
    first_pool = [True]
    first_sync = [True]
    first_pe = [True]

    def poolw(extra=()):
        w = list(extra)
        if first_pool[0]:
            w += fl_dma
            first_pool[0] = False
        return w

    def syncw(extra=()):
        w = list(extra)
        if first_sync[0]:
            w += fl_dma
            first_sync[0] = False
        return w

    def pew(extra=()):
        w = list(extra)
        if first_pe[0]:
            w += fl_pe
            first_pe[0] = False
        return w

    with ExitStack() as hs:
        QT = hs.enter_context(nc.sbuf_tensor(f"QT{half}", [128, NH, 1024], BF16)).ap()

        with ExitStack() as qs:
            if half == 0:
                env["emit_cast"](1)
                xT_for = {c: qs.enter_context(
                    nc.sbuf_tensor(f"xT{c}", [128, 16, 1024], BF16)).ap()
                    for c in (0, 1)}
            else:
                xT_for = {2: qs.enter_context(
                    nc.sbuf_tensor("xT2", [128, 16, 1024], BF16)).ap()}
            WvSB = qs.enter_context(nc.sbuf_tensor(f"WvSB{half}", [128, 16, 512], BF16)).ap()
            wq_buf = [qs.enter_context(nc.sbuf_tensor(f"wqb{half}_{i}", [128, 2048], BF16)).ap()
                      for i in range(2)]
            wk_buf = [qs.enter_context(nc.sbuf_tensor(f"wkb{half}_{i}", [128, 2048], BF16)).ap()
                      for i in range(4)]
            pq = [qs.enter_context(nc.psum_tensor(f"pq{half}_{i}", [128, 1024], F32)).ap()
                  for i in range(2)]
            pv = [qs.enter_context(nc.psum_tensor(f"pv{half}_{i}", [128, 512], F32)).ap()
                  for i in range(2)]

            if half == 0:
                sgt_tmp = qs.enter_context(nc.sbuf_tensor("sgt_tmp", [64, 256], F32)).ap()
                b.dma("y", "dK", sgt_tmp[:, 0:128], env["sg_in"])
                k_loads2 = b.dma("y", "dK", sgt_tmp[:, 128:256], env["su_in"])
                m = b.mk("t", lambda e: e.transpose(
                    pq[0][:, 0:64], sgt_tmp[:, 0:128], identf[0:64, 0:64]),
                    waits=pew([("dK", k_loads2), ("pool", env["consts_done"])]))
                m = b.mk("t", lambda e: e.transpose(
                    pq[0][:, 64:128], sgt_tmp[:, 128:256], identf[0:64, 0:64]))
                v1 = b.mk("v", lambda e: e.tensor_copy(sg_sb, pq[0][:, 0:64]),
                          waits=(("pe", m),))
                v1 = b.mk("v", lambda e: e.tensor_copy(su_sb, pq[0][:, 64:128]))
                st["pq_free"][0] = v1
                st["pq_free"][1] = v1

            wv_loads = 0
            for c in chunks:
                xTa = xT_for[c]
                first_chunk = (c == chunks[0])
                w = [(f"dXH{c}", 16)]
                if half == 1 and c == 2:
                    w += fl_dma + [("dW2", 16 * st["x2_spill"]),
                                   ("dX2a", 16 * st["nX2"]),
                                   ("dR0", 16 * st["xres_n"])]
                xt_base = M["dXT"]
                for dm in range(16):
                    b.dma("s", "dXT", xTa[:, dm, :],
                          xbf_hbm[c * 1024:(c + 1) * 1024, dm * 128:(dm + 1) * 128],
                          waits=w if dm == 0 else (), transpose=True)
                xt_ready = ("dXT", M["dXT"])

                for kh in range(KVH):
                    b.dma("g", f"dKW{kh}", wk_buf[kh],
                          wk[kh * 128:(kh + 1) * 128, :],
                          waits=poolw([("pe", st["wk_free"][kh])]))
                    st["nKW"][kh] += 1
                if first_chunk:
                    for dm in range(16):
                        wv_loads = b.dma("g", "dK", WvSB[:, dm, :],
                                         wv[dm * 128:(dm + 1) * 128, :],
                                         waits=poolw([xt_ready]) if dm == 0
                                         else poolw())
                for kh in range(KVH):
                    par = kh % 2
                    pm = None
                    for dm in range(16):
                        for nb in range(2):
                            w0 = []
                            if dm == 0 and nb == 0:
                                w0 = pew([(f"dKW{kh}", 16 * st["nKW"][kh]),
                                          ("dve", st["pq_free"][par])])
                                if not (first_chunk and kh == 0):
                                    w0.append(xt_ready)
                            if first_chunk and kh == 0 and nb == 0:
                                w0 = list(w0) + [("dXT", xt_base + 16 * (dm + 1))]
                            pm = b.mk("t", lambda e, dm=dm, nb=nb, kh=kh, par=par, xTa=xTa:
                                      e.matmul(pq[par][:, nb * 512:(nb + 1) * 512],
                                               wk_buf[kh][:, dm * 128:(dm + 1) * 128],
                                               xTa[:, dm, nb * 512:(nb + 1) * 512],
                                               start=(dm == 0), stop=(dm == 15)),
                                      waits=w0)
                    st["wk_free"][kh] = pm
                    ev = b.mk("v", lambda e, kh=kh, c=c, par=par:
                              e.tensor_copy(KT[:, kh, c * 1024:(c + 1) * 1024], pq[par]),
                              waits=(("pe", pm),))
                    st["pq_free"][par] = ev
                if half == 0 and c == 1:
                    for qh in (0, 1):
                        par = qh % 2
                        b.dma("g", f"dQ{par}", wq_buf[par],
                              wq[qh * 128:(qh + 1) * 128, :],
                              waits=poolw([("pe", st["wq_free"][par])]))
                        st["nQ"][par] += 1
                    st["q_pre"] = 2
                    env["emit_cast"](0)

                for tt in range(8):
                    par = tt % 2
                    pm = None
                    for dm in range(16):
                        w0 = ()
                        if dm == 0:
                            w0 = pew([("dK", wv_loads), xt_ready,
                                      ("dve", st["pv_free"][par])])
                        pm = b.mk("t", lambda e, tt=tt, dm=dm, par=par, xTa=xTa:
                                  e.matmul(pv[par],
                                           xTa[:, dm, tt * 128:(tt + 1) * 128],
                                           WvSB[:, dm, :],
                                           start=(dm == 0), stop=(dm == 15)),
                                  waits=w0)
                    ev = b.mk("v", lambda e, tt=tt, c=c, par=par:
                              e.tensor_copy(Vn[:, c * 8 + tt, :], pv[par]),
                              waits=(("pe", pm),))
                    st["pv_free"][par] = ev

                if c == half + 1:
                    for qh in range(NH):
                        par = qh % 2
                        if qh >= st.get("q_pre", 0):
                            b.dma("g", f"dQ{par}", wq_buf[par],
                                  wq[qh * 128:(qh + 1) * 128, :],
                                  waits=poolw([("pe", st["wq_free"][par])]))
                            st["nQ"][par] += 1
                        pm = None
                        for dm in range(16):
                            for nb in range(2):
                                w0 = ()
                                if dm == 0 and nb == 0:
                                    w0 = pew([(f"dQ{par}", 16 * st["nQ"][par]),
                                              xt_ready, ("dve", st["pq_free"][par])])
                                pm = b.mk("t", lambda e, qh=qh, dm=dm, nb=nb, par=par, xTa=xTa:
                                          e.matmul(pq[par][:, nb * 512:(nb + 1) * 512],
                                                   wq_buf[par][:, dm * 128:(dm + 1) * 128],
                                                   xTa[:, dm, nb * 512:(nb + 1) * 512],
                                                   start=(dm == 0), stop=(dm == 15)),
                                          waits=w0)
                        st["wq_free"][par] = pm
                        ev = b.mk("v", lambda e, qh=qh, par=par:
                                  e.tensor_scalar_mul(QT[:, qh, :], pq[par], SCALE),
                                  waits=(("pe", pm),))
                        st["pq_free"][par] = ev
                    st["q_pre"] = 0

            st["qkv_ready"] = M["dve"]

        with ExitStack() as ats:
            WoSB = ats.enter_context(nc.sbuf_tensor(f"WoSB{half}", [128, 16, DM], BF16)).ap()
            OT = ats.enter_context(nc.sbuf_tensor(f"OT{half}", [128, NH * 128], BF16)).ap()
            P_sb = [ats.enter_context(nc.sbuf_tensor(f"P{half}_{i}", [128, 1152], BF16)).ap()
                    for i in range(2)]
            Pn_sb = [ats.enter_context(nc.sbuf_tensor(f"Pn{half}_{i}", [128, 1152], BF16)).ap()
                     for i in range(2)]
            PT_sb = ats.enter_context(nc.sbuf_tensor(f"PT{half}", [128, 4608], BF16)).ap()
            h_sb = ats.enter_context(nc.sbuf_tensor(f"h{half}", [128, DM], F32)).ap()
            x2f = ats.enter_context(nc.sbuf_tensor(f"x2f{half}", [128, DM], F32)).ap()
            x2b = ats.enter_context(nc.sbuf_tensor(f"x2b{half}", [128, DM], BF16)).ap()
            xres = ats.enter_context(nc.sbuf_tensor(f"xres{half}", [128, DM], F32)).ap()
            tiny = ats.enter_context(nc.sbuf_tensor(f"tiny{half}", [128, 40], F32)).ap()
            psx = [ats.enter_context(nc.psum_tensor(f"psx{half}_{i}", [128, 1536], F32)).ap()
                   for i in range(2)]
            ppt = ats.enter_context(nc.psum_tensor(f"ppt{half}", [128, 1024], BF16)).ap()
            pav = ats.enter_context(nc.psum_tensor(f"pav{half}", [128, 512], F32)).ap()
            ppt_f32 = ppt.bitcast(F32)
            pav_bf = pav.bitcast(BF16)

            mx = [tiny[:, 0 + i:1 + i] for i in range(2)]
            mxb = [tiny[:, 2 + i:3 + i] for i in range(2)]
            ssum = [tiny[:, 4 + i:5 + i] for i in range(2)]
            rcp = [tiny[:, 6 + i:7 + i] for i in range(2)]
            bnst = tiny[:, 8:32]
            mv = tiny[:, 32:34]
            veps = tiny[:, 34:35]
            lnv = tiny[:, 35:36]
            rstd = tiny[:, 36:37]
            bm = tiny[:, 37:38]

            proj_pe = M["pe"]
            wo_loads = 0
            for h in range(16):
                wo_loads = b.dma("g", "dK", WoSB[:, h, :],
                                 wo[h * 128:(h + 1) * 128, :],
                                 waits=poolw([("pe", proj_pe)]) if h == 0
                                 else poolw())
            if half == 0:
                env["emit_cast"](2)
            wo_wait = {"v": wo_loads}


            exp_done = [0, 0]
            pn_used = [0, 0]
            pnorm = [0, 0]
            av_evac = 0
            wo_evac = [0, 0]
            ln_done = 0
            x2b_done = 0

            PT4 = PT_sb.rearrange("p (k h t) -> p k h t", h=4, t=128)
            ppt8 = ppt.rearrange("p (k t) -> p k t", t=128)

            def mk_ctx(qi):
                return {"qi": qi, "q": half * 8 + qi,
                        "sc": {}, "norm": {}, "trans": {},
                        "ev_main": {-1: ("act", 0)}, "ev_t8": {-1: 0}}

            def emit_scores(cx, hh):
                q, qi = cx["q"], cx["qi"]
                p = hh % 2
                ps = psx[p]
                w0 = pew([("act", exp_done[p]), ("dve", st["qkv_ready"])])
                widths = (512, 512, 128)
                pm = None
                for nb in range(3):
                    w = widths[nb]
                    k0 = q * 128 + nb * 512
                    r1w = max(0, min(HALO - k0, w)) if q <= 7 else 0
                    has_mask = nb != 1
                    pm = b.mk("t", lambda e, hh=hh, nb=nb, w=w, k0=k0, qi=qi,
                              ps=ps, alone=(r1w == 0 and not has_mask):
                              e.matmul(ps[:, nb * 512:nb * 512 + w],
                                       QT[:, hh, qi * 128:(qi + 1) * 128],
                                       KT[:, hh // 4, k0:k0 + w],
                                       start=True, stop=alone),
                              waits=w0)
                    w0 = ()
                    if r1w:
                        pm = b.mk("t", lambda e, nb=nb, rw=r1w, k0=k0, ps=ps,
                                  alone=(not has_mask):
                                  e.matmul(ps[:, nb * 512:nb * 512 + rw],
                                           ones_bf, kval_sb[:, k0:k0 + rw],
                                           start=False, stop=alone))
                    if has_mask:
                        moff = 0 if nb == 0 else 128
                        poff = 0 if nb == 0 else 1024
                        pm = b.mk("t", lambda e, ps=ps, moff=moff, poff=poff:
                                  e.matmul(ps[:, poff:poff + 128],
                                           mask2[:, moff:moff + 128],
                                           ident_bf,
                                           start=False, stop=True))
                    if nb == 0:
                        cx["sc"][(hh, "a")] = pm
                cx["sc"][hh] = pm

            def emit_m(cx, hh):
                p = hh % 2
                ps = psx[p]
                b.mk("v", lambda e, ps=ps, p=p: e.reduce_max(
                    mxb[p], ps[:, 0:512], axis=AX.X, negate=True),
                    waits=(("pe", cx["sc"][(hh, "a")]),))
                b.mk("v", lambda e, ps=ps, p=p: e.reduce_max(
                    mx[p], ps[:, 512:1152], axis=AX.X, negate=True),
                    waits=(("pe", cx["sc"][hh]),))
                cx["sc"][(hh, "m")] = b.mk("v", lambda e, p=p: e.tensor_tensor(
                    mx[p], mx[p], mxb[p], ALU.min))

            def emit_exp(cx, hh):
                p = hh % 2
                exp_done[p] = b.mk("s", lambda e, p=p: e.activation(
                    P_sb[p], psx[p][:, 0:1152], AF.Exp,
                    bias=mx[p], scale=1.0, accum_out=ssum[p]),
                    waits=(("dve", cx["sc"][(hh, "m")]), ("dve", pnorm[p])))

            def emit_rn(cx, hh):
                p = hh % 2
                rcpm = b.mk("v", lambda e, p=p: e.reciprocal(rcp[p], ssum[p]),
                            waits=(("act", exp_done[p]),))
                pnorm[p] = b.mk("v", lambda e, p=p: e.tensor_scalar_mul(
                    Pn_sb[p], P_sb[p], rcp[p]),
                    waits=(("pe", pn_used[p]), ("dve", rcpm)))
                cx["norm"][hh] = pnorm[p]

            def emit_trans_main(cx, hh):
                p = hh % 2
                w0 = [("dve", cx["norm"][hh]), cx["ev_main"][hh - 1],
                      ("dve", cx["ev_t8"][hh - 1])]
                if hh == 0:
                    w0 += [("dve", wo_evac[0]), ("dve", wo_evac[1])]
                tm = None
                for k in range(8):
                    tm = b.mk("t", lambda e, k=k, p=p: e.transpose(
                        ppt[:, k * 128:(k + 1) * 128],
                        Pn_sb[p][:, k * 128:(k + 1) * 128], ident_bf),
                        waits=w0 if k == 0 else ())
                    w0 = ()
                cx["trans"][hh] = tm

            def emit_tm8(cx, hh):
                p = hh % 2
                j = hh % 4
                w8 = [("dve", av_evac)] if j == 0 else []
                tm8 = b.mk("t", lambda e, p=p: e.transpose(
                    pav_bf[:, 0:128], Pn_sb[p][:, 1024:1152], ident_bf),
                    waits=w8)
                pn_used[p] = tm8
                cx["trans"][(hh, 8)] = tm8

            def emit_ev(cx, hh):
                tm = cx["trans"][hh]
                tm8 = cx["trans"][(hh, 8)]
                j = hh % 4
                w0 = [("pe", tm)]
                if j == 0:
                    w0.append(("pe", st["av_mark"]))
                if hh % 2 == 0:
                    cx["ev_main"][hh] = ("act", b.mk("s", lambda e, j=j: e.copy(
                        PT4[:, 0:8, j, :], ppt8), waits=w0))
                else:
                    cx["ev_main"][hh] = ("dve", b.mk("v", lambda e, j=j: e.tensor_copy(
                        PT4[:, 0:8, j, :], ppt8), waits=w0))
                cx["ev_t8"][hh] = b.mk("v", lambda e, j=j: e.tensor_copy(
                    PT4[:, 8, j, :], pav_bf[:, 0:128]),
                    waits=(("pe", tm8),))

            def emit_av(cx, kvg):
                nonlocal av_evac
                q = cx["q"]
                pm = None
                for k in range(9):
                    w0 = ()
                    if k == 0:
                        w0 = (cx["ev_main"][kvg * 4 + 3],
                              ("dve", cx["ev_t8"][kvg * 4 + 3]),
                              ("dve", av_evac))
                    pm = b.mk("t", lambda e, q=q, kvg=kvg, k=k:
                              e.matmul(pav,
                                       Vn[:, q + k, kvg * 128:(kvg + 1) * 128],
                                       PT_sb[:, k * 512:(k + 1) * 512],
                                       start=(k == 0), stop=(k == 8)),
                              waits=w0)
                st["av_mark"] = pm
                av_evac = b.mk("v", lambda e, kvg=kvg: e.tensor_copy(
                    OT[:, kvg * 512:(kvg + 1) * 512], pav),
                    waits=(("pe", pm), ("pe", st["wo_mark"])))

            def prelude(cx):
                for hh in (0, 1):
                    emit_scores(cx, hh)
                    emit_m(cx, hh)
                    emit_exp(cx, hh)
                emit_rn(cx, 0)

            def body(cx):
                q = cx["q"]
                b.dma("y", "dR0", xres,
                      x_pad[HALO + q * 128:HALO + (q + 1) * 128, :],
                      waits=syncw([("dve", st["xres_used"])]))
                st["xres_n"] += 1
                for hh in range(2, 16):
                    emit_scores(cx, hh)
                    emit_m(cx, hh)
                    emit_exp(cx, hh)
                    emit_rn(cx, hh - 1)
                    if st.get("pending_ln1"):
                        st["pending_ln1"].pop(0)()
                        if not st["pending_ln1"]:
                            st["pending_ln1"] = None
                    emit_trans_main(cx, hh - 2)
                    if hh >= 6 and (hh - 6) % 4 == 0:
                        emit_av(cx, (hh - 6) // 4)
                    emit_tm8(cx, hh - 2)
                    emit_ev(cx, hh - 2)
                emit_rn(cx, 15)
                for hh in (14, 15):
                    emit_trans_main(cx, hh)
                    emit_tm8(cx, hh)
                    emit_ev(cx, hh)
                    if hh % 4 == 3:
                        emit_av(cx, hh // 4)

            def wo_block(cx):
                q = cx["q"]
                for nch in range(4):
                    par = nch % 2
                    pwo_v = ppt_f32 if par == 0 else pav
                    pm = None
                    for h in range(16):
                        w0 = ()
                        if h == 0:
                            w0 = [("dve", av_evac), ("dve", wo_evac[par])]
                            if nch == 0:
                                w0.append(cx["ev_main"][15])
                            if wo_wait["v"]:
                                w0.append(("dK", wo_wait["v"]))
                                wo_wait["v"] = 0
                        pm = b.mk("t", lambda e, nch=nch, h=h, pwo_v=pwo_v:
                                  e.matmul(pwo_v,
                                           OT[:, h * 128:(h + 1) * 128],
                                           WoSB[:, h, nch * 512:(nch + 1) * 512],
                                           start=(h == 0), stop=(h == 15)),
                                  waits=w0)
                    if nch == 3:
                        st["wo_mark"] = pm
                    wo_evac[par] = b.mk("v", lambda e, nch=nch, pwo_v=pwo_v:
                                        e.tensor_add(h_sb[:, nch * 512:(nch + 1) * 512],
                                                     pwo_v,
                                                     xres[:, nch * 512:(nch + 1) * 512]),
                                        waits=(("pe", pm),
                                               ("dR0", 16 * st["xres_n"])))
                st["xres_used"] = M["dve"]

                def seg_bn(i4):
                    def f():
                        b.mk("v", lambda e, i4=i4: e.bn_stats(
                            bnst[:, 6 * i4:6 * i4 + 6],
                            h_sb[:, 512 * i4:512 * (i4 + 1)]))
                    return f

                def seg_rest(q=q):
                    bn_done = M["dve"]
                    ag = b.mk("v", lambda e: e.bn_aggr(mv, bnst),
                              waits=(("dve", bn_done),))
                    ve = b.mk("v", lambda e: e.tensor_scalar_add(
                        veps, mv[:, 1:2], EPS), waits=(("dve", ag),))
                    lm = b.mk("s", lambda e: e.activation(lnv, veps, AF.Ln),
                              waits=(("dve", ve),))
                    re = b.mk("s", lambda e: e.activation(rstd, lnv, AF.Exp,
                                                          scale=-0.5),
                              waits=(("act", lm),))
                    bmv = b.mk("v", lambda e: e.tensor_scalar(
                        bm, mv[:, 0:1], rstd, -1.0, ALU.mult, ALU.mult),
                        waits=(("act", re),))
                    ln_done = b.mk("v", lambda e: e.tensor_scalar(
                        x2f, h_sb, rstd, bm, ALU.mult, ALU.add),
                        waits=(("dW2", 16 * st["x2_spill"]), ("dve", bmv)))
                    st["x2_spill"] += 1
                    b.dma("y", "dW2", x2_hbm[q * 128:(q + 1) * 128, :], x2f,
                          waits=(("dve", ln_done),))
                    cpy = b.mk("v", lambda e: e.tensor_copy(x2b, x2f),
                               waits=(("dX2a", 16 * st["nX2"]),))
                    st["nX2"] += 1
                    b.dma("y", "dX2a",
                          x2bf_hbm[q * 128:(q + 1) * 128, :], x2b,
                          waits=(("dve", cpy),))
                st["pending_ln1"] = [seg_bn(0), seg_bn(1), seg_bn(2),
                                     seg_bn(3), seg_rest]

            cur = mk_ctx(0)
            prelude(cur)
            for qi in range(8):
                nxt = mk_ctx(qi + 1) if qi < 7 else None
                body(cur)
                if nxt is not None:
                    prelude(nxt)
                wo_block(cur)
                cur = nxt
            while st.get("pending_ln1"):
                st["pending_ln1"].pop(0)()
                if not st["pending_ln1"]:
                    st["pending_ln1"] = None


def emit_ffn(nc, b, st, env):
    M = b.M
    sg_sb, su_sb = env["sg_sb"], env["su_sb"]
    x2_hbm, x2bf_hbm, out = env["x2_hbm"], env["x2bf_hbm"], env["out"]
    wg, wu, wd = env["wg"], env["wu"], env["wd"]

    fl_dma = b.floors("t", "v", "s")
    fl_pe = b.floors("v", "s")
    first = {"g": True, "y": True, "t": True}

    def gatew(kind, extra=()):
        w = list(extra)
        if first[kind]:
            w += fl_dma if kind in ("g", "y") else fl_pe
            first[kind] = False
        return w

    with ExitStack() as fs:
        fuse = fs.enter_context(nc.sbuf_tensor("fuse", [128, 64, 512], BF16)).ap()
        x2T = fs.enter_context(nc.sbuf_tensor("x2T", [128, 16, 512], BF16)).ap()
        wgb = [fs.enter_context(nc.sbuf_tensor(f"wgb{i}", [128, 4096], BF16)).ap()
               for i in range(2)]
        wub = [fs.enter_context(nc.sbuf_tensor(f"wub{i}", [128, 4096], BF16)).ap()
               for i in range(2)]
        wdb = [fs.enter_context(nc.sbuf_tensor(f"wdb{i}", [128, 2048], BF16)).ap()
               for i in range(4)]
        x2r = [fs.enter_context(nc.sbuf_tensor(f"x2r{i}", [128, DM], F32)).ap()
               for i in range(4)]
        sscr = [fs.enter_context(nc.sbuf_tensor(f"sscr{i}", [128, 512], F32)).ap()
                for i in range(2)]
        uscr = [fs.enter_context(nc.sbuf_tensor(f"uscr{i}", [128, 512], F32)).ap()
                for i in range(2)]
        ftiny = fs.enter_context(nc.sbuf_tensor("ftiny", [128, 40], F32)).ap()
        sdb = fs.enter_context(nc.sbuf_tensor("sdb", [128, DM], F32)).ap()
        sd_sb = fs.enter_context(nc.sbuf_tensor("sd_sb", [1, DM], F32)).ap()

        sdl = b.dma("y", "dK", sd_sb, env["sd_in"], waits=gatew("y"))
        with nc.psum_tensor("psdb", [128, 512], F32) as psdb_h:
            psdb = psdb_h.ap()
            v2 = 0
            for n in range(4):
                m2 = b.mk("t", lambda e, n=n: e.matmul(
                    psdb, env["ones_f"], sd_sb[:, 512 * n:512 * (n + 1)],
                    start=True, stop=True),
                    waits=gatew("t", [("dK", sdl)]) if n == 0 else
                    ((("dve", v2),) if v2 else ()))
                v2 = b.mk("v", lambda e, n=n: e.tensor_copy(
                    sdb[:, 512 * n:512 * (n + 1)], psdb),
                    waits=(("pe", m2),))
        sdb_done = v2

        bnst = ftiny[:, 0:24]
        mv = ftiny[:, 32:34]
        veps = ftiny[:, 34:35]
        lnv = ftiny[:, 35:36]
        rstd = ftiny[:, 36:37]
        bm = ftiny[:, 37:38]

        nG = {0: 0, 1: 0}
        nU = {0: 0, 1: 0}
        nD = {0: 0, 1: 0, 2: 0, 3: 0}
        nR = {0: st["xres_n"], 1: 0}
        wg_free = [0, 0]
        wu_free = [0, 0]
        wd_free = [0, 0, 0, 0]
        silu_mark = [0, 0]
        upsc_mark = [0, 0]
        fuse_mark = [0, 0]
        gate_done = 0
        down_done_pe = 0
        prev_evac = [0, 0]
        out_n = {0: 0, 1: 0}
        norm_mark = [0, 0]
        x2r_used = [0, 0, 0, 0]
        wd_pre = [0]

        all_spills = (("dW2", 16 * st["x2_spill"]),
                      ("dX2a", 16 * st["nX2"]))

        gu_pre = [0]

        def emit_gu_load(icb):
            bp = icb % 2
            b.dma("g", f"dG{bp}", wgb[bp], wg[icb * 128:(icb + 1) * 128, :],
                  waits=gatew("g", [("pe", wg_free[bp])]))
            nG[bp] += 1
            b.dma("g", f"dU{bp}", wub[bp], wu[icb * 128:(icb + 1) * 128, :],
                  waits=[("pe", wu_free[bp])])
            nU[bp] += 1

        def emit_wd_load(nch, ibp):
            bp = ibp % 4
            b.dma("g", f"dD{bp}", wdb[bp],
                  wd[(nch * 16 + ibp) * 128:(nch * 16 + ibp + 1) * 128, :],
                  waits=[("pe", wd_free[bp])])
            nD[bp] += 1

        def emit_x2t(f, extra):
            w = gatew("y", extra)
            for dm in range(16):
                b.dma("y", "dXT", x2T[:, dm, :],
                      x2bf_hbm[f * 512:(f + 1) * 512, dm * 128:(dm + 1) * 128],
                      waits=w if dm == 0 else (), transpose=True)
            return ("dXT", M["dXT"])

        x2t_next = None
        for f in range(4):
            x2t_ready = emit_x2t(0, list(all_spills)) if f == 0 else x2t_next

            with ExitStack() as gs:
                pg = [gs.enter_context(nc.psum_tensor(f"pg{f}_{i}", [128, 512], F32)).ap()
                      for i in range(2)]
                pu = [gs.enter_context(nc.psum_tensor(f"pu{f}_{i}", [128, 512], F32)).ap()
                      for i in range(2)]
                for icg in range(64):
                    icb = icg // 2
                    par = icg % 2
                    boff = (icg % 2) * 128
                    bp = icb % 2
                    if icg % 2 == 0 and icb >= gu_pre[0]:
                        emit_gu_load(icb)
                    if icg in (48, 52, 56, 60):
                        emit_wd_load(0, (icg - 48) // 4)
                    pm = None
                    for dm in range(16):
                        w0 = ()
                        if dm == 0:
                            w0 = gatew("t", [(f"dG{bp}", 16 * nG[bp]), x2t_ready,
                                             ("act", silu_mark[par]),
                                             ("dve", prev_evac[0]),
                                             ("dve", prev_evac[1]),
                                             ("dve", sdb_done)]
                                       if icg == 0 else
                                       [(f"dG{bp}", 16 * nG[bp]),
                                        ("act", silu_mark[par])])
                        pm = b.mk("t", lambda e, dm=dm, par=par, bp=bp, boff=boff:
                                  e.matmul(pg[par],
                                           wgb[bp][:, dm * 256 + boff:dm * 256 + boff + 128],
                                           x2T[:, dm, :],
                                           start=(dm == 0), stop=(dm == 15)),
                                  waits=w0)
                    gm = pm
                    for dm in range(16):
                        w0 = ()
                        if dm == 0:
                            w0 = ((f"dU{bp}", 16 * nU[bp]),
                                  ("act", upsc_mark[par]))
                        pm = b.mk("t", lambda e, dm=dm, par=par, bp=bp, boff=boff:
                                  e.matmul(pu[par],
                                           wub[bp][:, dm * 256 + boff:dm * 256 + boff + 128],
                                           x2T[:, dm, :],
                                           start=(dm == 0), stop=(dm == 15)),
                                  waits=w0)
                    wg_free[bp] = pm
                    wu_free[bp] = pm
                    silu_mark[par] = b.mk("s", lambda e, par=par, icg=icg:
                                          e.activation(sscr[par], pg[par], AF.Silu,
                                                       scale=sg_sb[:, icg:icg + 1]),
                                          waits=(("pe", gm),
                                                 ("dve", fuse_mark[par])))
                    upsc_mark[par] = b.mk("s", lambda e, par=par, icg=icg:
                                          e.activation(uscr[par], pu[par], AF.Copy,
                                                       scale=su_sb[:, icg:icg + 1]),
                                          waits=(("pe", pm),))
                    w0 = [("act", upsc_mark[par])]
                    if icg <= 1:
                        w0.append(("pe", down_done_pe))
                    fuse_mark[par] = b.mk("v", lambda e, par=par, icg=icg:
                                          e.tensor_mul(fuse[:, icg, :],
                                                       sscr[par], uscr[par]),
                                          waits=w0)
                gate_done = M["pe"]
                fuse_all = max(fuse_mark)
                gu_pre[0] = 0
                wd_pre[0] = 4

            if f < 3:
                x2t_next = emit_x2t(f + 1, [("pe", gate_done)])

            with ExitStack() as ds:
                pd = [[ds.enter_context(
                    nc.psum_tensor(f"pdn{f}_{g}_{t}", [128, 512], F32)).ap()
                    for t in range(4)] for g in range(2)]
                for tt in range(4):
                    tq = f * 4 + tt
                    par = tq % 2
                    xw = [(f"dO{'ab'[par]}", 16 * out_n[par])]
                    if tq == 0:
                        xw.append(("dW2", 16 * st["x2_spill"]))
                    b.dma("y", f"dR{par}", x2r[tt],
                          x2_hbm[tq * 128:(tq + 1) * 128, :], waits=xw)
                    nR[par] += 1
                sweep_evac = {}
                for nch in range(4):
                    grp = nch % 2
                    last_mm = {}
                    for ic in range(64):
                        ibp = ic // 4
                        bp = ibp % 4
                        if ic % 4 == 0 and not (nch == 0 and ibp < wd_pre[0]):
                            emit_wd_load(nch, ibp)
                            if f < 3 and ic == 16 and nch in (2, 3):
                                emit_gu_load(0 if nch == 2 else 1)
                        roff = ((ic // 2) % 2) * 1024 + (ic % 2) * 512
                        for tt in range(4):
                            w0 = []
                            if ic == 0 and tt == 0:
                                w0 = [(f"dD{bp}", 16 * nD[bp])]
                                if nch == 0:
                                    w0.append(("dve", fuse_all))
                                if nch >= 2:
                                    w0.append(("dve", sweep_evac[nch - 2]))
                            elif ic % 4 == 0 and tt == 0:
                                w0 = [(f"dD{bp}", 16 * nD[bp])]
                            pm = b.mk("t", lambda e, ic=ic, bp=bp, tt=tt,
                                      grp=grp, roff=roff:
                                      e.matmul(pd[grp][tt],
                                               fuse[:, ic, tt * 128:(tt + 1) * 128],
                                               wdb[bp][:, roff:roff + 512],
                                               start=(ic == 0), stop=(ic == 63)),
                                      waits=w0)
                            last_mm[tt] = pm
                        if ic % 4 == 3:
                            wd_free[bp] = M["pe"]
                    ev = 0
                    for tt in range(4):
                        par = tt % 2
                        w0 = [("pe", last_mm[tt])]
                        b.mk("v", lambda e, tt=tt, nch=nch, grp=grp:
                             e.tensor_mul(sscr[tt % 2],
                                          pd[grp][tt],
                                          sdb[:, nch * 512:(nch + 1) * 512]),
                             waits=w0)
                        w1 = []
                        if nch == 0 and tt < 2:
                            w1 = [(f"dR{par}", 16 * nR[par])]
                        ev = b.mk("v", lambda e, tt=tt, nch=nch:
                                  e.tensor_add(x2r[tt][:, nch * 512:(nch + 1) * 512],
                                               x2r[tt][:, nch * 512:(nch + 1) * 512],
                                               sscr[tt % 2]),
                                  waits=w1)
                    sweep_evac[nch] = ev
                down_done_pe = M["pe"]
                prev_evac[0] = sweep_evac[2]
                prev_evac[1] = sweep_evac[3]
                for tt in range(4):
                    tq = f * 4 + tt
                    par = tq % 2
                    bs = 0
                    for i4 in range(4):
                        bs = b.mk("v", lambda e, i4=i4, tt=tt: e.bn_stats(
                            bnst[:, 6 * i4:6 * i4 + 6],
                            x2r[tt][:, 512 * i4:512 * (i4 + 1)]))
                    ag = b.mk("v", lambda e: e.bn_aggr(mv, bnst),
                              waits=(("dve", bs),))
                    ve = b.mk("v", lambda e: e.tensor_scalar_add(
                        veps, mv[:, 1:2], EPS), waits=(("dve", ag),))
                    sq = b.mk("s", lambda e: e.activation(lnv, veps, AF.Sqrt),
                              waits=(("dve", ve),))
                    rc = b.mk("v", lambda e: e.reciprocal(rstd, lnv),
                              waits=(("act", sq),))
                    bmv = b.mk("v", lambda e: e.tensor_scalar(
                        bm, mv[:, 0:1], rstd, -1.0, ALU.mult, ALU.mult),
                        waits=(("dve", rc),))
                    norm_mark[par] = b.mk("v", lambda e, tt=tt: e.tensor_scalar(
                        x2r[tt], x2r[tt], rstd, bm, ALU.mult, ALU.add),
                        waits=(("dve", bmv),))
                    out_n[par] += 1
                    b.dma("y", f"dO{'ab'[par]}",
                          out[tq * 128:(tq + 1) * 128, :], x2r[tt],
                          waits=(("dve", norm_mark[par]),))
            gu_pre[0] = 2 if f < 3 else 0


_NC_CACHE = {}


def _get_nc():
    if "nc" not in _NC_CACHE:
        _NC_CACHE["nc"] = build_nc()
    return _NC_CACHE["nc"]


def kernel(L0_attn_hidden_states, L0_attn_Wq, L0_attn_Wk, L0_attn_Wv,
           L0_attn_Wo, L0_ffn_w_gate, L0_ffn_w_up, L0_ffn_w_down,
           L0_ffn_s_gate, L0_ffn_s_up, L0_ffn_s_down):
    x = np.ascontiguousarray(np.asarray(L0_attn_hidden_states, dtype=np.float32))
    wq = np.ascontiguousarray(
        np.asarray(L0_attn_Wq, dtype=np.float32)
        .reshape(16, 128, 16, 128).transpose(2, 1, 0, 3).reshape(2048, 2048))
    wk = np.ascontiguousarray(
        np.asarray(L0_attn_Wk, dtype=np.float32)
        .reshape(16, 128, 4, 128).transpose(2, 1, 0, 3).reshape(512, 2048))
    wv = np.ascontiguousarray(np.asarray(L0_attn_Wv, dtype=np.float32))
    wo = np.ascontiguousarray(np.asarray(L0_attn_Wo, dtype=np.float32))
    wg = np.ascontiguousarray(
        np.asarray(L0_ffn_w_gate, dtype=np.int8)
        .reshape(16, 128, 32, 256).transpose(2, 1, 0, 3).reshape(4096, 4096))
    wu = np.ascontiguousarray(
        np.asarray(L0_ffn_w_up, dtype=np.int8)
        .reshape(16, 128, 32, 256).transpose(2, 1, 0, 3).reshape(4096, 4096))
    wd = np.ascontiguousarray(
        np.asarray(L0_ffn_w_down, dtype=np.int8)
        .reshape(16, 2, 2, 128, 4, 512).transpose(4, 0, 3, 1, 2, 5)
        .reshape(8192, 2048))
    sg = np.ascontiguousarray(np.asarray(L0_ffn_s_gate, dtype=np.float32).reshape(64, 128))
    su = np.ascontiguousarray(np.asarray(L0_ffn_s_up, dtype=np.float32).reshape(64, 128))
    sd = np.ascontiguousarray(np.asarray(L0_ffn_s_down, dtype=np.float32).reshape(1, DM))

    B, S = 2, 8192
    in_maps = []
    for c in range(8):
        bb, s0 = c // 4, (c % 4) * T
        if s0 == 0:
            halo = np.zeros((HALO, DM), dtype=np.float32)
            kv = np.full((1, HALO), NEG, dtype=ml_dtypes.bfloat16)
        else:
            halo = x[bb, s0 - HALO:s0]
            kv = np.zeros((1, HALO), dtype=ml_dtypes.bfloat16)
        x_pad = np.ascontiguousarray(np.concatenate([halo, x[bb, s0:s0 + T]], axis=0))
        in_maps.append({
            "x_pad": x_pad, "kval": kv,
            "wq": wq, "wk": wk, "wv": wv, "wo": wo,
            "wg": wg, "wu": wu, "wd": wd,
            "sg_in": sg, "su_in": su, "sd_in": sd,
        })

    nc = _get_nc()
    res = run_bass_kernel_spmd(nc, in_maps, core_ids=list(range(8)))

    outp = np.empty((B, S, DM), dtype=np.float32)
    for c in range(8):
        bb, s0 = c // 4, (c % 4) * T
        outp[bb, s0:s0 + T] = res.results[c]["out"]
    return outp
